# version 16
# baseline (speedup 1.0000x reference)
import sys
sys.path.insert(0, "/opt/trn_rl_repo")
from contextlib import ExitStack

import numpy as np
import ml_dtypes
import concourse.bass as bass
import concourse.mybir as mybir

F32 = mybir.dt.float32
BF16 = mybir.dt.bfloat16
AF = mybir.ActivationFunctionType
ALU = mybir.AluOpType
AX = mybir.AxisListType

D = 4096
NCHUNK = 32
TFULL = 4096
TOWN = 2048
EPS = 1e-5

ENGS = ("tensor", "vector", "scalar", "gpsimd", "sync")
DEBUG_X1P = False
DEBUG_L1 = False


class Sem:
    __slots__ = ("name", "count", "h")

    def __init__(self, name):
        self.name = name
        self.count = 0
        self.h = None


class Deferred:
    __slots__ = ("meth", "args", "kwargs", "incs")

    def __init__(self, meth, args, kwargs):
        self.meth = meth
        self.args = args
        self.kwargs = kwargs
        self.incs = []

    def then_inc(self, sem, n=1):
        sem.count += n
        self.incs.append((sem, n))
        return self

    def run(self, eng):
        args = [a.h if isinstance(a, Sem) else a for a in self.args]
        kwargs = {k: (v.h if isinstance(v, Sem) else v) for k, v in self.kwargs.items()}
        instr = getattr(eng, self.meth)(*args, **kwargs)
        for sem, n in self.incs:
            instr.then_inc(sem.h, n)
        return instr


class EngPlan:
    def __init__(self, pl, name):
        self._pl = pl
        self._name = name

    def _rec(self, meth, args, kwargs, auto=True):
        pl, name = self._pl, self._name
        if (name in ("vector", "scalar", "gpsimd")
                and meth not in ("wait_ge", "nop")):
            cnt = pl.prog[name].count
            if cnt > 0 and pl._selfwait.get(name, 0) < cnt:
                w = Deferred("wait_ge", (pl.prog[name], cnt), {})
                pl.ops[name].append(w)
                pl._selfwait[name] = cnt
        d = Deferred(meth, args, kwargs)
        pl.ops[name].append(d)
        if meth in ("dma_start", "dma_start_transpose"):
            d.then_inc(pl.dmas[name], 16)
        elif meth == "collective_compute":
            d.then_inc(pl.cc_sem, 1)
        elif auto and meth not in ("wait_ge", "nop"):
            d.then_inc(pl.prog[name], 1)
        return d

    def __getattr__(self, meth):
        def rec(*a, auto=True, **k):
            return self._rec(meth, a, k, auto=auto)

        return rec

    def wait_ge_(self, sem, val):
        d = Deferred("wait_ge", (sem, int(val)), {})
        self._pl.ops[self._name].append(d)
        return d


class Planner:
    def __init__(self, nc, stack):
        self.nc = nc
        self.stack = stack
        self.ops = {e: [] for e in ENGS}
        self.prog = {e: self._mksem(f"prog_{e}") for e in ENGS}
        self.dmas = {e: self._mksem(f"dmas_{e}") for e in ENGS}
        self.qf = {e: self._mksem(f"qf_{e}") for e in ENGS}
        self._fenced = {e: ({}, 0) for e in ENGS}
        self.cc_sem = self._mksem("cc_done")
        self.t = EngPlan(self, "tensor")
        self.v = EngPlan(self, "vector")
        self.s = EngPlan(self, "scalar")
        self.g = EngPlan(self, "gpsimd")
        self.q = EngPlan(self, "sync")
        self.eng = {"tensor": self.t, "vector": self.v, "scalar": self.s,
                    "gpsimd": self.g, "sync": self.q}
        self.sbufs = {}
        self._ctxs = {}
        self._semcache = {}
        self._qslots = {}
        self._selfwait = {}

    def _mksem(self, name):
        s = Sem(name)
        s.h = self.stack.enter_context(self.nc.semaphore(name))
        return s

    def sem(self, name):
        if name not in self._semcache:
            self._semcache[name] = self._mksem(name)
        return self._semcache[name]

    def sb(self, name, shape, dtype, side=None):
        if name not in self.sbufs:
            kw = {"allow_name_mangling": True}
            if side is not None:
                kw["side"] = side
            cm = self.nc.sbuf_tensor(name, shape, dtype, **kw)
            self.sbufs[name] = cm.__enter__()
            self._ctxs[name] = cm
        return self.sbufs[name]

    def ps(self, name, shape, dtype):
        if name not in self.sbufs:
            cm = self.nc.psum_tensor(name, shape, dtype)
            self.sbufs[name] = cm.__enter__()
            self._ctxs[name] = cm
        return self.sbufs[name]

    def free(self, *names):
        for name in names:
            cm = self._ctxs.pop(name)
            cm.__exit__(None, None, None)
            del self.sbufs[name]

    POOL_BIG = ("sA", "sB", "sC", "sD", "rspl", "nspl", "bA", "bB", "bC", "bD")

    def pool(self, rows=True):
        p = {}
        names = ("sA", "sB", "sC", "sD", "rspl", "nspl") if rows else \
                ("sA", "sB", "sC", "sD")
        self.POOL_BIG = list(names) + ["bA", "bB", "bC", "bD"]
        for n in names:
            p[n] = self.sb(n, [128, 4096], F32, side="right")
        for n in ("bA", "bB", "bC", "bD"):
            p[n] = self.sb(n, [128, 4096], BF16, side="right")
        for n in ("cA", "cB", "cC", "cD"):
            if n not in self.sbufs:
                p[n] = self.sb(n, [128, 128], F32)
                self.g.memset(p[n][:], 0.0)
        return p

    def free_pool(self):
        for n in reversed(self.POOL_BIG):
            self.free(n)

    def tok(self, *engs):
        out = []
        for e in engs:
            if self.prog[e].count > 0:
                out.append((self.prog[e], self.prog[e].count))
            if self.qf[e].count > 0:
                out.append((self.qf[e], self.qf[e].count))
        return out

    def all_slots_tok(self):
        out = []
        for name, sem in self._semcache.items():
            if name.startswith("slot_") and sem.count > 0:
                out.append((sem, sem.count))
        return out

    def stok(self, *slotnames):
        out = []
        for n in slotnames:
            sl = self.slot(n)
            if sl.count > 0:
                out.append((sl, sl.count))
        return out

    def waits(self, consumer, *toks):
        c = self.eng[consumer]
        for tk in toks:
            if tk is None:
                continue
            for sem, val in tk:
                if val > 0:
                    c.wait_ge_(sem, val)

    def after(self, consumer, *producers):
        self.waits(consumer, self.tok(*producers))

    def slot(self, name):
        return self.sem("slot_" + name)

    def dma(self, queue, slotname, out, in_, **kw):
        sl = self.slot(slotname)
        d = self.eng[queue]._rec("dma_start", (), dict(out=out, in_=in_, **kw),
                                auto=False)
        d.incs.clear()
        self.dmas[queue].count -= 16
        d.then_inc(sl, 16)
        self._qslots.setdefault(queue, {})[slotname] = True
        return d

    def wait_slot(self, consumer, slotname, val=None):
        sl = self.slot(slotname)
        v = sl.count if val is None else val
        if v > 0:
            self.eng[consumer].wait_ge_(sl, v)

    def fence(self, queue):
        dm = self.dmas[queue]
        prev_slots, prev_dm = self._fenced[queue]
        last = None
        if dm.count > prev_dm:
            last = self.eng[queue].wait_ge_(dm, dm.count)
        new_slots = dict(prev_slots)
        for slotname in self._qslots.get(queue, {}):
            sl = self.slot(slotname)
            if sl.count > prev_slots.get(slotname, 0):
                last = self.eng[queue].wait_ge_(sl, sl.count)
                new_slots[slotname] = sl.count
        if last is not None:
            last.then_inc(self.qf[queue], 1)
            self._fenced[queue] = (new_slots, dm.count)

    def fence_all(self, queues=("sync", "scalar", "gpsimd", "vector")):
        for qn in queues:
            self.fence(qn)

    def tmark(self):
        self.ops["tensor"][-1].then_inc(self.prog["tensor"], 1)

    def finalize(self):
        nc = self.nc
        with nc.Block() as block:
            for name, deco in (("tensor", block.tensor), ("vector", block.vector),
                               ("scalar", block.scalar), ("gpsimd", block.gpsimd),
                               ("sync", block.sync)):
                ops = self.ops[name]
                if not ops:
                    continue

                def emit(eng, ops=ops):
                    for d in ops:
                        d.run(eng)

                deco(emit)
        return {e: len(self.ops[e]) for e in ENGS}


def host_constants():
    c = {}
    c["ones_bf_col"] = np.ones((128, 1), dtype=np.float32).astype(ml_dtypes.bfloat16)
    c["ones_f32_row"] = np.ones((1, 128), dtype=np.float32)
    c["ident_bf"] = np.eye(128, dtype=np.float32).astype(ml_dtypes.bfloat16)
    c["ident_f32"] = np.eye(128, dtype=np.float32)
    sw = np.zeros((128, 128), dtype=np.float32)
    for i_ in range(128):
        sw[i_, (i_ + 64) % 128] = 1.0
    c["swap_bf"] = sw.astype(ml_dtypes.bfloat16)
    E = np.zeros((64, 1024), dtype=np.float32)
    for k in range(64):
        E[k, k * 16:(k + 1) * 16] = 1.0
    c["E_exp"] = E
    return c


CONST_SPECS = [
    ("ones_bf_col", [128, 1], BF16),
    ("ones_f32_row", [1, 128], F32),
    ("ident_bf", [128, 128], BF16),
    ("ident_f32", [128, 128], F32),
    ("E_exp", [64, 1024], F32),
    ("swap_bf", [128, 128], BF16),
]


def declare_and_load_consts(pl, need_E=False, need_swap=False):
    nc = pl.nc
    aps = {}
    for name, shape, dt in CONST_SPECS:
        if name == "E_exp" and not need_E:
            continue
        if name == "swap_bf" and not need_swap:
            continue
        aps[name] = nc.dram_tensor(name, shape, dt, kind="ExternalInput").ap()
    pl.sb("c_ones_bf_col", [128, 1], BF16)
    pl.sb("c_ones_f32_row", [1, 128], F32)
    pl.sb("c_ident_bf", [128, 128], BF16)
    pl.sb("c_ident_f32", [128, 128], F32)
    pl.dma("sync", "consts", pl.sbufs["c_ones_bf_col"][:], aps["ones_bf_col"])
    pl.dma("sync", "consts", pl.sbufs["c_ones_f32_row"][:], aps["ones_f32_row"])
    pl.dma("sync", "consts", pl.sbufs["c_ident_bf"][:], aps["ident_bf"])
    pl.dma("sync", "consts", pl.sbufs["c_ident_f32"][:], aps["ident_f32"])
    if need_swap:
        pl.sb("c_swap_bf", [128, 128], BF16)
        pl.dma("sync", "consts", pl.sbufs["c_swap_bf"][:], aps["swap_bf"])
    eps_col = pl.sb("c_eps_col", [128, 1], F32)
    pl.g.memset(eps_col[:], EPS)
    onesf32c = pl.sb("c_ones_f32_col", [128, 1], F32)
    pl.g.memset(onesf32c[:], 1.0)
    onesbfr = pl.sb("c_ones_bf_row", [1, 128], BF16)
    pl.g.memset(onesbfr[:], 1.0)
    return aps


def host_const_inputs(consts, need_E=False, need_swap=False):
    d = {}
    for name, _, _ in CONST_SPECS:
        if name == "E_exp" and not need_E:
            continue
        if name == "swap_bf" and not need_swap:
            continue
        d[name] = consts[name]
    return d


def host_col_order(h):
    own = [512 * (2 * i + 1) if h == 0 else 1024 * i for i in range(4)]
    sib = [1024 * i if h == 0 else 512 * (2 * i + 1) for i in range(4)]
    cols = []
    for t0 in own + sib:
        cols.extend(range(t0, t0 + 512))
    return np.array(cols)


def tile_blocks(i):
    own = list(range(0, 4 * i + 4))
    sib = list(range(16, 16 + 4 * i + 4))
    return own + sib, own[-4:] + sib[-4:]


def host_masks(h):
    cols = host_col_order(h)
    M = np.zeros((4, 8, 128, 512), dtype=np.float32)
    for i in range(4):
        aq = cols[512 * i:512 * (i + 1)]
        blocks, win = tile_blocks(i)
        for j, blk in enumerate(win):
            as_ = cols[128 * blk:128 * (blk + 1)]
            M[i, j] = (as_[:, None] <= aq[None, :])
    return M.astype(ml_dtypes.bfloat16)


def host_rope(h):
    hd = 128
    inv = (1.0 / (10000.0 ** (np.arange(0, hd, 2, dtype=np.float32) / hd))).astype(np.float32)
    cols = host_col_order(h).astype(np.float32)
    ang = cols[None, :] * inv[:, None]
    cos = np.cos(ang.astype(np.float64)).astype(np.float32)
    sin = np.sin(ang.astype(np.float64)).astype(np.float32)
    cosd = np.concatenate([cos, cos], axis=0)
    sinds = np.concatenate([-sin, sin], axis=0)
    return (cosd[:, 0:2048].astype(ml_dtypes.bfloat16),
            sinds[:, 0:2048].astype(ml_dtypes.bfloat16),
            cosd.astype(ml_dtypes.bfloat16),
            sinds.astype(ml_dtypes.bfloat16))



def ln_tm_alloc(pl):
    st = {}
    st["sA"] = pl.sb("sA", [128, 4096], F32, side="right")
    st["bA"] = pl.sb("bA", [128, 4096], BF16, side="right")
    st["snk"] = pl.sb("snk", [128, 2048], BF16, side="right")
    st["bD"] = pl.sb("bD", [128, 2048], BF16, side="right")
    for n in ("cA", "cB", "cC", "cD"):
        if n not in pl.sbufs:
            pl.sb(n, [128, 128], F32)
            pl.g.memset(pl.sbufs[n][:], 0.0)
    pl.after("scalar", "gpsimd")
    pl.after("vector", "gpsimd")
    st["cols"] = (pl.sbufs["cA"], pl.sbufs["cB"], pl.sbufs["cC"], pl.sbufs["cD"])
    st["use"] = [None, None]
    st["out"] = [None, None]
    st["ntok"] = [None, None]
    st["etok"] = [None, None]
    st["ftok"] = [None, None]
    st["tail"] = False
    return st


def ln_tm_own_extra(pl, st):
    st["sB"] = pl.sb("sB", [128, 4096], F32, side="right")
    st["bB"] = pl.sb("bB", [128, 4096], BF16, side="right")


def ln_tm_part1(pl, st, c, x_tm, xn_tm):
    t, v, s, g = pl.t, pl.v, pl.s, pl.g
    P = pl.sbufs
    col_sum, col_sq, col_r, col_nmr = st["cols"]
    qn = "sync" if c % 2 == 0 else "scalar"
    if st["tail"]:
        ui, oi = 0, 0
    else:
        ui, oi = c % 2, c % 2
    ldb = st[("sA", "sB")[ui]]
    slot = ("sA", "sB")[ui]
    tmb = st[("bA", "bB")[oi]]
    pl.waits(qn, st["use"][ui])
    pl.dma(qn, slot, ldb[:], x_tm[128 * c:128 * (c + 1), :])
    pl.wait_slot("vector", slot)
    v.reduce_sum(col_sum[:, c:c + 1], ldb[:], axis=AX.X)
    pl.wait_slot("scalar", slot)
    s.activation(out=st["snk"][:], in_=ldb[:, 0:2048], func=AF.Square,
                 accum_out=col_sq[:, c:c + 1])
    s.activation(out=st["snk"][:], in_=ldb[:, 2048:4096], func=AF.Square,
                 accum_out=col_r[:, c:c + 1])
    pl.after("vector", "scalar")
    v.tensor_tensor(col_sq[:, c:c + 1], col_sq[:, c:c + 1], col_r[:, c:c + 1], op=ALU.add)
    v.tensor_scalar_mul(col_sum[:, c:c + 1], col_sum[:, c:c + 1], 1.0 / D)
    v.tensor_scalar_mul(col_sq[:, c:c + 1], col_sq[:, c:c + 1], 1.0 / D)
    v.tensor_tensor(col_r[:, c:c + 1], col_sum[:, c:c + 1], col_sum[:, c:c + 1], op=ALU.mult)
    v.tensor_tensor(col_r[:, c:c + 1], col_sq[:, c:c + 1], col_r[:, c:c + 1], op=ALU.subtract)
    pl.after("scalar", "vector")
    s.activation(out=col_r[:, c:c + 1], in_=col_r[:, c:c + 1], func=AF.Ln,
                 bias=P["c_eps_col"][:])
    s.activation(out=col_r[:, c:c + 1], in_=col_r[:, c:c + 1], func=AF.Exp, scale=-0.5)
    pl.after("vector", "scalar")
    v.tensor_tensor(col_nmr[:, c:c + 1], col_sum[:, c:c + 1], col_r[:, c:c + 1], op=ALU.mult)
    v.tensor_scalar_mul(col_nmr[:, c:c + 1], col_nmr[:, c:c + 1], -1.0)
    pl.after("gpsimd", "vector", "scalar")
    pl.waits("gpsimd", st["out"][oi])
    g.tensor_scalar(out=tmb[:], in0=ldb[:], scalar1=col_r[:, c:c + 1],
                    scalar2=col_nmr[:, c:c + 1], op0=ALU.mult, op1=ALU.add)
    st["use"][ui] = pl.tok("gpsimd", "scalar")
    pl.dma("gpsimd", "xntm" + str(oi), xn_tm[128 * c:128 * (c + 1), :], tmb[:])
    st["ntok"][oi] = pl.tok("gpsimd")


def ln_tm_part2(pl, st, c, xn_fm):
    t, v, s, g = pl.t, pl.v, pl.s, pl.g
    P = pl.sbufs
    idb = P["c_ident_bf"]
    pb = st["pb"]
    qn = "sync" if c % 2 == 0 else "scalar"
    if st["tail"]:
        oi, ei = 0, 0
        bs = 4
    else:
        oi = c % 2
        ei = oi
        bs = 4 * oi
    tmb = st[("bA", "bB")[oi]]
    pl.waits("tensor", st["ntok"][oi], st["etok"][ei])
    for dc in range(32):
        bank = pb[bs + dc // 8]
        reg = bank[:, 64 * (dc % 8):64 * (dc % 8 + 1)].bitcast(BF16)
        t.transpose(reg, tmb[:, 128 * dc:128 * (dc + 1)], idb[:], auto=False)
    pl.tmark()
    ttok = pl.tok("tensor")
    st["out"][oi] = pl.stok("xntm" + str(oi)) + ttok
    pl.waits("scalar", ttok)
    pl.waits("vector", ttok)
    if st["tail"]:
        dstf = xn_fm[:, 128 * c:128 * (c + 1)].rearrange("(dc p) tt -> p dc tt", p=128)
        for half in range(2):
            pl.waits("scalar", st["ftok"][1 - half])
            pl.waits("vector", st["ftok"][1 - half])
            for b in range(2):
                dst = st["bD"][:, 1024 * b:1024 * (b + 1)]
                srcb = pb[bs + 2 * half + b][:, :].bitcast(BF16)
                if b == 0:
                    s.copy(out=dst, in_=srcb)
                else:
                    v.tensor_copy(dst, srcb)
            st["etok"][0] = pl.tok("scalar", "vector")
            qn2 = "sync" if half == 0 else "scalar"
            pl.waits(qn2, st["etok"][0])
            pl.dma(qn2, "xfm" + str(half), dstf[:, 16 * half:16 * (half + 1), :],
                   st["bD"].rearrange("p (a b2) -> p a b2", b2=128))
            st["ftok"][half] = pl.stok("xfm" + str(half))
    else:
        for b in range(4):
            dst = st["xn_res"][:, 8 * b:8 * (b + 1), 128 * c:128 * (c + 1)]
            srcb = pb[bs + b][:, :].bitcast(BF16).rearrange("p (a b2) -> p a b2", b2=128)
            if b < 2:
                s.copy(out=dst, in_=srcb)
            else:
                v.tensor_copy(dst, srcb)
        st["etok"][ei] = pl.tok("scalar", "vector")


def ln_tm_own_stores(pl, st, xn_fm, nown):
    gtok = pl.tok("scalar", "vector")
    xr = st["xn_res"]
    for gg in range((nown + 7) // 8):
        w = min(1024, 128 * nown - 1024 * gg)
        dst = xn_fm[:, 1024 * gg:1024 * gg + w].rearrange("(dc p) tt -> p dc tt", p=128)
        pl.waits("sync", gtok)
        pl.dma("sync", "xfmA", dst[:, 0:16, :], xr[:, 0:16, 1024 * gg:1024 * gg + w])
        pl.waits("scalar", gtok)
        pl.dma("scalar", "xfmB", dst[:, 16:32, :], xr[:, 16:32, 1024 * gg:1024 * gg + w])
        st["ftok"][0] = pl.stok("xfmA")
        st["ftok"][1] = pl.stok("xfmB")


def ln_fm_phase(pl, T, x_fm, xn_fm, tag, pbanks):
    t, v, s, g, q = pl.t, pl.v, pl.s, pl.g, pl.q
    P = pl.sbufs
    ones_bf = P["c_ones_bf_col"]

    xb = [P["sA"], P["sB"]]
    slots = ["sA", "sB"]
    xbf = [P["bA"], P["bB"]]
    xsq = [P["bC"], P["bD"]]
    sum_row = P["rspl"][0:1, 0:T]
    sq_row = P["nspl"][0:1, 0:T]
    r_row = P["sC"][0:1, 0:T]
    nmr_row = P["sD"][0:1, 0:T]

    nhalf = (T + 2047) // 2048
    mm_tok = [None, None]
    for th in range(nhalf):
        W = min(2048, T - 2048 * th)
        ntt = W // 512
        for c in range(NCHUNK):
            it = th * NCHUNK + c
            qn = "sync" if it % 2 == 0 else "scalar"
            pl.waits(qn, mm_tok[it % 2])
            pl.dma(qn, slots[it % 2], xb[it % 2][:, 0:W],
                   x_fm[128 * c:128 * (c + 1), 2048 * th:2048 * th + W])
            pl.wait_slot("vector", slots[it % 2])
            v.tensor_copy(xbf[it % 2][:, 0:W], xb[it % 2][:, 0:W])
            pl.wait_slot("scalar", slots[it % 2])
            s.activation(out=xsq[it % 2][:, 0:W], in_=xb[it % 2][:, 0:W], func=AF.Square)
            pl.after("tensor", "vector", "scalar")
            for tt in range(ntt):
                t.matmul(pbanks[tt][0:1, :], ones_bf[:], xbf[it % 2][:, 512 * tt:512 * (tt + 1)],
                         start=(c == 0), stop=(c == NCHUNK - 1), auto=False)
                t.matmul(pbanks[4 + tt][0:1, :], ones_bf[:], xsq[it % 2][:, 512 * tt:512 * (tt + 1)],
                         start=(c == 0), stop=(c == NCHUNK - 1), auto=False)
            pl.tmark()
            mm_tok[it % 2] = pl.tok("tensor")
        base = 2048 * th
        pl.after("vector", "tensor")
        for tt in range(ntt):
            v.tensor_copy(sum_row[0:1, base + 512 * tt:base + 512 * (tt + 1)], pbanks[tt][0:1, :])
            v.tensor_copy(sq_row[0:1, base + 512 * tt:base + 512 * (tt + 1)], pbanks[4 + tt][0:1, :])

    v.tensor_scalar_mul(sum_row[:], sum_row[:], 1.0 / D)
    v.tensor_scalar_mul(sq_row[:], sq_row[:], 1.0 / D)
    v.tensor_tensor(r_row[:], sum_row[:], sum_row[:], op=ALU.mult)
    v.tensor_tensor(r_row[:], sq_row[:], r_row[:], op=ALU.subtract)
    pl.after("scalar", "vector")
    s.activation(out=r_row[:], in_=r_row[:], func=AF.Ln, bias=P["c_eps_col"][0:1, :])
    s.activation(out=r_row[:], in_=r_row[:], func=AF.Exp, scale=-0.5)
    pl.after("vector", "scalar")
    v.tensor_tensor(nmr_row[:], sum_row[:], r_row[:], op=ALU.mult)
    v.tensor_scalar_mul(nmr_row[:], nmr_row[:], -1.0)

    _fm_normalize(pl, T, x_fm, xn_fm, tag, pbanks, rows_sbuf=(r_row, nmr_row))


def _fm_normalize(pl, T, x_fm, xn_fm, tag, pbanks, rows_dram=None, rows_sbuf=None):
    t, v, s, g, q = pl.t, pl.v, pl.s, pl.g, pl.q
    P = pl.sbufs
    ones_f32 = P["c_ones_f32_row"]

    if rows_dram is not None:
        rr = P["sC"][0:1, 0:T]
        nn = P["sD"][0:1, 0:T]
        pl.wait_slot("sync", "rowsw")
        pl.dma("sync", "rowr", rr, rows_dram[0])
        pl.dma("sync", "rown", nn, rows_dram[1])
        pl.wait_slot("tensor", "rowr")
        pl.wait_slot("tensor", "rown")
    else:
        rr, nn = rows_sbuf
        pl.after("tensor", "vector", "scalar")

    r_spl = P["rspl"][:, 0:T]
    n_spl = P["nspl"][:, 0:T]
    xch = [P["sA"], P["sB"]]
    slots = [tag + "nA", tag + "nB"]
    tmp = [P["sC"], P["sD"]]
    xnb = [P["bA"], P["bB"]]

    for tt in range(T // 512):
        bankA = pbanks[2 * (tt % 2)]
        bankB = pbanks[2 * (tt % 2) + 1]
        if tt >= 2:
            pl.after("tensor", "scalar")
        t.matmul(bankA[:, :], ones_f32[:], rr[0:1, 512 * tt:512 * (tt + 1)],
                 start=True, stop=True, auto=False)
        t.matmul(bankB[:, :], ones_f32[:], nn[0:1, 512 * tt:512 * (tt + 1)],
                 start=True, stop=True, auto=False)
        pl.tmark()
        pl.after("scalar", "tensor")
        s.copy(out=r_spl[:, 512 * tt:512 * (tt + 1)], in_=bankA[:, :])
        s.copy(out=n_spl[:, 512 * tt:512 * (tt + 1)], in_=bankB[:, :])

    spl_tok = pl.tok("scalar")
    pre = pl.tok("vector", "scalar", "gpsimd", "tensor") + pl.all_slots_tok()
    cons_tok = [pre, pre]
    st_tok = [None, None]
    for c in range(NCHUNK):
        qn = "sync" if c % 2 == 0 else "scalar"
        pl.waits(qn, cons_tok[c % 2])
        pl.dma(qn, slots[c % 2], xch[c % 2][:, 0:T], x_fm[128 * c:128 * (c + 1), :])
        e = "vector" if c % 2 == 0 else "gpsimd"
        ee = pl.eng[e]
        pl.wait_slot(e, slots[c % 2])
        pl.waits(e, spl_tok, st_tok[c % 2])
        ee.tensor_tensor(tmp[c % 2][:, 0:T], xch[c % 2][:, 0:T], r_spl, op=ALU.mult)
        ee.tensor_tensor(xnb[c % 2][:, 0:T], tmp[c % 2][:, 0:T], n_spl, op=ALU.add)
        cons_tok[c % 2] = pl.tok(e)
        pl.after("gpsimd", e)
        pl.dma("gpsimd", tag + "xo" + str(c % 2),
               xn_fm[128 * c:128 * (c + 1), :], xnb[c % 2][:, 0:T])
        st_tok[c % 2] = pl.stok(tag + "xo" + str(c % 2))


def stream_w_oc(pl, w_dram, n_ic, col0, width, wgt_sb, slotname, war_tok=None,
                queue="gpsimd"):
    view = w_dram[:, col0:col0 + width].rearrange("(ic p) j -> p ic j", p=128)
    half = n_ic // 2 if n_ic >= 2 else n_ic
    pl.waits(queue, war_tok)
    pl.dma(queue, slotname, wgt_sb[:, 0:half, 0:width], view[:, 0:half, :])
    if half < n_ic:
        pl.dma(queue, slotname, wgt_sb[:, half:n_ic, 0:width], view[:, half:n_ic, :])


def load_resident_fm(pl, name, src_fm, ranges, nic=NCHUNK, queue="sync"):
    Ttot = sum(a1 - a0 for a0, a1 in ranges)
    res = pl.sb(name, [128, nic, Ttot], BF16, side="right")
    for ic in range(nic):
        o = 0
        for a0, a1 in ranges:
            pl.dma(queue, name, res[:, ic, o:o + (a1 - a0)],
                   src_fm[128 * ic:128 * (ic + 1), a0:a1])
            o += a1 - a0
    return res


def l0_ffn_c1(pl, xn_res, w12T, b1cols, b2cols, x4_dram, T, pbanks, hook=None):
    t, v, s, g = pl.t, pl.v, pl.s, pl.g
    wgtG = [pl.sb("wgtA", [128, NCHUNK, 128], BF16), pl.sb("wgtB", [128, NCHUNK, 128], BF16)]
    wgtU = [pl.sb("wgtC", [128, NCHUNK, 128], BF16), pl.sb("wgtD", [128, NCHUNK, 128], BF16)]
    silu_sb = pl.sb("silu_sb", [128, 512], F32)
    x4row = [pl.sb("x4rowA", [128, 2048], BF16), pl.sb("x4rowB", [128, 2048], BF16)]
    ntt = T // 512

    entry = pl.tok("tensor", "vector", "scalar", "gpsimd") + pl.all_slots_tok()
    wtok = [entry, entry]
    btok = [entry] * 4
    rtok = [entry, entry]
    stok = entry
    for a in range(32):
        if hook is not None:
            hook(a)
        stream_w_oc(pl, w12T, NCHUNK, a * 128, 128, wgtG[a % 2], "wgtG" + str(a % 2),
                    war_tok=wtok[a % 2])
        stream_w_oc(pl, w12T, NCHUNK, 4096 + a * 128, 128, wgtU[a % 2], "wgtU" + str(a % 2),
                    war_tok=wtok[a % 2])
        pl.wait_slot("tensor", "wgtG" + str(a % 2))
        pl.wait_slot("tensor", "wgtU" + str(a % 2))
        for tt in range(ntt):
            bg = pbanks[2 * (tt % 2)]
            bu = pbanks[2 * (tt % 2) + 1]
            pl.waits("tensor", btok[2 * (tt % 2)], btok[2 * (tt % 2) + 1])
            for ic in range(NCHUNK):
                t.matmul(bg[:, :], wgtG[a % 2][:, ic, :], xn_res[:, ic, 512 * tt:512 * (tt + 1)],
                         start=(ic == 0), stop=(ic == NCHUNK - 1), auto=False)
            pl.tmark()
            for ic in range(NCHUNK):
                t.matmul(bu[:, :], wgtU[a % 2][:, ic, :], xn_res[:, ic, 512 * tt:512 * (tt + 1)],
                         start=(ic == 0), stop=(ic == NCHUNK - 1), auto=False)
            pl.tmark()
            mm_tok = pl.tok("tensor")
            pl.waits("scalar", mm_tok, stok)
            s.activation(out=silu_sb[:, :], in_=bg[:, :], func=AF.Silu,
                         bias=b1cols[:, a:a + 1])
            pl.waits("vector", mm_tok)
            pl.after("vector", "scalar")
            if tt == 0:
                pl.waits("vector", rtok[a % 2])
            v.scalar_tensor_tensor(out=x4row[a % 2][:, 512 * tt:512 * (tt + 1)],
                                   in0=bu[:, :], scalar=b2cols[:, a:a + 1],
                                   in1=silu_sb[:, :], op0=ALU.add, op1=ALU.mult)
            btok[2 * (tt % 2)] = pl.tok("vector")
            btok[2 * (tt % 2) + 1] = pl.tok("vector")
            stok = pl.tok("vector")
        wtok[a % 2] = pl.tok("tensor")
        pl.after("scalar", "vector")
        pl.dma("scalar", "x4row" + str(a % 2),
               x4_dram[128 * a:128 * (a + 1), 0:T], x4row[a % 2][:, 0:T])
        rtok[a % 2] = pl.stok("x4row" + str(a % 2))


def l0_ffn_c2(pl, x4_res, w3T, b3cols, x_fm_own, x1p_dram, T, own_rngs, pbanks):
    t, v, s, g = pl.t, pl.v, pl.s, pl.g
    wgt = [pl.sbufs["wgtA"], pl.sbufs["wgtB"]]
    xbuf = [pl.sb("c2xA", [128, 512], F32, side="right"),
            pl.sb("c2xB", [128, 512], F32, side="right")]
    orow = [pl.sb("c2oA", [128, 2048], F32, side="right"),
            pl.sb("c2oB", [128, 2048], F32, side="right")]
    ntt = T // 512
    tcols = []
    for a0, a1 in own_rngs:
        for a in range(a0, a1, 512):
            tcols.append(a)

    wslots = ["wgtA", "wgtB"]
    entry = pl.tok("tensor", "vector", "scalar", "gpsimd") + pl.all_slots_tok()
    wtok = [entry, entry]
    btok = [entry] * 4
    xtok = [entry, entry]
    otok = [entry, entry]
    for nc_ in range(32):
        stream_w_oc(pl, w3T, NCHUNK, nc_ * 128, 128, wgt[nc_ % 2], wslots[nc_ % 2],
                    war_tok=wtok[nc_ % 2])
        pl.wait_slot("tensor", wslots[nc_ % 2])
        for tt in range(ntt):
            bank = pbanks[(nc_ * ntt + tt) % 4]
            pl.waits("sync", xtok[tt % 2])
            pl.dma("sync", "c2x" + str(tt % 2), xbuf[tt % 2][:, :],
                   x_fm_own[128 * nc_:128 * (nc_ + 1), tcols[tt]:tcols[tt] + 512])
            pl.waits("tensor", btok[(nc_ * ntt + tt) % 4])
            for ic in range(NCHUNK):
                t.matmul(bank[:, :], wgt[nc_ % 2][:, ic, :], x4_res[:, ic, 512 * tt:512 * (tt + 1)],
                         start=(ic == 0), stop=(ic == NCHUNK - 1), auto=False)
            pl.tmark()
            mm_tok = pl.tok("tensor")
            pl.waits("vector", mm_tok)
            pl.wait_slot("vector", "c2x" + str(tt % 2))
            if tt <= 1:
                pl.waits("vector", otok[nc_ % 2])
            v.scalar_tensor_tensor(out=orow[nc_ % 2][:, 512 * tt:512 * (tt + 1)],
                                   in0=bank[:, :], scalar=b3cols[:, nc_:nc_ + 1],
                                   in1=xbuf[tt % 2][:, :],
                                   op0=ALU.add, op1=ALU.add)
            btok[(nc_ * ntt + tt) % 4] = pl.tok("vector")
            xtok[tt % 2] = pl.tok("vector")
        wtok[nc_ % 2] = pl.tok("tensor")
        pl.after("scalar", "vector")
        pl.dma("scalar", "c2o" + str(nc_ % 2),
               x1p_dram[128 * nc_:128 * (nc_ + 1), 0:T], orow[nc_ % 2][:, 0:T])
        otok[nc_ % 2] = pl.stok("c2o" + str(nc_ % 2))


def l0_bind_prep(pl, bv_q, bv_k, bv_v, qk_dram, vb_dram):
    v, s, q = pl.v, pl.s, pl.q
    qk_cols = pl.sb("qk_cols", [128, 32], F32)
    vb_cols = pl.sb("vb_cols", [128, 32], F32)
    bq = pl.sb("bq", [4, 1024], F32)
    bk = pl.sb("bk", [4, 1024], F32)
    bvv = pl.sb("bv", [4, 1024], F32)
    alpha = pl.sb("alpha", [4, 3], F32)
    zmask = pl.sb("zmask", [4, 1024], F32)
    hard = pl.sb("hard", [4, 3, 1024], F32)
    pl.dma("sync", "bvld", bq[:], bv_q)
    pl.dma("sync", "bvld", bk[:], bv_k)
    pl.dma("sync", "bvld", bvv[:], bv_v)
    pl.wait_slot("vector", "bvld")
    pl.wait_slot("scalar", "bvld")
    for i, b in enumerate((bq, bk, bvv)):
        v.tensor_reduce(alpha[:, i:i + 1], b[:], axis=AX.X, op=ALU.add,
                        apply_absolute_value=True)
        v.tensor_scalar_mul(alpha[:, i:i + 1], alpha[:, i:i + 1], 1.0 / 1024.0)
        s.activation(out=hard[:, i, :], in_=b[:], func=AF.Sign)
        pl.after("vector", "scalar")
        v.tensor_scalar(out=hard[:, i, :], in0=hard[:, i, :],
                        scalar1=alpha[:, i:i + 1], scalar2=None,
                        op0=ALU.mult, op1=ALU.bypass)
        v.tensor_single_scalar(out=zmask[:], in_=hard[:, i, :],
                               scalar=0.0, op=ALU.is_equal)
        v.scalar_tensor_tensor(out=hard[:, i, :], in0=pl.sbufs["zmask"][:],
                               scalar=alpha[:, i:i + 1], in1=hard[:, i, :],
                               op0=ALU.mult, op1=ALU.add)
    v.tensor_tensor(hard[:, 0, :], hard[:, 0, :], hard[:, 1, :], op=ALU.mult)
    v.tensor_scalar_mul(hard[:, 0, :], hard[:, 0, :], 0.125)
    pl.after("sync", "vector")
    pl.dma("sync", "bvst", qk_dram.rearrange("(h i) -> h i", h=4), hard[:, 0, :])
    pl.dma("sync", "bvst", vb_dram.rearrange("(h i) -> h i", h=4), hard[:, 2, :])
    pl.wait_slot("sync", "bvst")
    for c in range(32):
        pl.dma("sync", "bvld2", qk_cols[:, c:c + 1], qk_dram[128 * c:128 * (c + 1)].rearrange("(p o) -> p o", o=1))
        pl.dma("sync", "bvld2", vb_cols[:, c:c + 1], vb_dram[128 * c:128 * (c + 1)].rearrange("(p o) -> p o", o=1))
    pl.free("hard", "zmask", "alpha", "bv", "bk", "bq")
    return qk_cols, vb_cols


def l0_attn(pl, xn_fm, xn_tm, masks_dram, x1p_dram, x1_dram, qk_cols, vb_cols,
            pbanks):
    t, v, s, g = pl.t, pl.v, pl.s, pl.g

    K_res = pl.sb("l0K", [128, 8, TFULL], BF16, side="right")
    V_res = pl.sb("l0V", [128, 32, 1024], BF16, side="right")
    Ptile = pl.sb("l0P", [128, 32, 512], BF16, side="right")
    qres = [pl.sb("l0qA", [128, 8, 512], BF16, side="right"),
            pl.sb("l0qB", [128, 8, 512], BF16, side="right")]
    mres = [pl.sb("l0mA", [128, 8, 512], BF16, side="right"),
            pl.sb("l0mB", [128, 8, 512], BF16, side="right")]
    x1buf = [pl.sb("l0x1A", [128, 512], F32, side="right"),
             pl.sb("l0x1B", [128, 512], F32, side="right")]
    orow = [pl.sb("l0oA", [128, 512], F32, side="right"),
            pl.sb("l0oB", [128, 512], F32, side="right")]

    entry0 = pl.tok("tensor", "vector", "scalar", "gpsimd") + pl.all_slots_tok()
    ktok = entry0
    vtok = entry0
    qtok = [entry0, entry0]
    mtok = [entry0, entry0]
    x1tok = [entry0, entry0]
    ti_glob = 0
    for hd in range(4):
        d0c = 8 * hd
        pl.waits("gpsimd", ktok)
        for dc in range(8):
            pl.dma("gpsimd", "l0K", K_res[:, dc, :],
                   xn_fm[128 * (d0c + dc):128 * (d0c + dc + 1), :])
        pl.wait_slot("vector", "l0K")
        for dc in range(8):
            v.tensor_scalar(out=K_res[:, dc, :], in0=K_res[:, dc, :],
                            scalar1=qk_cols[:, d0c + dc:d0c + dc + 1], scalar2=None,
                            op0=ALU.mult, op1=ALU.bypass)
        kready = pl.tok("vector")
        pl.waits("sync", vtok)
        for sb in range(32):
            pl.dma("sync", "l0V", V_res[:, sb, :],
                   xn_tm[128 * sb:128 * (sb + 1), 1024 * hd:1024 * (hd + 1)])
        pl.wait_slot("tensor", "l0V")
        pl.waits("tensor", kready)

        ptok = entry0
        for ti in range(4):
            blocks, win = tile_blocks(ti)
            qb = ti_glob % 2
            pl.waits("sync", qtok[qb])
            for dc in range(8):
                pl.dma("sync", "l0q" + str(qb), qres[qb][:, dc, :],
                       xn_fm[128 * (d0c + dc):128 * (d0c + dc + 1), 512 * ti:512 * (ti + 1)])
            pl.waits("gpsimd", mtok[qb])
            pl.dma("gpsimd", "l0m" + str(qb), mres[qb][:, :, :],
                   masks_dram[ti].rearrange("k p j -> p k j"))
            pl.wait_slot("tensor", "l0q" + str(qb))
            pl.after("tensor", "scalar", "vector", "gpsimd")
            win_set = {blk: j for j, blk in enumerate(win)}
            sig_tok = []
            for si, sb in enumerate(blocks):
                bank = pbanks[4 + si % 2]
                if si >= 2:
                    pl.waits("tensor", sig_tok[si - 2])
                for dc in range(8):
                    t.matmul(bank[:, :], K_res[:, dc, 128 * sb:128 * (sb + 1)],
                             qres[qb][:, dc, :],
                             start=(dc == 0), stop=(dc == 7), auto=False)
                pl.tmark()
                mm_tok = pl.tok("tensor")
                pl.waits("scalar", mm_tok)
                if si == 0:
                    pl.waits("scalar", ptok)
                s.activation(out=Ptile[:, sb, :], in_=bank[:, :], func=AF.Sigmoid)
                if sb in win_set:
                    jj = win_set[sb]
                    e = "vector" if jj % 2 == 0 else "gpsimd"
                    pl.after(e, "scalar")
                    pl.wait_slot(e, "l0m" + str(qb))
                    ee = pl.eng[e]
                    ee.tensor_tensor(Ptile[:, sb, :], Ptile[:, sb, :],
                                     mres[qb][:, jj, :], op=ALU.mult)
                    sig_tok.append(pl.tok(e))
                else:
                    sig_tok.append(pl.tok("scalar"))
            pa_tok = pl.tok("vector", "scalar", "gpsimd")
            for dc in range(8):
                cidx = d0c + dc
                bank = pbanks[dc % 2]
                pl.waits("sync", x1tok[dc % 2])
                pl.dma("sync", "l0x1" + str(dc % 2), x1buf[dc % 2][:, :],
                       x1p_dram[128 * cidx:128 * (cidx + 1), 512 * ti:512 * (ti + 1)])
                if dc == 0:
                    pl.waits("tensor", pa_tok)
                if dc >= 2:
                    pl.waits("tensor", x1tok[dc % 2])
                for si, sb in enumerate(blocks):
                    t.matmul(bank[:, :], V_res[:, sb, 128 * dc:128 * (dc + 1)],
                             Ptile[:, sb, :], start=(si == 0), stop=(si == len(blocks) - 1),
                             auto=False)
                pl.tmark()
                mm_tok = pl.tok("tensor")
                pl.waits("vector", mm_tok)
                pl.wait_slot("vector", "l0x1" + str(dc % 2))
                pl.waits("vector", pl.stok("l0o" + str(dc % 2)))
                v.scalar_tensor_tensor(out=orow[dc % 2][:, :], in0=bank[:, :],
                                       scalar=vb_cols[:, cidx:cidx + 1],
                                       in1=x1buf[dc % 2][:, :], op0=ALU.mult, op1=ALU.add)
                x1tok[dc % 2] = pl.tok("vector")
                pl.after("scalar", "vector")
                pl.dma("scalar", "l0o" + str(dc % 2),
                       x1_dram[128 * cidx:128 * (cidx + 1), 512 * ti:512 * (ti + 1)],
                       orow[dc % 2][:, :])
            qtok[qb] = pl.tok("tensor")
            mtok[qb] = pl.tok("vector", "gpsimd")
            ptok = pl.tok("tensor")
            ti_glob += 1
        ktok = pl.tok("tensor")
        vtok = pl.tok("tensor")


def emit_l0(pl, io):
    pb = io["pb"]
    qk_cols, vb_cols = l0_bind_prep(pl, io["bv_q"], io["bv_k"], io["bv_v"],
                                    io["qkd"], io["vbd"])
    pl.wait_slot("vector", "bvld2")
    pl.wait_slot("tensor", "bvld2")
    st = ln_tm_alloc(pl)
    st["pb"] = pb
    st["xn_res"] = pl.sb("xn_res", [128, NCHUNK, TOWN], BF16, side="right")
    ln_tm_own_extra(pl, st)
    nown = TOWN // 128
    ntc = TFULL // 128
    for c in range(nown):
        ln_tm_part1(pl, st, c, io["x_tm"], io["xn0_tm"])
        ln_tm_part2(pl, st, c, io["xn0_fm"])
    ln_tm_own_stores(pl, st, io["xn0_fm"], nown)
    pl.free("bB", "sB")
    st["tail"] = True
    st["etok"][0] = (st["etok"][0] or []) + (st["etok"][1] or [])
    st["use"][0] = (st["use"][0] or []) + (st["use"][1] or [])
    st["out"][0] = (st["out"][0] or []) + (st["out"][1] or [])
    st["ntok"][0] = (st["ntok"][0] or []) + (st["ntok"][1] or [])

    def lnhook(a):
        if a >= 1 and 15 + a <= ntc - 1:
            ln_tm_part2(pl, st, 15 + a, io["xn0_fm"])
        if 16 + a <= ntc - 1:
            ln_tm_part1(pl, st, 16 + a, io["x_tm"], io["xn0_tm"])

    b1_sb = pl.sb("b1_sb", [128, 32], F32)
    b2_sb = pl.sb("b2_sb", [128, 32], F32)
    b3_sb = pl.sb("b3_sb", [128, 32], F32)
    pl.dma("sync", "bld", b1_sb[:], io["b1c"])
    pl.dma("sync", "bld", b2_sb[:], io["b2c"])
    pl.dma("sync", "bld", b3_sb[:], io["b3c"])
    pl.wait_slot("scalar", "bld")
    pl.wait_slot("vector", "bld")
    l0_ffn_c1(pl, st["xn_res"], io["w12T"], b1_sb, b2_sb, io["x4d"], TOWN, pb,
              hook=lnhook)
    pl.free("xn_res")
    pl.free("bD", "snk", "bA", "sA")
    pl.waits("sync", pl.tok("tensor", "vector", "scalar") + pl.all_slots_tok())
    x4_res = load_resident_fm(pl, "x4_res", io["x4d"], [(0, TOWN)])
    pl.wait_slot("tensor", "x4_res")
    l0_ffn_c2(pl, x4_res, io["w3T"], b3_sb, io["x_fm"], io["x1p"], TOWN,
              [(0, TOWN)], pb)
    for nme in ("c2oB", "c2oA", "c2xB", "c2xA", "x4_res"):
        pl.free(nme)
    for nme in ("x4rowB", "x4rowA", "silu_sb", "wgtD", "wgtC", "wgtB", "wgtA"):
        pl.free(nme)
    l0_attn(pl, io["xn0_fm"], io["xn0_tm"], io["masks"], io["x1p"], io["x1"],
            qk_cols, vb_cols, pb)


def declare_l0_io(nc):
    io = {}
    io["x_fm"] = nc.dram_tensor("x_fm", [D, TFULL], F32, kind="ExternalInput").ap()
    io["x_tm"] = nc.dram_tensor("x_tm", [TFULL, D], F32, kind="ExternalInput").ap()
    io["masks"] = nc.dram_tensor("masks", [4, 8, 128, 512], BF16, kind="ExternalInput").ap()
    for nme in ("bv_q", "bv_k", "bv_v"):
        io[nme] = nc.dram_tensor(nme, [4, 1024], F32, kind="ExternalInput").ap()
    io["w12T"] = nc.dram_tensor("w12T", [4096, 8192], F32, kind="ExternalInput").ap()
    io["w3T"] = nc.dram_tensor("w3T", [4096, 4096], F32, kind="ExternalInput").ap()
    for nme in ("b1c", "b2c", "b3c"):
        io[nme] = nc.dram_tensor(nme, [128, 32], F32, kind="ExternalInput").ap()
    io["xn0_fm"] = nc.dram_tensor("xn0_fm", [D, TFULL], BF16, kind="Internal").ap()
    io["xn0_tm"] = nc.dram_tensor("xn0_tm", [TFULL, D], BF16, kind="Internal").ap()
    io["rows_r"] = nc.dram_tensor("rows_r", [1, TFULL], F32, kind="Internal").ap()
    io["rows_n"] = nc.dram_tensor("rows_n", [1, TFULL], F32, kind="Internal").ap()
    io["x4d"] = nc.dram_tensor("x4d", [4096, TOWN], BF16, kind="Internal").ap()
    io["x1p"] = nc.dram_tensor("x1p", [D, TOWN], F32,
                               kind="ExternalOutput" if DEBUG_X1P else "Internal").ap()
    io["qkd"] = nc.dram_tensor("qkd", [4096], F32, kind="Internal").ap()
    io["vbd"] = nc.dram_tensor("vbd", [4096], F32, kind="Internal").ap()
    return io


def build_l0_program():
    nc = bass.Bass("TRN2", target_bir_lowering=False)
    stack = ExitStack()
    io = declare_l0_io(nc)
    io["x1"] = nc.dram_tensor("x1", [D, TOWN], F32, kind="ExternalOutput").ap()
    with stack:
        pl = Planner(nc, stack)
        declare_and_load_consts(pl)
        io["pb"] = [pl.ps(f"pb{i}", [128, 512], F32) for i in range(8)]
        for e in ("tensor", "vector", "scalar"):
            pl.wait_slot(e, "consts")
        emit_l0(pl, io)
        pl.fence_all()
        pl.after("sync", "scalar", "gpsimd", "vector", "tensor")
        counts = pl.finalize()
    return nc, counts


def prep_l0_inputs(x, bv_q, bv_k, bv_v, w12, b12, w3, b3, consts):
    shared = {
        "bv_q": np.asarray(bv_q, dtype=np.float32),
        "bv_k": np.asarray(bv_k, dtype=np.float32),
        "bv_v": np.asarray(bv_v, dtype=np.float32),
        "w12T": np.ascontiguousarray(np.asarray(w12).T),
        "b1c": np.ascontiguousarray(np.asarray(b12)[:4096].reshape(32, 128).T),
        "b2c": np.ascontiguousarray(np.asarray(b12)[4096:].reshape(32, 128).T),
        "w3T": np.ascontiguousarray(np.asarray(w3).T),
        "b3c": np.ascontiguousarray(np.asarray(b3).reshape(32, 128).T),
    }
    shared.update(host_const_inputs(consts))
    per_core = []
    for c in range(8):
        b, h = c // 2, c % 2
        cols = host_col_order(h)
        xb = np.asarray(x[b], dtype=np.float32)[cols]
        m = dict(shared)
        m["x_tm"] = np.ascontiguousarray(xb)
        m["x_fm"] = np.ascontiguousarray(xb.T)
        m["masks"] = host_masks(h)
        per_core.append(m)
    return per_core


def place_own_cols(out_full_b, own_cols_arr, h):
    cols = host_col_order(h)[:TOWN]
    out_full_b[cols] = own_cols_arr


def _rope_tile(pl, bank, bank_swap, cosd, sinds, out_slice, swap_bf, bufs, e="vector"):
    t, s = pl.t, pl.s
    ee = pl.eng[e]
    raw, swp, t1, t2 = bufs["raw"], bufs["swp"], bufs["t1"], bufs["t2"]
    mm_tok = pl.tok("tensor")
    pl.waits("scalar", mm_tok)
    s.copy(out=raw[:, :], in_=bank[:, :])
    pl.after("tensor", "scalar")
    t.matmul(bank_swap[:, :], swap_bf[:], raw[:, :], start=True, stop=True, auto=False)
    pl.tmark()
    pl.after("scalar", "tensor")
    s.copy(out=swp[:, :], in_=bank_swap[:, :])
    pl.after(e, "scalar")
    ee.tensor_tensor(t1[:, :], raw[:, :], cosd, op=ALU.mult)
    ee.tensor_tensor(t2[:, :], swp[:, :], sinds, op=ALU.mult)
    ee.tensor_tensor(out_slice, t1[:, :], t2[:, :], op=ALU.add)


def l1_qkv_q(pl, xn_res, wqkvp, ropeq_cos, ropeq_sin, q1_dram, pbanks, T, swap_bf):
    t = pl.t
    wgt = [pl.sb("wgtA", [128, NCHUNK, 128], BF16), pl.sb("wgtB", [128, NCHUNK, 128], BF16)]
    bufs = {n: pl.sb("rp_" + n, [128, 512], BF16, side="right") for n in ("raw", "swp", "t1", "t2")}
    q1row = [pl.sb("q1rowA", [128, 2048], BF16, side="right"),
             pl.sb("q1rowB", [128, 2048], BF16, side="right")]
    ntt = T // 512

    entry = pl.tok("tensor", "vector", "scalar", "gpsimd") + pl.all_slots_tok()
    wtok = [entry, entry]
    btok = [entry] * 2
    rtok = [entry, entry]
    rope_tok = entry
    for oc in range(32):
        stream_w_oc(pl, wqkvp, NCHUNK, oc * 128, 128, wgt[oc % 2], "wgt" + str(oc % 2),
                    war_tok=wtok[oc % 2])
        pl.wait_slot("tensor", "wgt" + str(oc % 2))
        for tt in range(ntt):
            bank = pbanks[tt % 2]
            bank_swap = pbanks[2 + tt % 2]
            pl.waits("tensor", btok[tt % 2])
            for ic in range(NCHUNK):
                t.matmul(bank[:, :], wgt[oc % 2][:, ic, :], xn_res[:, ic, 512 * tt:512 * (tt + 1)],
                         start=(ic == 0), stop=(ic == NCHUNK - 1), auto=False)
            pl.tmark()
            pl.waits("scalar", rope_tok)
            if tt == 0:
                pl.waits("vector", rtok[oc % 2])
            _rope_tile(pl, bank, bank_swap, ropeq_cos[:, 512 * tt:512 * (tt + 1)],
                       ropeq_sin[:, 512 * tt:512 * (tt + 1)],
                       q1row[oc % 2][:, 512 * tt:512 * (tt + 1)], swap_bf, bufs)
            rope_tok = pl.tok("vector")
            btok[tt % 2] = rope_tok + pl.tok("scalar")
        wtok[oc % 2] = pl.tok("tensor")
        pl.after("scalar", "vector")
        pl.dma("scalar", "q1row" + str(oc % 2),
               q1_dram[128 * oc:128 * (oc + 1), 0:T], q1row[oc % 2][:, 0:T])
        rtok[oc % 2] = pl.stok("q1row" + str(oc % 2))
    pl.fence("scalar")
    pl.waits("gpsimd", pl.tok("tensor", "vector", "scalar"))
    for nme in ("q1rowB", "q1rowA", "rp_t2", "rp_t1", "rp_swp", "rp_raw"):
        pl.free(nme)


def l1_qkv_kv(pl, xn_src_fm, Tkv, wqkvp, ropek_cos, ropek_sin, k1_dram, v1t_dram,
              pbanks, swap_bf, rank_order=False):
    t, v, s, g = pl.t, pl.v, pl.s, pl.g
    idb = pl.sbufs["c_ident_bf"]
    wkv = pl.sb("wkv_res", [128, NCHUNK, 2048], BF16, side="right")
    xt = pl.sb("kv_xt", [128, NCHUNK, 512], BF16, side="right")
    bufs = {n: pl.sb("rp_" + n, [128, 512], BF16, side="right") for n in ("raw", "swp", "t1", "t2")}
    krow = pl.sb("krow", [128, 512], BF16, side="right")
    vt_sb = pl.sb("vt_sb", [128, 4, 1024], BF16, side="right")
    pbt = pbanks[7][:, :].bitcast(BF16)

    entry = pl.tok("tensor", "vector", "scalar", "gpsimd") + pl.all_slots_tok()
    pl.waits("gpsimd", entry)
    for ic in range(NCHUNK):
        pl.dma("gpsimd", "wkv_res", wkv[:, ic, :],
               wqkvp[128 * ic:128 * (ic + 1), 4096:6144])
    pl.wait_slot("tensor", "wkv_res")

    xtok = entry
    rope_tok = entry
    ktok = [entry, entry]
    vtok = entry
    for tt in range(Tkv // 512):
        pl.waits("sync", xtok)
        if rank_order:
            rk, c0 = tt // 4, 512 * (tt % 4)
            xview = xn_src_fm[rk * D:(rk + 1) * D, c0:c0 + 512]
        else:
            xview = xn_src_fm[:, 512 * tt:512 * (tt + 1)]
        pl.dma("sync", "kv_xt", xt[:, :, :],
               xview.rearrange("(ic p) j -> p ic j", p=128))
        pl.wait_slot("tensor", "kv_xt")
        for oc in range(16):
            bank = pbanks[oc % 2]
            bank2 = pbanks[2 + oc % 2]
            pl.waits("tensor", ktok[oc % 2])
            for ic in range(NCHUNK):
                t.matmul(bank[:, :], wkv[:, ic, 128 * oc:128 * (oc + 1)], xt[:, ic, :],
                         start=(ic == 0), stop=(ic == NCHUNK - 1), auto=False)
            pl.tmark()
            if oc < 8:
                pl.waits("scalar", rope_tok)
                pl.waits("vector", pl.stok("krow"))
                _rope_tile(pl, bank, bank2, ropek_cos[:, 512 * tt:512 * (tt + 1)],
                           ropek_sin[:, 512 * tt:512 * (tt + 1)], krow[:, :], swap_bf, bufs)
                rope_tok = pl.tok("vector")
                ktok[oc % 2] = rope_tok + pl.tok("scalar")
                pl.after("scalar", "vector")
                pl.dma("scalar", "krow",
                       k1_dram[128 * oc:128 * (oc + 1), 512 * tt:512 * (tt + 1)], krow[:, :])
            else:
                mm_tok = pl.tok("tensor")
                pl.waits("scalar", mm_tok)
                pl.waits("scalar", pl.tok("tensor"))
                s.copy(out=bufs["raw"][:, :], in_=bank[:, :])
                pl.after("tensor", "scalar")
                if oc == 8:
                    pl.waits("tensor", vtok)
                for j in range(4):
                    t.transpose(pbt[:, 128 * j:128 * (j + 1)],
                                bufs["raw"][:, 128 * j:128 * (j + 1)], idb[:], auto=False)
                pl.tmark()
                pl.after("vector", "tensor")
                pl.waits("vector", pl.stok("vt_sb"))
                for j in range(4):
                    v.tensor_copy(vt_sb[:, j, 128 * (oc - 8):128 * (oc - 7)],
                                  pbt[:, 128 * j:128 * (j + 1)])
                rope_tok = rope_tok + pl.tok("vector")
                ktok[oc % 2] = pl.tok("vector")
        pl.after("sync", "vector")
        for j in range(4):
            pl.dma("sync", "vt_sb", v1t_dram[512 * tt + 128 * j:512 * tt + 128 * (j + 1), :],
                   vt_sb[:, j, :])
        vtok = pl.tok("vector")
        xtok = pl.tok("tensor")
    pl.fence("scalar")
    pl.fence("sync")
    pl.waits("gpsimd", pl.tok("tensor", "vector", "scalar"))
    for nme in ("vt_sb", "krow", "rp_t2", "rp_t1", "rp_swp", "rp_raw", "kv_xt", "wkv_res"):
        pl.free(nme)


def l1_moe(pl, xn_res, WrT, A2, B2, E_dram, x1_fm, x2p_dram, pbanks, T):
    t, v, s, g = pl.t, pl.v, pl.s, pl.g
    idf = pl.sbufs["c_ident_f32"]
    ntt = T // 512
    wr_sb = pl.sb("wr_sb", [128, NCHUNK, 64], BF16, side="right")
    sc_sb = pl.sb("sc_sb", [64, 2048], F32, side="right")
    st_tm = pl.sb("st_tm", [128, 16, 64], F32, side="right")
    tk = {n: pl.sb("tk_" + n, [128, 64], F32, side="right")
          for n in ("m16", "srep", "srep2", "ge", "mask")}
    Gfm = pl.sb("Gfm", [64, 2048], BF16, side="right")
    e_sb = pl.sb("e_sb", [64, 1024], BF16, side="right")
    wgt = [pl.sb("mwgtA", [128, NCHUNK, 128], BF16, side="right"),
           pl.sb("mwgtB", [128, NCHUNK, 128], BF16, side="right")]
    hg_tt = pl.sb("hg_tt", [128, 8, 512], BF16, side="right")
    gexp = pl.sb("gexp", [128, 512], BF16, side="right")
    x1b = [pl.sb("mx1A", [128, 512], F32, side="right"),
           pl.sb("mx1B", [128, 512], F32, side="right")]
    x2b = [pl.sb("mx2A", [128, 512], F32, side="right"),
           pl.sb("mx2B", [128, 512], F32, side="right")]

    entry = pl.tok("tensor", "vector", "scalar", "gpsimd") + pl.all_slots_tok()
    pl.waits("gpsimd", entry)
    stream_w_oc(pl, WrT, NCHUNK, 0, 64, wr_sb, "wr_sb", war_tok=None)
    pl.dma("gpsimd", "e_sb", e_sb[:, :], E_dram)
    pl.wait_slot("tensor", "wr_sb")
    pl.waits("tensor", entry)
    for tt in range(ntt):
        bank = pbanks[tt % 2]
        if tt >= 2:
            pl.after("tensor", "scalar")
        for ic in range(NCHUNK):
            t.matmul(bank[0:64, :], wr_sb[:, ic, :], xn_res[:, ic, 512 * tt:512 * (tt + 1)],
                     start=(ic == 0), stop=(ic == NCHUNK - 1), auto=False)
        pl.tmark()
        pl.after("scalar", "tensor")
        s.copy(out=sc_sb[:, 512 * tt:512 * (tt + 1)], in_=bank[0:64, :])
    pl.after("tensor", "scalar")
    for tb in range(T // 128):
        bank = pbanks[4 + tb % 2]
        if tb >= 2:
            pl.after("tensor", "vector")
        t.transpose(bank[0:128, 0:64], sc_sb[0:64, 128 * tb:128 * (tb + 1)],
                    idf[0:64, 0:64], auto=False)
        pl.tmark()
        pl.after("vector", "tensor")
        v.tensor_copy(st_tm[:, tb, 0:64], bank[0:128, 0:64])
    for tb in range(T // 128):
        pl.after("vector", "scalar")
        v.max(out=tk["m16"][:, 0:8], in_=st_tm[:, tb, :])
        v.match_replace(out=tk["srep"][:, :], in_to_replace=tk["m16"][:, 0:8],
                        in_values=st_tm[:, tb, :], imm_value=-1e30)
        v.max(out=tk["m16"][:, 8:16], in_=tk["srep"][:, :])
        v.match_replace(out=tk["srep2"][:, :], in_to_replace=tk["m16"][:, 8:16],
                        in_values=tk["srep"][:, :], imm_value=-1e30)
        v.tensor_scalar_mul(tk["m16"][:, 16:17], tk["m16"][:, 0:1], -1.0)
        pl.after("scalar", "vector")
        s.activation(out=tk["m16"][:, 32:48], in_=tk["m16"][:, 0:16], func=AF.Exp,
                     bias=tk["m16"][:, 16:17])
        s.activation(out=tk["ge"][:, :], in_=st_tm[:, tb, :], func=AF.Exp,
                     bias=tk["m16"][:, 16:17])
        pl.after("vector", "scalar")
        v.reduce_sum(tk["m16"][:, 17:18], tk["m16"][:, 32:48], axis=AX.X)
        v.reciprocal(tk["m16"][:, 18:19], tk["m16"][:, 17:18])
        v.tensor_scalar_mul(tk["m16"][:, 18:19], tk["m16"][:, 18:19], 2.0)
        v.tensor_tensor(tk["mask"][:, :], st_tm[:, tb, :], tk["srep2"][:, :], op=ALU.is_gt)
        v.scalar_tensor_tensor(out=st_tm[:, tb, :], in0=tk["ge"][:, :],
                               scalar=tk["m16"][:, 18:19], in1=tk["mask"][:, :],
                               op0=ALU.mult, op1=ALU.mult)
    for tb in range(T // 128):
        bank = pbanks[4 + tb % 2]
        pl.after("tensor", "vector", "scalar")
        t.transpose(bank[0:64, 0:128], st_tm[:, tb, 0:64], idf[:], auto=False)
        pl.tmark()
        pl.after("vector", "tensor")
        v.tensor_copy(Gfm[0:64, 128 * tb:128 * (tb + 1)], bank[0:64, 0:128])
    wtok = [pl.tok("tensor"), pl.tok("tensor")]
    btok = [pl.tok("vector")] * 2
    x1tok = [pl.tok("vector"), pl.tok("vector")]
    x2tok = [None, None]
    pl.wait_slot("tensor", "e_sb")
    for tt in range(ntt):
        for krc in range(8):
            it = tt * 8 + krc
            stream_w_oc(pl, A2, NCHUNK, krc * 128, 128, wgt[it % 2], "mwgt" + str(it % 2),
                        war_tok=wtok[it % 2])
            bankG = pbanks[4 + krc % 2]
            bankH = pbanks[2 + krc % 2]
            pl.after("tensor", "scalar", "vector")
            t.matmul(bankG[:, :], e_sb[:, 128 * krc:128 * (krc + 1)],
                     Gfm[0:64, 512 * tt:512 * (tt + 1)], start=True, stop=True, auto=False)
            pl.tmark()
            pl.after("scalar", "tensor")
            pl.after("scalar", "vector")
            s.copy(out=gexp[:, :], in_=bankG[:, :])
            pl.wait_slot("tensor", "mwgt" + str(it % 2))
            for ic in range(NCHUNK):
                t.matmul(bankH[:, :], wgt[it % 2][:, ic, :], xn_res[:, ic, 512 * tt:512 * (tt + 1)],
                         start=(ic == 0), stop=(ic == NCHUNK - 1), auto=False)
            pl.tmark()
            wtok[it % 2] = pl.tok("tensor")
            pl.after("vector", "tensor", "scalar")
            v.tensor_tensor(hg_tt[:, krc, :], bankH[:, :], gexp[:, :], op=ALU.mult)
        hgrdy = pl.tok("vector")
        for nc_ in range(32):
            it = tt * 32 + nc_
            stream_w_oc(pl, B2, 8, nc_ * 128, 128, wgt[it % 2], "mwgt" + str(it % 2),
                        war_tok=wtok[it % 2])
            bank = pbanks[nc_ % 2]
            pl.waits("sync", x1tok[nc_ % 2])
            pl.dma("sync", "mx1" + str(nc_ % 2), x1b[nc_ % 2][:, :],
                   x1_fm[128 * nc_:128 * (nc_ + 1), 512 * tt:512 * (tt + 1)])
            pl.wait_slot("tensor", "mwgt" + str(it % 2))
            pl.waits("tensor", btok[nc_ % 2], hgrdy)
            for ic in range(8):
                t.matmul(bank[:, :], wgt[it % 2][:, ic, :], hg_tt[:, ic, :],
                         start=(ic == 0), stop=(ic == 7), auto=False)
            pl.tmark()
            wtok[it % 2] = pl.tok("tensor")
            pl.waits("vector", pl.tok("tensor"))
            pl.wait_slot("vector", "mx1" + str(nc_ % 2))
            pl.waits("vector", x2tok[nc_ % 2])
            v.tensor_tensor(x2b[nc_ % 2][:, :], bank[:, :], x1b[nc_ % 2][:, :], op=ALU.add)
            btok[nc_ % 2] = pl.tok("vector")
            x1tok[nc_ % 2] = pl.tok("vector")
            pl.after("scalar", "vector")
            pl.dma("scalar", "mx2" + str(nc_ % 2),
                   x2p_dram[128 * nc_:128 * (nc_ + 1), 512 * tt:512 * (tt + 1)],
                   x2b[nc_ % 2][:, :])
            x2tok[nc_ % 2] = pl.stok("mx2" + str(nc_ % 2))
    for nme in ("mx2B", "mx2A", "mx1B", "mx1A", "gexp", "hg_tt", "mwgtB", "mwgtA",
                "e_sb", "Gfm", "tk_mask", "tk_ge", "tk_srep2", "tk_srep", "tk_m16",
                "st_tm", "sc_sb", "wr_sb"):
        pl.free(nme)


def l1_attn(pl, masks_dram, q1_dram, k1_dram, v1t_dram, Tkv, pbanks, kg_list=range(8), ql_list=range(4)):
    t, v, s, g = pl.t, pl.v, pl.s, pl.g
    ones_bf = pl.sbufs["c_ones_bf_col"]
    SC = 0.08838834764831845
    C2 = (SC * 1.0002) ** 2

    ao = pl.sb("ao_res", [128, NCHUNK, 2048], BF16, side="right")
    Kg = [pl.sb("l1KA", [128, Tkv], BF16, side="right"),
          pl.sb("l1KB", [128, Tkv], BF16, side="right")]
    Vg = pl.sb("l1V", [128, Tkv // 128, 128], BF16, side="right")
    Pt = pl.sb("l1P", [128, 32, 512], BF16, side="right")
    qre = [pl.sb("l1qA", [128, 2048], BF16, side="right"),
           pl.sb("l1qB", [128, 2048], BF16, side="right")]
    mre = pl.sb("l1mA", [128, 8, 512], BF16, side="right")
    ubias = pl.sb("l1ub", [128, 16], F32, side="right")
    stat = pl.sb("l1stat", [128, 16], F32, side="right")
    qmax = pl.sb("l1qmax", [1, 128], F32, side="right")
    negr = pl.sb("l1neg", [1, 128], F32, side="right")
    rnb = pl.sb("l1rnb", [128, 512], BF16, side="right")
    nrow = Pt[0:1, 30:32, :].rearrange("p a b -> p (a b)").bitcast(F32)
    rn_bf = pl.sb("l1rnr", [1, 512], BF16, side="right")

    pl.after("gpsimd", "tensor", "vector", "scalar")
    g.memset(stat[:, :], 0.0)
    g.memset(qmax[:, :], 0.0)
    g.memset(negr[:, :], -1.0)
    entry = pl.tok("tensor", "vector", "scalar", "gpsimd") + pl.all_slots_tok()
    pl.waits("tensor", entry)
    QKB = [pbanks[0], pbanks[1], pbanks[4], pbanks[5]]
    klist = list(kg_list)
    qlist = list(ql_list)
    sched = [(kg_, ql_) for kg_ in klist for ql_ in qlist]
    half = Tkv // 256

    def load_K(kg_):
        pl.dma("sync", "l1K" + str(kg_ % 2), Kg[kg_ % 2][:, :],
               k1_dram[128 * kg_:128 * (kg_ + 1), 0:Tkv])

    def load_V(kg_):
        pl.dma("sync", "l1V", Vg[:, 0:half, :],
               v1t_dram[0:Tkv // 2, 128 * kg_:128 * (kg_ + 1)].rearrange("(sb p) d -> p sb d", p=128))
        pl.dma("sync", "l1V", Vg[:, half:2 * half, :],
               v1t_dram[Tkv // 2:Tkv, 128 * kg_:128 * (kg_ + 1)].rearrange("(sb p) d -> p sb d", p=128))

    def load_q(qh_):
        pl.dma("sync", "l1q" + str(qh_ % 2), qre[qh_ % 2][:, :],
               q1_dram[128 * qh_:128 * (qh_ + 1), :])

    pl.waits("sync", entry)
    load_K(klist[0])
    load_V(klist[0])

    sqtok = [entry, entry]
    mmtok = [entry, entry]
    for qh in range(32):
        qb = qh % 2
        pl.waits("sync", sqtok[qb])
        load_q(qh)
        eng = "scalar" if qb == 0 else "vector"
        pl.wait_slot(eng, "l1q" + str(qb))
        pl.waits(eng, mmtok[qb])
        reg = Pt[:, 4 * qb:4 * qb + 4, :]
        if qb == 0:
            s.activation(out=reg, in_=qre[qb][:, :], func=AF.Square)
        else:
            v.tensor_tensor(reg, qre[qb][:, :], qre[qb][:, :], op=ALU.mult)
        sqtok[qb] = pl.tok(eng)
        pl.waits("tensor", sqtok[qb])
        pl.after("tensor", "vector")
        for jj in range(4):
            t.matmul(QKB[jj][0:1, :], ones_bf[:], Pt[:, 4 * qb + jj, :],
                     start=True, stop=True, auto=False)
        pl.tmark()
        mmtok[qb] = pl.tok("tensor")
        pl.waits("vector", mmtok[qb])
        for jj in range(4):
            v.reduce_max(qmax[0:1, 4 * qh + jj:4 * qh + jj + 1],
                         QKB[jj][0:1, :], axis=AX.X)

    vtok = entry
    kvtok = [entry, entry]
    qtok = [sqtok[0], sqtok[1]]
    mtok = entry
    otok = [entry, entry]
    pend = {"bankO": None}
    ti_glob = 0
    for ki, kg in enumerate(klist):
        kgb = kg % 2
        if ki > 0:
            pl.waits("sync", vtok)
            load_V(kg)
        if ki == 0:
            p0 = qlist[0] % 2
            pl.waits("sync", qtok[p0])
            load_q(4 * kg + qlist[0])
        if ki + 1 < len(klist):
            ng = klist[ki + 1]
            pl.waits("sync", kvtok[ng % 2])
            load_K(ng)
        pl.wait_slot("scalar", "l1K" + str(kgb))
        pl.waits("scalar", pl.tok("tensor"))
        s.activation(out=Pt[:, 0:8, :], in_=Kg[kgb][:, :], func=AF.Square)
        kst = pl.tok("scalar")
        pl.waits("tensor", kst)
        pl.after("tensor", "vector")
        pl.wait_slot("tensor", "l1V")
        for jj in range(Tkv // 512):
            if jj == 4:
                pl.after("tensor", "vector")
            t.matmul(QKB[jj % 4][0:1, :], ones_bf[:], Pt[:, jj, :],
                     start=True, stop=True, auto=False)
            if jj % 4 == 3:
                pl.tmark()
                grp = pl.tok("tensor")
                pl.waits("vector", grp)
                for j2 in range(jj - 3, jj + 1):
                    v.reduce_max(stat[0:1, j2:j2 + 1], QKB[j2 % 4][0:1, :],
                                 axis=AX.X)
        v.reduce_max(stat[0:1, 8:9], stat[0:1, 0:8], axis=AX.X)
        v.tensor_scalar(out=nrow[0:1, 0:16], in0=qmax[0:1, 16 * kg:16 * kg + 16],
                        scalar1=stat[0:1, 8:9], scalar2=C2,
                        op0=ALU.mult, op1=ALU.mult)
        pl.after("scalar", "vector")
        s.activation(out=nrow[0:1, 16:32], in_=nrow[0:1, 0:16], func=AF.Sqrt)
        pl.after("tensor", "vector")
        pl.after("tensor", "scalar")
        t.matmul(pbanks[6][0:128, 0:16], negr[:], nrow[0:1, 16:32],
                 start=True, stop=True, auto=False)
        pl.tmark()
        pl.after("scalar", "tensor")
        s.copy(out=ubias[:, :], in_=pbanks[6][0:128, 0:16])

        for qi, ql in enumerate(qlist):
            qh = 4 * kg + ql
            qb = ql % 2
            pl.wait_slot("tensor", "l1q" + str(qb))
            prev_tail = [pl.tok("scalar") + pl.tok("vector")] * 4
            for ti in range(4):
                blocks, win = tile_blocks(ti)
                if ti == 3:
                    idx = ki * len(qlist) + qi
                    if idx + 1 < len(sched):
                        nkg, nql = sched[idx + 1]
                        pl.waits("sync", qtok[nql % 2])
                        load_q(4 * nkg + nql)
                pl.waits("gpsimd", mtok)
                pl.dma("gpsimd", "l1mA", mre[:, :, :],
                       masks_dram[ti].rearrange("k p j -> p k j"))
                win_set = {blk: jx for jx, blk in enumerate(win)}
                nonwin = [b_ for b_ in blocks if b_ not in win_set]
                blocks = nonwin[0:4] + win + nonwin[4:]
                bankO = pbanks[2 + ti_glob % 2]
                bankN = pbanks[6]
                exp_toks = []
                nb = len(blocks)
                for si, sb in enumerate(blocks):
                    bank = QKB[si % 4]
                    pl.waits("tensor", exp_toks[si - 4] if si >= 4 else prev_tail[si])
                    t.matmul(bank[:, :], Kg[kgb][:, 128 * sb:128 * (sb + 1)],
                             qre[qb][:, 512 * ti:512 * (ti + 1)],
                             start=True, stop=True, auto=False)
                    pl.tmark()
                    mm_tok = pl.tok("tensor")
                    pl.waits("scalar", mm_tok)
                    s.activation(out=Pt[:, sb, :], in_=bank[:, :], func=AF.Exp,
                                 bias=ubias[:, 4 * ql + ti:4 * ql + ti + 1],
                                 scale=SC)
                    if sb in win_set:
                        jx = win_set[sb]
                        pl.after("vector", "scalar")
                        pl.wait_slot("vector", "l1mA")
                        v.tensor_tensor(Pt[:, sb, :], Pt[:, sb, :],
                                        mre[:, jx, :], op=ALU.mult)
                        exp_toks.append(pl.tok("vector"))
                    else:
                        exp_toks.append(pl.tok("scalar"))
                    if si == 2 and pend["bankO"] is not None:
                        pl.waits("tensor", pend["nc_tok"])
                        pl.after("tensor", "scalar")
                        t.matmul(bankN[:, :], pl.sbufs["c_ones_bf_row"][:],
                                 pend["rn"], start=True, stop=True, auto=False)
                        pl.tmark()
                        pl.after("scalar", "tensor")
                        s.copy(out=rnb[:, :], in_=bankN[:, :])
                        pl.after("vector", "scalar")
                        v.tensor_tensor(pend["ao"], pend["bankO"][:, :], rnb[:, :],
                                        op=ALU.mult)
                        otok[pend["par"]] = pl.tok("vector")
                        pend["bankO"] = None
                    if si >= 2:
                        pb_ = blocks[si - 2]
                        pl.waits("tensor", exp_toks[si - 2])
                        if si == 2:
                            pl.waits("tensor", otok[ti_glob % 2])
                            pl.after("tensor", "scalar")
                        t.matmul(bankO[:, :], Vg[:, pb_, :], Pt[:, pb_, :],
                                 start=(si - 2 == 0), stop=False, auto=False)
                        t.matmul(bankN[0:1, :], ones_bf[:], Pt[:, pb_, :],
                                 start=(si - 2 == 0), stop=False, auto=False)
                        pl.tmark()
                for tail in (nb - 2, nb - 1):
                    pb_ = blocks[tail]
                    pl.waits("tensor", exp_toks[tail])
                    t.matmul(bankO[:, :], Vg[:, pb_, :], Pt[:, pb_, :],
                             start=(tail == 0), stop=(tail == nb - 1), auto=False)
                    t.matmul(bankN[0:1, :], ones_bf[:], Pt[:, pb_, :],
                             start=(tail == 0), stop=(tail == nb - 1), auto=False)
                    pl.tmark()
                prev_tail = [exp_toks[-4], exp_toks[-3], exp_toks[-2], exp_toks[-1]]
                nrm_tok = pl.tok("tensor")
                pl.waits("scalar", nrm_tok)
                s.activation(out=nrow[0:1, :], in_=bankN[0:1, :], func=AF.Ln)
                s.activation(out=rn_bf[0:1, :], in_=nrow[0:1, :], func=AF.Exp,
                             scale=-1.0)
                pend["bankO"] = bankO
                pend["ao"] = ao[:, qh, 512 * ti:512 * (ti + 1)]
                pend["rn"] = rn_bf[0:1, :]
                pend["nc_tok"] = pl.tok("vector")
                pend["par"] = ti_glob % 2
                mtok = pl.tok("vector", "gpsimd")
                ti_glob += 1
            qtok[qb] = pl.tok("tensor")
        kvtok[kgb] = pl.tok("tensor", "scalar")
        vtok = pl.tok("tensor")
    if pend["bankO"] is not None:
        pl.waits("tensor", pend["nc_tok"])
        pl.after("tensor", "scalar")
        t.matmul(pbanks[6][:, :], pl.sbufs["c_ones_bf_row"][:], pend["rn"],
                 start=True, stop=True, auto=False)
        pl.tmark()
        pl.after("scalar", "tensor")
        s.copy(out=rnb[:, :], in_=pbanks[6][:, :])
        pl.after("vector", "scalar")
        v.tensor_tensor(pend["ao"], pend["bankO"][:, :], rnb[:, :], op=ALU.mult)
    return ao


def l1_wo(pl, ao_res, woT, x2p_dram, out_dram, pbanks, T):
    t, v = pl.t, pl.v
    wgt = [pl.sb("wgtA", [128, NCHUNK, 128], BF16), pl.sb("wgtB", [128, NCHUNK, 128], BF16)]
    x2b = [pl.sb("wx2A", [128, 512], F32, side="right"),
           pl.sb("wx2B", [128, 512], F32, side="right")]
    orow = [pl.sb("woA", [128, 2048], F32, side="right"),
            pl.sb("woB", [128, 2048], F32, side="right")]
    ntt = T // 512

    entry = pl.tok("tensor", "vector", "scalar", "gpsimd") + pl.all_slots_tok()
    wtok = [entry, entry]
    btok = [entry] * 4
    xtok = [entry, entry]
    otok = [entry, entry]
    for nc_ in range(32):
        stream_w_oc(pl, woT, NCHUNK, nc_ * 128, 128, wgt[nc_ % 2], "wgt" + str(nc_ % 2),
                    war_tok=wtok[nc_ % 2])
        pl.wait_slot("tensor", "wgt" + str(nc_ % 2))
        for tt in range(ntt):
            bank = pbanks[(nc_ * ntt + tt) % 4]
            pl.waits("sync", xtok[tt % 2])
            pl.dma("sync", "wx2" + str(tt % 2), x2b[tt % 2][:, :],
                   x2p_dram[128 * nc_:128 * (nc_ + 1), 512 * tt:512 * (tt + 1)])
            pl.waits("tensor", btok[(nc_ * ntt + tt) % 4])
            for ic in range(NCHUNK):
                t.matmul(bank[:, :], wgt[nc_ % 2][:, ic, :], ao_res[:, ic, 512 * tt:512 * (tt + 1)],
                         start=(ic == 0), stop=(ic == NCHUNK - 1), auto=False)
            pl.tmark()
            mm_tok = pl.tok("tensor")
            pl.waits("vector", mm_tok)
            pl.wait_slot("vector", "wx2" + str(tt % 2))
            if tt <= 1:
                pl.waits("vector", otok[nc_ % 2])
            v.tensor_tensor(orow[nc_ % 2][:, 512 * tt:512 * (tt + 1)], bank[:, :],
                            x2b[tt % 2][:, :], op=ALU.add)
            btok[(nc_ * ntt + tt) % 4] = pl.tok("vector")
            xtok[tt % 2] = pl.tok("vector")
        wtok[nc_ % 2] = pl.tok("tensor")
        pl.after("scalar", "vector")
        pl.dma("scalar", "wo_out" + str(nc_ % 2),
               out_dram[128 * nc_:128 * (nc_ + 1), 0:T], orow[nc_ % 2][:, 0:T])
        otok[nc_ % 2] = pl.stok("wo_out" + str(nc_ % 2))


def emit_l1(pl, io):
    pb = io["pb"]
    pl.pool()
    ln_fm_phase(pl, TFULL, io["x1_fm"], io["xn1_fm"], "A1", pb)
    pl.free_pool()
    pl.waits("sync", pl.tok("vector", "gpsimd", "scalar", "tensor") + pl.all_slots_tok())
    xn_res = load_resident_fm(pl, "xn1_res", io["xn1_fm"], [(0, TOWN)])
    pl.wait_slot("tensor", "xn1_res")
    rqc = pl.sb("rqc", [128, 2048], BF16, side="right")
    rqs = pl.sb("rqs", [128, 2048], BF16, side="right")
    pl.dma("sync", "ropeq", rqc[:], io["rqc"])
    pl.dma("sync", "ropeq", rqs[:], io["rqs"])
    pl.wait_slot("vector", "ropeq")
    pl.wait_slot("tensor", "ropeq")
    l1_qkv_q(pl, xn_res, io["wqkvp"], rqc, rqs, io["q1"], pb, TOWN,
             pl.sbufs["c_swap_bf"])
    pl.free("rqs", "rqc")
    l1_moe(pl, xn_res, io["WrT"], io["A2"], io["B2"], io["E_exp"],
           io["x1_fm"][:, 0:TOWN], io["x2p"], pb, TOWN)
    pl.free("xn1_res")
    pl.free("wgtB", "wgtA")
    rkc = pl.sb("rkc", [128, TFULL], BF16, side="right")
    rks = pl.sb("rks", [128, TFULL], BF16, side="right")
    pl.dma("sync", "ropek", rkc[:], io["rkc"])
    pl.dma("sync", "ropek", rks[:], io["rks"])
    pl.wait_slot("vector", "ropek")
    pl.wait_slot("tensor", "ropek")
    pl.waits("gpsimd", pl.tok("tensor", "vector", "scalar") + pl.all_slots_tok())
    l1_qkv_kv(pl, io["xn1_fm"], TFULL, io["wqkvp"], rkc, rks, io["k1"], io["v1t"],
              pb, pl.sbufs["c_swap_bf"])
    pl.free("rks", "rkc")
    pl.waits("sync", pl.tok("tensor", "vector", "scalar", "gpsimd") + pl.all_slots_tok())
    ao = l1_attn(pl, io["masks"], io["q1"], io["k1"], io["v1t"], TFULL, pb)
    for nme in ("l1rnr", "l1rnb", "l1neg", "l1qmax", "l1stat",
                "l1ub", "l1mA", "l1qB", "l1qA", "l1P", "l1V", "l1KB", "l1KA"):
        pl.free(nme)
    if "aodbg" in io:
        pl.after("sync", "vector")
        for ic in range(NCHUNK):
            pl.dma("sync", "aodbg", io["aodbg"][128 * ic:128 * (ic + 1), :],
                   ao[:, ic, :])
        pl.wait_slot("sync", "aodbg")
    l1_wo(pl, ao, io["woT"], io["x2p"], io["out"], pb, TOWN)


def declare_l1_io(nc, out_kind="ExternalOutput"):
    io = {}
    io["x1_fm"] = nc.dram_tensor("x1_fm", [D, TFULL], F32, kind="ExternalInput").ap()
    io["masks"] = nc.dram_tensor("masks", [4, 8, 128, 512], BF16, kind="ExternalInput").ap()
    io["wqkvp"] = nc.dram_tensor("wqkvp", [4096, 6144], F32, kind="ExternalInput").ap()
    io["woT"] = nc.dram_tensor("woT", [4096, 4096], F32, kind="ExternalInput").ap()
    io["A2"] = nc.dram_tensor("A2", [4096, 1024], F32, kind="ExternalInput").ap()
    io["B2"] = nc.dram_tensor("B2", [1024, 4096], F32, kind="ExternalInput").ap()
    io["WrT"] = nc.dram_tensor("WrT", [4096, 64], F32, kind="ExternalInput").ap()
    io["rqc"] = nc.dram_tensor("rqc", [128, 2048], BF16, kind="ExternalInput").ap()
    io["rqs"] = nc.dram_tensor("rqs", [128, 2048], BF16, kind="ExternalInput").ap()
    io["rkc"] = nc.dram_tensor("rkc", [128, TFULL], BF16, kind="ExternalInput").ap()
    io["rks"] = nc.dram_tensor("rks", [128, TFULL], BF16, kind="ExternalInput").ap()
    k_ = "ExternalOutput" if DEBUG_L1 else "Internal"
    io["xn1_fm"] = nc.dram_tensor("xn1_fm", [D, TFULL], BF16, kind=k_).ap()
    io["q1"] = nc.dram_tensor("q1", [4096, TOWN], BF16, kind=k_).ap()
    io["k1"] = nc.dram_tensor("k1", [1024, TFULL], BF16, kind=k_).ap()
    io["v1t"] = nc.dram_tensor("v1t", [TFULL, 1024], BF16, kind=k_).ap()
    io["x2p"] = nc.dram_tensor("x2p", [D, TOWN], F32, kind=k_).ap()
    return io


def build_l1_program():
    nc = bass.Bass("TRN2", target_bir_lowering=False)
    stack = ExitStack()
    io = declare_l1_io(nc)
    io["out"] = nc.dram_tensor("out", [D, TOWN], F32, kind="ExternalOutput").ap()
    if DEBUG_L1:
        io["aodbg"] = nc.dram_tensor("aodbg", [4096, TOWN], BF16, kind="ExternalOutput").ap()
    with stack:
        pl = Planner(nc, stack)
        io["E_exp"] = declare_and_load_consts(pl, need_E=True, need_swap=True)["E_exp"]
        io["pb"] = [pl.ps(f"pb{i}", [128, 512], F32) for i in range(8)]
        for e in ("tensor", "vector", "scalar"):
            pl.wait_slot(e, "consts")
        emit_l1(pl, io)
        pl.fence_all()
        pl.after("sync", "scalar", "gpsimd", "vector", "tensor")
        counts = pl.finalize()
    return nc, counts


def prep_l1_shared(w_qkv, w_o, A, B, Wr, consts):
    perm = []
    for hq in range(32):
        base = 128 * hq
        perm.extend(range(base, base + 128, 2))
        perm.extend(range(base + 1, base + 128, 2))
    for kh in range(8):
        base = 4096 + 128 * kh
        perm.extend(range(base, base + 128, 2))
        perm.extend(range(base + 1, base + 128, 2))
    perm.extend(range(5120, 6144))
    perm = np.array(perm)
    shared = {
        "wqkvp": np.ascontiguousarray(np.asarray(w_qkv)[:, perm]),
        "woT": np.ascontiguousarray(np.asarray(w_o)),
        "A2": np.ascontiguousarray(np.asarray(A).transpose(2, 0, 1).reshape(4096, 1024)),
        "B2": np.ascontiguousarray(np.asarray(B).transpose(0, 2, 1).reshape(1024, 4096)),
        "WrT": np.ascontiguousarray(np.asarray(Wr).T),
    }
    shared.update(host_const_inputs(consts, need_E=True, need_swap=True))
    return shared


def prep_l1_core_inputs(h, x1_full_reordered, shared):
    m = dict(shared)
    m["x1_fm"] = np.ascontiguousarray(x1_full_reordered.T)
    m["masks"] = host_masks(h)
    qc, qs, kc, ks = host_rope(h)
    m["rqc"], m["rqs"], m["rkc"], m["rks"] = qc, qs, kc, ks
    return m


def host_masks_l1(h):
    qcols = host_col_order(h)[:2048]
    s_abs = []
    for t0 in [512 * (2 * i + 1) for i in range(4)]:
        s_abs.extend(range(t0, t0 + 512))
    for t0 in [1024 * i for i in range(4)]:
        s_abs.extend(range(t0, t0 + 512))
    s_abs = np.array(s_abs)
    M = np.zeros((4, 8, 128, 512), dtype=np.float32)
    for i in range(4):
        aq = qcols[512 * i:512 * (i + 1)]
        blocks, win = tile_blocks(i)
        for j, blk in enumerate(win):
            as_ = s_abs[128 * blk:128 * (blk + 1)]
            M[i, j] = (as_[:, None] <= aq[None, :])
    return M.astype(ml_dtypes.bfloat16)


def host_rope_k_rank():
    hd = 128
    inv = (1.0 / (10000.0 ** (np.arange(0, hd, 2, dtype=np.float32) / hd))).astype(np.float32)
    s_abs = []
    for t0 in [512 * (2 * i + 1) for i in range(4)]:
        s_abs.extend(range(t0, t0 + 512))
    for t0 in [1024 * i for i in range(4)]:
        s_abs.extend(range(t0, t0 + 512))
    cols = np.array(s_abs, dtype=np.float32)
    ang = cols[None, :] * inv[:, None]
    cos = np.cos(ang.astype(np.float64)).astype(np.float32)
    sin = np.sin(ang.astype(np.float64)).astype(np.float32)
    return (np.concatenate([cos, cos], 0).astype(ml_dtypes.bfloat16),
            np.concatenate([-sin, sin], 0).astype(ml_dtypes.bfloat16))


def build_fused_program():
    nc = bass.Bass("TRN2", target_bir_lowering=False)
    stack = ExitStack()
    io = declare_l0_io(nc)
    io["x1"] = nc.dram_tensor("x1", [D, TOWN], F32, kind="Internal").ap()
    io1 = {}
    io1["masksL1"] = nc.dram_tensor("masksL1", [4, 8, 128, 512], BF16, kind="ExternalInput").ap()
    io1["wqkvp"] = nc.dram_tensor("wqkvp", [4096, 6144], F32, kind="ExternalInput").ap()
    io1["woT"] = nc.dram_tensor("woT", [4096, 4096], F32, kind="ExternalInput").ap()
    io1["A2"] = nc.dram_tensor("A2", [4096, 1024], F32, kind="ExternalInput").ap()
    io1["B2"] = nc.dram_tensor("B2", [1024, 4096], F32, kind="ExternalInput").ap()
    io1["WrT"] = nc.dram_tensor("WrT", [4096, 64], F32, kind="ExternalInput").ap()
    io1["rqc"] = nc.dram_tensor("rqc", [128, 2048], BF16, kind="ExternalInput").ap()
    io1["rqs"] = nc.dram_tensor("rqs", [128, 2048], BF16, kind="ExternalInput").ap()
    io1["rkc"] = nc.dram_tensor("rkc", [128, TFULL], BF16, kind="ExternalInput").ap()
    io1["rks"] = nc.dram_tensor("rks", [128, TFULL], BF16, kind="ExternalInput").ap()
    io1["xn1_own"] = nc.dram_tensor("xn1_own", [D, TOWN], BF16, kind="Internal").ap()
    io1["xn1_full"] = nc.dram_tensor("xn1_full", [2 * D, TOWN], BF16, kind="Internal").ap()
    io1["q1"] = nc.dram_tensor("q1", [4096, TOWN], BF16, kind="Internal").ap()
    io1["k1"] = nc.dram_tensor("k1", [1024, TFULL], BF16, kind="Internal").ap()
    io1["v1t"] = nc.dram_tensor("v1t", [TFULL, 1024], BF16, kind="Internal").ap()
    io1["x2p"] = nc.dram_tensor("x2p", [D, TOWN], F32, kind="Internal").ap()
    io1["out"] = nc.dram_tensor("out", [D, TOWN], F32, kind="ExternalOutput").ap()

    with stack:
        pl = Planner(nc, stack)
        cc_aps = declare_and_load_consts(pl, need_E=True, need_swap=True)
        io1["E_exp"] = cc_aps["E_exp"]
        pb = [pl.ps(f"pb{i}", [128, 512], F32) for i in range(8)]
        io["pb"] = pb
        io1["pb"] = pb
        for e in ("tensor", "vector", "scalar"):
            pl.wait_slot(e, "consts")
        emit_l0(pl, io)
        for nme in ("l0oB", "l0oA", "l0x1B", "l0x1A", "l0mB", "l0mA",
                    "l0qB", "l0qA", "l0P", "l0V", "l0K"):
            pl.free(nme)
        pl.pool()
        ln_fm_phase(pl, TOWN, io["x1"], io1["xn1_own"], "A1", pb)
        pl.free_pool()
        pl.fence("scalar")
        pl.waits("gpsimd", pl.tok("scalar"))
        pl.g.collective_compute(
            "AllGather", ALU.bypass,
            replica_groups=[[0, 1], [2, 3], [4, 5], [6, 7]],
            ins=[io1["xn1_own"].opt()], outs=[io1["xn1_full"].opt()])
        cc_count = pl.cc_sem.count
        pl.waits("sync", pl.tok("vector", "gpsimd", "scalar", "tensor") + pl.all_slots_tok())
        xn_res = load_resident_fm(pl, "xn1_res", io1["xn1_own"], [(0, TOWN)])
        pl.wait_slot("tensor", "xn1_res")
        rqc = pl.sb("rqc", [128, 2048], BF16, side="right")
        rqs = pl.sb("rqs", [128, 2048], BF16, side="right")
        pl.dma("sync", "ropeq", rqc[:], io1["rqc"])
        pl.dma("sync", "ropeq", rqs[:], io1["rqs"])
        pl.wait_slot("vector", "ropeq")
        pl.wait_slot("tensor", "ropeq")
        l1_qkv_q(pl, xn_res, io1["wqkvp"], rqc, rqs, io1["q1"], pb, TOWN,
                 pl.sbufs["c_swap_bf"])
        pl.free("rqs", "rqc")
        l1_moe(pl, xn_res, io1["WrT"], io1["A2"], io1["B2"], io1["E_exp"],
               io["x1"], io1["x2p"], pb, TOWN)
        pl.free("xn1_res")
        pl.free("wgtB", "wgtA")
        rkc = pl.sb("rkc", [128, TFULL], BF16, side="right")
        rks = pl.sb("rks", [128, TFULL], BF16, side="right")
        pl.dma("sync", "ropek", rkc[:], io1["rkc"])
        pl.dma("sync", "ropek", rks[:], io1["rks"])
        pl.wait_slot("vector", "ropek")
        pl.wait_slot("tensor", "ropek")
        pl.waits("gpsimd", pl.tok("tensor", "vector", "scalar") + pl.all_slots_tok())
        pl.eng["sync"].wait_ge_(pl.cc_sem, cc_count)
        pl.eng["gpsimd"].wait_ge_(pl.cc_sem, cc_count)
        l1_qkv_kv(pl, io1["xn1_full"], TFULL, io1["wqkvp"], rkc, rks,
                  io1["k1"], io1["v1t"], pb, pl.sbufs["c_swap_bf"],
                  rank_order=True)
        pl.free("rks", "rkc")
        pl.waits("sync", pl.tok("tensor", "vector", "scalar", "gpsimd") + pl.all_slots_tok())
        ao = l1_attn(pl, io1["masksL1"], io1["q1"], io1["k1"], io1["v1t"], TFULL, pb)
        for nme in ("l1rnr", "l1rnb", "l1neg", "l1qmax", "l1stat",
                    "l1ub", "l1mA", "l1qB", "l1qA", "l1P", "l1V", "l1KB", "l1KA"):
            pl.free(nme)
        l1_wo(pl, ao, io1["woT"], io1["x2p"], io1["out"], pb, TOWN)
        pl.fence_all()
        pl.after("sync", "scalar", "gpsimd", "vector", "tensor")
        counts = pl.finalize()
    return nc, counts


def prep_fused_inputs(inputs_dict, consts):
    x = np.asarray(inputs_dict["L0_attn_x"], dtype=np.float32)
    maps0 = prep_l0_inputs(x, inputs_dict["L0_attn_bv_q"], inputs_dict["L0_attn_bv_k"],
                           inputs_dict["L0_attn_bv_v"], inputs_dict["L0_ffn_w12"],
                           inputs_dict["L0_ffn_b12"], inputs_dict["L0_ffn_w3"],
                           inputs_dict["L0_ffn_b3"], consts)
    shared1 = prep_l1_shared(inputs_dict["L1_attn_w_qkv"], inputs_dict["L1_attn_w_o"],
                             inputs_dict["L1_ffn_A"], inputs_dict["L1_ffn_B"],
                             inputs_dict["L1_ffn_Wr"], consts)
    rkc, rks = host_rope_k_rank()
    per_core = []
    for c in range(8):
        h = c % 2
        m = dict(maps0[c])
        for k in ("wqkvp", "woT", "A2", "B2", "WrT", "E_exp", "swap_bf"):
            m[k] = shared1[k]
        qc, qs, _, _ = host_rope(h)
        m["rqc"], m["rqs"] = qc, qs
        m["rkc"], m["rks"] = rkc, rks
        m["masksL1"] = host_masks_l1(h)
        per_core.append(m)
    return per_core


from concourse.bass_utils import run_bass_kernel_spmd

_CACHE = {}


def _program():
    if "fused" not in _CACHE:
        _CACHE["fused"] = build_fused_program()
        _CACHE["consts"] = host_constants()
    return _CACHE


def kernel(L0_attn_x, L0_attn_bv_q, L0_attn_bv_k, L0_attn_bv_v,
           L0_ffn_w12, L0_ffn_b12, L0_ffn_w3, L0_ffn_b3,
           L1_attn_w_qkv, L1_attn_w_o, L1_ffn_A, L1_ffn_B, L1_ffn_Wr):
    P = _program()
    nc, _ = P["fused"]
    consts = P["consts"]
    inputs = dict(L0_attn_x=L0_attn_x, L0_attn_bv_q=L0_attn_bv_q,
                  L0_attn_bv_k=L0_attn_bv_k, L0_attn_bv_v=L0_attn_bv_v,
                  L0_ffn_w12=L0_ffn_w12, L0_ffn_b12=L0_ffn_b12,
                  L0_ffn_w3=L0_ffn_w3, L0_ffn_b3=L0_ffn_b3,
                  L1_attn_w_qkv=L1_attn_w_qkv, L1_attn_w_o=L1_attn_w_o,
                  L1_ffn_A=L1_ffn_A, L1_ffn_B=L1_ffn_B, L1_ffn_Wr=L1_ffn_Wr)
    in_maps = prep_fused_inputs(inputs, consts)
    res = run_bass_kernel_spmd(nc, in_maps, core_ids=list(range(8)))
    out = np.empty((4, TFULL, D), dtype=np.float32)
    for b in range(4):
        place_own_cols(out[b], res.results[2 * b]["out"].T, 0)
        place_own_cols(out[b], res.results[2 * b + 1]["out"].T, 1)
    return out


# version 17
# speedup vs baseline: 1.0378x; 1.0378x over previous
import sys
sys.path.insert(0, "/opt/trn_rl_repo")
from contextlib import ExitStack

import numpy as np
import ml_dtypes
import concourse.bass as bass
import concourse.mybir as mybir

F32 = mybir.dt.float32
BF16 = mybir.dt.bfloat16
AF = mybir.ActivationFunctionType
ALU = mybir.AluOpType
AX = mybir.AxisListType

D = 4096
NCHUNK = 32
TFULL = 4096
TOWN = 2048
EPS = 1e-5

ENGS = ("tensor", "vector", "scalar", "gpsimd", "sync")
DEBUG_X1P = False
DEBUG_L1 = False


class Sem:
    __slots__ = ("name", "count", "h")

    def __init__(self, name):
        self.name = name
        self.count = 0
        self.h = None


class Deferred:
    __slots__ = ("meth", "args", "kwargs", "incs")

    def __init__(self, meth, args, kwargs):
        self.meth = meth
        self.args = args
        self.kwargs = kwargs
        self.incs = []

    def then_inc(self, sem, n=1):
        sem.count += n
        self.incs.append((sem, n))
        return self

    def run(self, eng):
        args = [a.h if isinstance(a, Sem) else a for a in self.args]
        kwargs = {k: (v.h if isinstance(v, Sem) else v) for k, v in self.kwargs.items()}
        instr = getattr(eng, self.meth)(*args, **kwargs)
        for sem, n in self.incs:
            instr.then_inc(sem.h, n)
        return instr


class EngPlan:
    def __init__(self, pl, name):
        self._pl = pl
        self._name = name

    def _rec(self, meth, args, kwargs, auto=True):
        pl, name = self._pl, self._name
        if (name in ("vector", "scalar", "gpsimd")
                and meth not in ("wait_ge", "nop")):
            cnt = pl.prog[name].count
            if cnt > 0 and pl._selfwait.get(name, 0) < cnt:
                w = Deferred("wait_ge", (pl.prog[name], cnt), {})
                pl.ops[name].append(w)
                pl._selfwait[name] = cnt
        d = Deferred(meth, args, kwargs)
        pl.ops[name].append(d)
        if meth in ("dma_start", "dma_start_transpose"):
            d.then_inc(pl.dmas[name], 16)
        elif meth == "collective_compute":
            d.then_inc(pl.cc_sem, 1)
        elif auto and meth not in ("wait_ge", "nop"):
            d.then_inc(pl.prog[name], 1)
        return d

    def __getattr__(self, meth):
        def rec(*a, auto=True, **k):
            return self._rec(meth, a, k, auto=auto)

        return rec

    def wait_ge_(self, sem, val):
        d = Deferred("wait_ge", (sem, int(val)), {})
        self._pl.ops[self._name].append(d)
        return d


class Planner:
    def __init__(self, nc, stack):
        self.nc = nc
        self.stack = stack
        self.ops = {e: [] for e in ENGS}
        self.prog = {e: self._mksem(f"prog_{e}") for e in ENGS}
        self.dmas = {e: self._mksem(f"dmas_{e}") for e in ENGS}
        self.qf = {e: self._mksem(f"qf_{e}") for e in ENGS}
        self._fenced = {e: ({}, 0) for e in ENGS}
        self.cc_sem = self._mksem("cc_done")
        self.t = EngPlan(self, "tensor")
        self.v = EngPlan(self, "vector")
        self.s = EngPlan(self, "scalar")
        self.g = EngPlan(self, "gpsimd")
        self.q = EngPlan(self, "sync")
        self.eng = {"tensor": self.t, "vector": self.v, "scalar": self.s,
                    "gpsimd": self.g, "sync": self.q}
        self.sbufs = {}
        self._ctxs = {}
        self._semcache = {}
        self._qslots = {}
        self._selfwait = {}

    def _mksem(self, name):
        s = Sem(name)
        s.h = self.stack.enter_context(self.nc.semaphore(name))
        return s

    def sem(self, name):
        if name not in self._semcache:
            self._semcache[name] = self._mksem(name)
        return self._semcache[name]

    def sb(self, name, shape, dtype, side=None):
        if name not in self.sbufs:
            kw = {"allow_name_mangling": True}
            if side is not None:
                kw["side"] = side
            cm = self.nc.sbuf_tensor(name, shape, dtype, **kw)
            self.sbufs[name] = cm.__enter__()
            self._ctxs[name] = cm
        return self.sbufs[name]

    def ps(self, name, shape, dtype):
        if name not in self.sbufs:
            cm = self.nc.psum_tensor(name, shape, dtype)
            self.sbufs[name] = cm.__enter__()
            self._ctxs[name] = cm
        return self.sbufs[name]

    def free(self, *names):
        for name in names:
            cm = self._ctxs.pop(name)
            cm.__exit__(None, None, None)
            del self.sbufs[name]

    POOL_BIG = ("sA", "sB", "sC", "sD", "rspl", "nspl", "bA", "bB", "bC", "bD")

    def pool(self, rows=True):
        p = {}
        names = ("sA", "sB", "sC", "sD", "rspl", "nspl") if rows else \
                ("sA", "sB", "sC", "sD")
        self.POOL_BIG = list(names) + ["bA", "bB", "bC", "bD"]
        for n in names:
            p[n] = self.sb(n, [128, 4096], F32, side="right")
        for n in ("bA", "bB", "bC", "bD"):
            p[n] = self.sb(n, [128, 4096], BF16, side="right")
        for n in ("cA", "cB", "cC", "cD"):
            if n not in self.sbufs:
                p[n] = self.sb(n, [128, 128], F32)
                self.g.memset(p[n][:], 0.0)
        return p

    def free_pool(self):
        for n in reversed(self.POOL_BIG):
            self.free(n)

    def tok(self, *engs):
        out = []
        for e in engs:
            if self.prog[e].count > 0:
                out.append((self.prog[e], self.prog[e].count))
            if self.qf[e].count > 0:
                out.append((self.qf[e], self.qf[e].count))
        return out

    def all_slots_tok(self):
        out = []
        for name, sem in self._semcache.items():
            if name.startswith("slot_") and sem.count > 0:
                out.append((sem, sem.count))
        return out

    def stok(self, *slotnames):
        out = []
        for n in slotnames:
            sl = self.slot(n)
            if sl.count > 0:
                out.append((sl, sl.count))
        return out

    def waits(self, consumer, *toks):
        c = self.eng[consumer]
        for tk in toks:
            if tk is None:
                continue
            for sem, val in tk:
                if val > 0:
                    c.wait_ge_(sem, val)

    def after(self, consumer, *producers):
        self.waits(consumer, self.tok(*producers))

    def slot(self, name):
        return self.sem("slot_" + name)

    def dma(self, queue, slotname, out, in_, **kw):
        sl = self.slot(slotname)
        d = self.eng[queue]._rec("dma_start", (), dict(out=out, in_=in_, **kw),
                                auto=False)
        d.incs.clear()
        self.dmas[queue].count -= 16
        d.then_inc(sl, 16)
        self._qslots.setdefault(queue, {})[slotname] = True
        return d

    def wait_slot(self, consumer, slotname, val=None):
        sl = self.slot(slotname)
        v = sl.count if val is None else val
        if v > 0:
            self.eng[consumer].wait_ge_(sl, v)

    def fence(self, queue):
        dm = self.dmas[queue]
        prev_slots, prev_dm = self._fenced[queue]
        last = None
        if dm.count > prev_dm:
            last = self.eng[queue].wait_ge_(dm, dm.count)
        new_slots = dict(prev_slots)
        for slotname in self._qslots.get(queue, {}):
            sl = self.slot(slotname)
            if sl.count > prev_slots.get(slotname, 0):
                last = self.eng[queue].wait_ge_(sl, sl.count)
                new_slots[slotname] = sl.count
        if last is not None:
            last.then_inc(self.qf[queue], 1)
            self._fenced[queue] = (new_slots, dm.count)

    def fence_all(self, queues=("sync", "scalar", "gpsimd", "vector")):
        for qn in queues:
            self.fence(qn)

    def tmark(self):
        self.ops["tensor"][-1].then_inc(self.prog["tensor"], 1)

    def finalize(self):
        nc = self.nc
        with nc.Block() as block:
            for name, deco in (("tensor", block.tensor), ("vector", block.vector),
                               ("scalar", block.scalar), ("gpsimd", block.gpsimd),
                               ("sync", block.sync)):
                ops = self.ops[name]
                if not ops:
                    continue

                def emit(eng, ops=ops):
                    for d in ops:
                        d.run(eng)

                deco(emit)
        return {e: len(self.ops[e]) for e in ENGS}


def host_constants():
    c = {}
    c["ones_bf_col"] = np.ones((128, 1), dtype=np.float32).astype(ml_dtypes.bfloat16)
    c["ones_f32_row"] = np.ones((1, 128), dtype=np.float32)
    c["ident_bf"] = np.eye(128, dtype=np.float32).astype(ml_dtypes.bfloat16)
    c["ident_f32"] = np.eye(128, dtype=np.float32)
    sw = np.zeros((128, 128), dtype=np.float32)
    for i_ in range(128):
        sw[i_, (i_ + 64) % 128] = 1.0
    c["swap_bf"] = sw.astype(ml_dtypes.bfloat16)
    E = np.zeros((64, 1024), dtype=np.float32)
    for k in range(64):
        E[k, k * 16:(k + 1) * 16] = 1.0
    c["E_exp"] = E
    return c


CONST_SPECS = [
    ("ones_bf_col", [128, 1], BF16),
    ("ones_f32_row", [1, 128], F32),
    ("ident_bf", [128, 128], BF16),
    ("ident_f32", [128, 128], F32),
    ("E_exp", [64, 1024], F32),
    ("swap_bf", [128, 128], BF16),
]


def declare_and_load_consts(pl, need_E=False, need_swap=False):
    nc = pl.nc
    aps = {}
    for name, shape, dt in CONST_SPECS:
        if name == "E_exp" and not need_E:
            continue
        if name == "swap_bf" and not need_swap:
            continue
        aps[name] = nc.dram_tensor(name, shape, dt, kind="ExternalInput").ap()
    pl.sb("c_ones_bf_col", [128, 1], BF16)
    pl.sb("c_ones_f32_row", [1, 128], F32)
    pl.sb("c_ident_bf", [128, 128], BF16)
    pl.sb("c_ident_f32", [128, 128], F32)
    pl.dma("sync", "consts", pl.sbufs["c_ones_bf_col"][:], aps["ones_bf_col"])
    pl.dma("sync", "consts", pl.sbufs["c_ones_f32_row"][:], aps["ones_f32_row"])
    pl.dma("sync", "consts", pl.sbufs["c_ident_bf"][:], aps["ident_bf"])
    pl.dma("sync", "consts", pl.sbufs["c_ident_f32"][:], aps["ident_f32"])
    if need_swap:
        pl.sb("c_swap_bf", [128, 128], BF16)
        pl.dma("sync", "consts", pl.sbufs["c_swap_bf"][:], aps["swap_bf"])
    eps_col = pl.sb("c_eps_col", [128, 1], F32)
    pl.g.memset(eps_col[:], EPS)
    onesf32c = pl.sb("c_ones_f32_col", [128, 1], F32)
    pl.g.memset(onesf32c[:], 1.0)
    onesbfr = pl.sb("c_ones_bf_row", [1, 128], BF16)
    pl.g.memset(onesbfr[:], 1.0)
    return aps


def host_const_inputs(consts, need_E=False, need_swap=False):
    d = {}
    for name, _, _ in CONST_SPECS:
        if name == "E_exp" and not need_E:
            continue
        if name == "swap_bf" and not need_swap:
            continue
        d[name] = consts[name]
    return d


def host_col_order(h):
    own = [512 * (2 * i + 1) if h == 0 else 1024 * i for i in range(4)]
    sib = [1024 * i if h == 0 else 512 * (2 * i + 1) for i in range(4)]
    cols = []
    for t0 in own + sib:
        cols.extend(range(t0, t0 + 512))
    return np.array(cols)


def tile_blocks(i):
    own = list(range(0, 4 * i + 4))
    sib = list(range(16, 16 + 4 * i + 4))
    return own + sib, own[-4:] + sib[-4:]


def host_masks(h):
    cols = host_col_order(h)
    M = np.zeros((4, 8, 128, 512), dtype=np.float32)
    for i in range(4):
        aq = cols[512 * i:512 * (i + 1)]
        blocks, win = tile_blocks(i)
        for j, blk in enumerate(win):
            as_ = cols[128 * blk:128 * (blk + 1)]
            M[i, j] = (as_[:, None] <= aq[None, :])
    return M.astype(ml_dtypes.bfloat16)


def host_rope(h):
    hd = 128
    inv = (1.0 / (10000.0 ** (np.arange(0, hd, 2, dtype=np.float32) / hd))).astype(np.float32)
    cols = host_col_order(h).astype(np.float32)
    ang = cols[None, :] * inv[:, None]
    cos = np.cos(ang.astype(np.float64)).astype(np.float32)
    sin = np.sin(ang.astype(np.float64)).astype(np.float32)
    cosd = np.concatenate([cos, cos], axis=0)
    sinds = np.concatenate([-sin, sin], axis=0)
    return (cosd[:, 0:2048].astype(ml_dtypes.bfloat16),
            sinds[:, 0:2048].astype(ml_dtypes.bfloat16),
            cosd.astype(ml_dtypes.bfloat16),
            sinds.astype(ml_dtypes.bfloat16))



def ln_tm_alloc(pl):
    st = {}
    st["sA"] = pl.sb("sA", [128, 4096], F32, side="right")
    st["bA"] = pl.sb("bA", [128, 4096], BF16, side="right")
    st["snk"] = pl.sb("snk", [128, 2048], BF16, side="right")
    st["bD"] = pl.sb("bD", [128, 2048], BF16, side="right")
    for n in ("cA", "cB", "cC", "cD"):
        if n not in pl.sbufs:
            pl.sb(n, [128, 128], F32)
            pl.g.memset(pl.sbufs[n][:], 0.0)
    pl.after("scalar", "gpsimd")
    pl.after("vector", "gpsimd")
    st["cols"] = (pl.sbufs["cA"], pl.sbufs["cB"], pl.sbufs["cC"], pl.sbufs["cD"])
    st["use"] = [None, None]
    st["out"] = [None, None]
    st["ntok"] = [None, None]
    st["etok"] = [None, None]
    st["ftok"] = [None, None]
    st["tail"] = False
    return st


def ln_tm_own_extra(pl, st):
    st["sB"] = pl.sb("sB", [128, 4096], F32, side="right")
    st["bB"] = pl.sb("bB", [128, 4096], BF16, side="right")


def ln_tm_part1(pl, st, c, x_tm, xn_tm):
    t, v, s, g = pl.t, pl.v, pl.s, pl.g
    P = pl.sbufs
    col_sum, col_sq, col_r, col_nmr = st["cols"]
    qn = "sync" if c % 2 == 0 else "scalar"
    if st["tail"]:
        ui, oi = 0, 0
    else:
        ui, oi = c % 2, c % 2
    ldb = st[("sA", "sB")[ui]]
    slot = ("sA", "sB")[ui]
    tmb = st[("bA", "bB")[oi]]
    pl.waits(qn, st["use"][ui])
    pl.dma(qn, slot, ldb[:], x_tm[128 * c:128 * (c + 1), :])
    pl.wait_slot("vector", slot)
    v.reduce_sum(col_sum[:, c:c + 1], ldb[:], axis=AX.X)
    pl.wait_slot("scalar", slot)
    s.activation(out=st["snk"][:], in_=ldb[:, 0:2048], func=AF.Square,
                 accum_out=col_sq[:, c:c + 1])
    s.activation(out=st["snk"][:], in_=ldb[:, 2048:4096], func=AF.Square,
                 accum_out=col_r[:, c:c + 1])
    pl.after("vector", "scalar")
    v.tensor_tensor(col_sq[:, c:c + 1], col_sq[:, c:c + 1], col_r[:, c:c + 1], op=ALU.add)
    v.tensor_scalar_mul(col_sum[:, c:c + 1], col_sum[:, c:c + 1], 1.0 / D)
    v.tensor_scalar_mul(col_sq[:, c:c + 1], col_sq[:, c:c + 1], 1.0 / D)
    v.tensor_tensor(col_r[:, c:c + 1], col_sum[:, c:c + 1], col_sum[:, c:c + 1], op=ALU.mult)
    v.tensor_tensor(col_r[:, c:c + 1], col_sq[:, c:c + 1], col_r[:, c:c + 1], op=ALU.subtract)
    pl.after("scalar", "vector")
    s.activation(out=col_r[:, c:c + 1], in_=col_r[:, c:c + 1], func=AF.Ln,
                 bias=P["c_eps_col"][:])
    s.activation(out=col_r[:, c:c + 1], in_=col_r[:, c:c + 1], func=AF.Exp, scale=-0.5)
    pl.after("vector", "scalar")
    v.tensor_tensor(col_nmr[:, c:c + 1], col_sum[:, c:c + 1], col_r[:, c:c + 1], op=ALU.mult)
    v.tensor_scalar_mul(col_nmr[:, c:c + 1], col_nmr[:, c:c + 1], -1.0)
    pl.after("gpsimd", "vector", "scalar")
    pl.waits("gpsimd", st["out"][oi])
    g.tensor_scalar(out=tmb[:], in0=ldb[:], scalar1=col_r[:, c:c + 1],
                    scalar2=col_nmr[:, c:c + 1], op0=ALU.mult, op1=ALU.add)
    st["use"][ui] = pl.tok("gpsimd", "scalar")
    if st["tail"]:
        tq, tslot = "sync", "xntmT"
    else:
        tq, tslot = "gpsimd", "xntm" + str(oi)
    pl.after(tq, "gpsimd")
    pl.dma(tq, tslot, xn_tm[128 * c:128 * (c + 1), :], tmb[:])
    st["tslot"] = tslot
    st["ntok"][oi] = pl.tok("gpsimd")


def ln_tm_part2(pl, st, c, xn_fm):
    t, v, s, g = pl.t, pl.v, pl.s, pl.g
    P = pl.sbufs
    idb = P["c_ident_bf"]
    pb = st["pb"]
    qn = "sync" if c % 2 == 0 else "scalar"
    if st["tail"]:
        oi, ei = 0, 0
        bs = 4
    else:
        oi = c % 2
        ei = oi
        bs = 4 * oi
    tmb = st[("bA", "bB")[oi]]
    pl.waits("tensor", st["ntok"][oi], st["etok"][ei])
    for dc in range(32):
        bank = pb[bs + dc // 8]
        reg = bank[:, 64 * (dc % 8):64 * (dc % 8 + 1)].bitcast(BF16)
        t.transpose(reg, tmb[:, 128 * dc:128 * (dc + 1)], idb[:], auto=False)
    pl.tmark()
    ttok = pl.tok("tensor")
    st["out"][oi] = pl.stok(st["tslot"]) + ttok
    pl.waits("scalar", ttok)
    pl.waits("vector", ttok)
    if st["tail"]:
        dstf = xn_fm[:, 128 * c:128 * (c + 1)].rearrange("(dc p) tt -> p dc tt", p=128)
        for half in range(2):
            pl.waits("scalar", st["ftok"][1 - half])
            pl.waits("vector", st["ftok"][1 - half])
            for b in range(2):
                dst = st["bD"][:, 1024 * b:1024 * (b + 1)]
                srcb = pb[bs + 2 * half + b][:, :].bitcast(BF16)
                if b == 0:
                    s.copy(out=dst, in_=srcb)
                else:
                    v.tensor_copy(dst, srcb)
            st["etok"][0] = pl.tok("scalar", "vector")
            qn2 = "sync" if half == 0 else "scalar"
            pl.waits(qn2, st["etok"][0])
            pl.dma(qn2, "xfm" + str(half), dstf[:, 16 * half:16 * (half + 1), :],
                   st["bD"].rearrange("p (a b2) -> p a b2", b2=128))
            st["ftok"][half] = pl.stok("xfm" + str(half))
    else:
        for b in range(4):
            dst = st["xn_res"][:, 8 * b:8 * (b + 1), 128 * c:128 * (c + 1)]
            srcb = pb[bs + b][:, :].bitcast(BF16).rearrange("p (a b2) -> p a b2", b2=128)
            if b < 2:
                s.copy(out=dst, in_=srcb)
            else:
                v.tensor_copy(dst, srcb)
        st["etok"][ei] = pl.tok("scalar", "vector")


def ln_tm_own_stores(pl, st, xn_fm, nown):
    gtok = pl.tok("scalar", "vector")
    xr = st["xn_res"]
    for gg in range((nown + 7) // 8):
        w = min(1024, 128 * nown - 1024 * gg)
        dst = xn_fm[:, 1024 * gg:1024 * gg + w].rearrange("(dc p) tt -> p dc tt", p=128)
        pl.waits("sync", gtok)
        pl.dma("sync", "xfmA", dst[:, 0:16, :], xr[:, 0:16, 1024 * gg:1024 * gg + w])
        pl.waits("scalar", gtok)
        pl.dma("scalar", "xfmB", dst[:, 16:32, :], xr[:, 16:32, 1024 * gg:1024 * gg + w])
        st["ftok"][0] = pl.stok("xfmA")
        st["ftok"][1] = pl.stok("xfmB")


def ln_fm_phase(pl, T, x_fm, xn_fm, tag, pbanks):
    t, v, s, g, q = pl.t, pl.v, pl.s, pl.g, pl.q
    P = pl.sbufs
    ones_bf = P["c_ones_bf_col"]

    xb = [P["sA"], P["sB"]]
    slots = ["sA", "sB"]
    xbf = [P["bA"], P["bB"]]
    xsq = [P["bC"], P["bD"]]
    sum_row = P["rspl"][0:1, 0:T]
    sq_row = P["nspl"][0:1, 0:T]
    r_row = P["sC"][0:1, 0:T]
    nmr_row = P["sD"][0:1, 0:T]

    nhalf = (T + 2047) // 2048
    mm_tok = [None, None]
    for th in range(nhalf):
        W = min(2048, T - 2048 * th)
        ntt = W // 512
        for c in range(NCHUNK):
            it = th * NCHUNK + c
            qn = "sync" if it % 2 == 0 else "scalar"
            pl.waits(qn, mm_tok[it % 2])
            pl.dma(qn, slots[it % 2], xb[it % 2][:, 0:W],
                   x_fm[128 * c:128 * (c + 1), 2048 * th:2048 * th + W])
            pl.wait_slot("vector", slots[it % 2])
            v.tensor_copy(xbf[it % 2][:, 0:W], xb[it % 2][:, 0:W])
            pl.wait_slot("scalar", slots[it % 2])
            s.activation(out=xsq[it % 2][:, 0:W], in_=xb[it % 2][:, 0:W], func=AF.Square)
            pl.after("tensor", "vector", "scalar")
            for tt in range(ntt):
                t.matmul(pbanks[tt][0:1, :], ones_bf[:], xbf[it % 2][:, 512 * tt:512 * (tt + 1)],
                         start=(c == 0), stop=(c == NCHUNK - 1), auto=False)
                t.matmul(pbanks[4 + tt][0:1, :], ones_bf[:], xsq[it % 2][:, 512 * tt:512 * (tt + 1)],
                         start=(c == 0), stop=(c == NCHUNK - 1), auto=False)
            pl.tmark()
            mm_tok[it % 2] = pl.tok("tensor")
        base = 2048 * th
        pl.after("vector", "tensor")
        for tt in range(ntt):
            v.tensor_copy(sum_row[0:1, base + 512 * tt:base + 512 * (tt + 1)], pbanks[tt][0:1, :])
            v.tensor_copy(sq_row[0:1, base + 512 * tt:base + 512 * (tt + 1)], pbanks[4 + tt][0:1, :])

    v.tensor_scalar_mul(sum_row[:], sum_row[:], 1.0 / D)
    v.tensor_scalar_mul(sq_row[:], sq_row[:], 1.0 / D)
    v.tensor_tensor(r_row[:], sum_row[:], sum_row[:], op=ALU.mult)
    v.tensor_tensor(r_row[:], sq_row[:], r_row[:], op=ALU.subtract)
    pl.after("scalar", "vector")
    s.activation(out=r_row[:], in_=r_row[:], func=AF.Ln, bias=P["c_eps_col"][0:1, :])
    s.activation(out=r_row[:], in_=r_row[:], func=AF.Exp, scale=-0.5)
    pl.after("vector", "scalar")
    v.tensor_tensor(nmr_row[:], sum_row[:], r_row[:], op=ALU.mult)
    v.tensor_scalar_mul(nmr_row[:], nmr_row[:], -1.0)

    _fm_normalize(pl, T, x_fm, xn_fm, tag, pbanks, rows_sbuf=(r_row, nmr_row))


def _fm_normalize(pl, T, x_fm, xn_fm, tag, pbanks, rows_dram=None, rows_sbuf=None):
    t, v, s, g, q = pl.t, pl.v, pl.s, pl.g, pl.q
    P = pl.sbufs
    ones_f32 = P["c_ones_f32_row"]

    if rows_dram is not None:
        rr = P["sC"][0:1, 0:T]
        nn = P["sD"][0:1, 0:T]
        pl.wait_slot("sync", "rowsw")
        pl.dma("sync", "rowr", rr, rows_dram[0])
        pl.dma("sync", "rown", nn, rows_dram[1])
        pl.wait_slot("tensor", "rowr")
        pl.wait_slot("tensor", "rown")
    else:
        rr, nn = rows_sbuf
        pl.after("tensor", "vector", "scalar")

    r_spl = P["rspl"][:, 0:T]
    n_spl = P["nspl"][:, 0:T]
    xch = [P["sA"], P["sB"]]
    slots = [tag + "nA", tag + "nB"]
    tmp = [P["sC"], P["sD"]]
    xnb = [P["bA"], P["bB"]]

    for tt in range(T // 512):
        bankA = pbanks[2 * (tt % 2)]
        bankB = pbanks[2 * (tt % 2) + 1]
        if tt >= 2:
            pl.after("tensor", "scalar")
        t.matmul(bankA[:, :], ones_f32[:], rr[0:1, 512 * tt:512 * (tt + 1)],
                 start=True, stop=True, auto=False)
        t.matmul(bankB[:, :], ones_f32[:], nn[0:1, 512 * tt:512 * (tt + 1)],
                 start=True, stop=True, auto=False)
        pl.tmark()
        pl.after("scalar", "tensor")
        s.copy(out=r_spl[:, 512 * tt:512 * (tt + 1)], in_=bankA[:, :])
        s.copy(out=n_spl[:, 512 * tt:512 * (tt + 1)], in_=bankB[:, :])

    spl_tok = pl.tok("scalar")
    pre = pl.tok("vector", "scalar", "gpsimd", "tensor") + pl.all_slots_tok()
    cons_tok = [pre, pre]
    st_tok = [None, None]
    for c in range(NCHUNK):
        qn = "sync" if c % 2 == 0 else "scalar"
        pl.waits(qn, cons_tok[c % 2])
        pl.dma(qn, slots[c % 2], xch[c % 2][:, 0:T], x_fm[128 * c:128 * (c + 1), :])
        e = "vector" if c % 2 == 0 else "gpsimd"
        ee = pl.eng[e]
        pl.wait_slot(e, slots[c % 2])
        pl.waits(e, spl_tok, st_tok[c % 2])
        ee.tensor_tensor(tmp[c % 2][:, 0:T], xch[c % 2][:, 0:T], r_spl, op=ALU.mult)
        ee.tensor_tensor(xnb[c % 2][:, 0:T], tmp[c % 2][:, 0:T], n_spl, op=ALU.add)
        cons_tok[c % 2] = pl.tok(e)
        pl.after("gpsimd", e)
        pl.dma("gpsimd", tag + "xo" + str(c % 2),
               xn_fm[128 * c:128 * (c + 1), :], xnb[c % 2][:, 0:T])
        st_tok[c % 2] = pl.stok(tag + "xo" + str(c % 2))


def stream_w_oc(pl, w_dram, n_ic, col0, width, wgt_sb, slotname, war_tok=None,
                queue="gpsimd"):
    view = w_dram[:, col0:col0 + width].rearrange("(ic p) j -> p ic j", p=128)
    half = n_ic // 2 if n_ic >= 2 else n_ic
    pl.waits(queue, war_tok)
    pl.dma(queue, slotname, wgt_sb[:, 0:half, 0:width], view[:, 0:half, :])
    if half < n_ic:
        pl.dma(queue, slotname, wgt_sb[:, half:n_ic, 0:width], view[:, half:n_ic, :])


def load_resident_fm(pl, name, src_fm, ranges, nic=NCHUNK, queue="sync"):
    Ttot = sum(a1 - a0 for a0, a1 in ranges)
    res = pl.sb(name, [128, nic, Ttot], BF16, side="right")
    for ic in range(nic):
        o = 0
        for a0, a1 in ranges:
            pl.dma(queue, name, res[:, ic, o:o + (a1 - a0)],
                   src_fm[128 * ic:128 * (ic + 1), a0:a1])
            o += a1 - a0
    return res


def l0_ffn_c1(pl, xn_res, w12T, b1cols, b2cols, x4_dram, T, pbanks, hook=None):
    t, v, s, g = pl.t, pl.v, pl.s, pl.g
    wgtG = [pl.sb("wgtA", [128, NCHUNK, 128], BF16), pl.sb("wgtB", [128, NCHUNK, 128], BF16)]
    wgtU = [pl.sb("wgtC", [128, NCHUNK, 128], BF16), pl.sb("wgtD", [128, NCHUNK, 128], BF16)]
    silu_sb = pl.sb("silu_sb", [128, 512], F32)
    x4row = [pl.sb("x4rowA", [128, 2048], BF16), pl.sb("x4rowB", [128, 2048], BF16)]
    ntt = T // 512

    entry = pl.tok("tensor", "vector", "scalar", "gpsimd") + pl.all_slots_tok()
    wtok = [entry, entry]
    btok = [entry] * 4
    rtok = [entry, entry]
    stok = entry
    for a in range(32):
        stream_w_oc(pl, w12T, NCHUNK, a * 128, 128, wgtG[a % 2], "wgtG" + str(a % 2),
                    war_tok=wtok[a % 2])
        stream_w_oc(pl, w12T, NCHUNK, 4096 + a * 128, 128, wgtU[a % 2], "wgtU" + str(a % 2),
                    war_tok=wtok[a % 2])
        if hook is not None:
            hook(a)
        pl.wait_slot("tensor", "wgtG" + str(a % 2))
        pl.wait_slot("tensor", "wgtU" + str(a % 2))
        for tt in range(ntt):
            bg = pbanks[2 * (tt % 2)]
            bu = pbanks[2 * (tt % 2) + 1]
            pl.waits("tensor", btok[2 * (tt % 2)], btok[2 * (tt % 2) + 1])
            for ic in range(NCHUNK):
                t.matmul(bg[:, :], wgtG[a % 2][:, ic, :], xn_res[:, ic, 512 * tt:512 * (tt + 1)],
                         start=(ic == 0), stop=(ic == NCHUNK - 1), auto=False)
            pl.tmark()
            for ic in range(NCHUNK):
                t.matmul(bu[:, :], wgtU[a % 2][:, ic, :], xn_res[:, ic, 512 * tt:512 * (tt + 1)],
                         start=(ic == 0), stop=(ic == NCHUNK - 1), auto=False)
            pl.tmark()
            mm_tok = pl.tok("tensor")
            pl.waits("scalar", mm_tok, stok)
            s.activation(out=silu_sb[:, :], in_=bg[:, :], func=AF.Silu,
                         bias=b1cols[:, a:a + 1])
            pl.waits("vector", mm_tok)
            pl.after("vector", "scalar")
            if tt == 0:
                pl.waits("vector", rtok[a % 2])
            v.scalar_tensor_tensor(out=x4row[a % 2][:, 512 * tt:512 * (tt + 1)],
                                   in0=bu[:, :], scalar=b2cols[:, a:a + 1],
                                   in1=silu_sb[:, :], op0=ALU.add, op1=ALU.mult)
            btok[2 * (tt % 2)] = pl.tok("vector")
            btok[2 * (tt % 2) + 1] = pl.tok("vector")
            stok = pl.tok("vector")
        wtok[a % 2] = pl.tok("tensor")
        pl.after("scalar", "vector")
        pl.dma("scalar", "x4row" + str(a % 2),
               x4_dram[128 * a:128 * (a + 1), 0:T], x4row[a % 2][:, 0:T])
        rtok[a % 2] = pl.stok("x4row" + str(a % 2))


def l0_ffn_c2(pl, x4_res, w3T, b3cols, x_fm_own, x1p_dram, T, own_rngs, pbanks):
    t, v, s, g = pl.t, pl.v, pl.s, pl.g
    wgt = [pl.sbufs["wgtA"], pl.sbufs["wgtB"]]
    xbuf = [pl.sb("c2xA", [128, 512], F32, side="right"),
            pl.sb("c2xB", [128, 512], F32, side="right")]
    orow = [pl.sb("c2oA", [128, 2048], F32, side="right"),
            pl.sb("c2oB", [128, 2048], F32, side="right")]
    ntt = T // 512
    tcols = []
    for a0, a1 in own_rngs:
        for a in range(a0, a1, 512):
            tcols.append(a)

    wslots = ["wgtA", "wgtB"]
    entry = pl.tok("tensor", "vector", "scalar", "gpsimd") + pl.all_slots_tok()
    wtok = [entry, entry]
    btok = [entry] * 4
    xtok = [entry, entry]
    otok = [entry, entry]
    for nc_ in range(32):
        stream_w_oc(pl, w3T, NCHUNK, nc_ * 128, 128, wgt[nc_ % 2], wslots[nc_ % 2],
                    war_tok=wtok[nc_ % 2])
        pl.wait_slot("tensor", wslots[nc_ % 2])
        for tt in range(ntt):
            bank = pbanks[(nc_ * ntt + tt) % 4]
            pl.waits("sync", xtok[tt % 2])
            pl.dma("sync", "c2x" + str(tt % 2), xbuf[tt % 2][:, :],
                   x_fm_own[128 * nc_:128 * (nc_ + 1), tcols[tt]:tcols[tt] + 512])
            pl.waits("tensor", btok[(nc_ * ntt + tt) % 4])
            for ic in range(NCHUNK):
                t.matmul(bank[:, :], wgt[nc_ % 2][:, ic, :], x4_res[:, ic, 512 * tt:512 * (tt + 1)],
                         start=(ic == 0), stop=(ic == NCHUNK - 1), auto=False)
            pl.tmark()
            mm_tok = pl.tok("tensor")
            pl.waits("vector", mm_tok)
            pl.wait_slot("vector", "c2x" + str(tt % 2))
            if tt <= 1:
                pl.waits("vector", otok[nc_ % 2])
            v.scalar_tensor_tensor(out=orow[nc_ % 2][:, 512 * tt:512 * (tt + 1)],
                                   in0=bank[:, :], scalar=b3cols[:, nc_:nc_ + 1],
                                   in1=xbuf[tt % 2][:, :],
                                   op0=ALU.add, op1=ALU.add)
            btok[(nc_ * ntt + tt) % 4] = pl.tok("vector")
            xtok[tt % 2] = pl.tok("vector")
        wtok[nc_ % 2] = pl.tok("tensor")
        pl.after("scalar", "vector")
        pl.dma("scalar", "c2o" + str(nc_ % 2),
               x1p_dram[128 * nc_:128 * (nc_ + 1), 0:T], orow[nc_ % 2][:, 0:T])
        otok[nc_ % 2] = pl.stok("c2o" + str(nc_ % 2))


def l0_bind_prep(pl, bv_q, bv_k, bv_v, qk_dram, vb_dram):
    v, s, q = pl.v, pl.s, pl.q
    qk_cols = pl.sb("qk_cols", [128, 32], F32)
    vb_cols = pl.sb("vb_cols", [128, 32], F32)
    bq = pl.sb("bq", [4, 1024], F32)
    bk = pl.sb("bk", [4, 1024], F32)
    bvv = pl.sb("bv", [4, 1024], F32)
    alpha = pl.sb("alpha", [4, 3], F32)
    zmask = pl.sb("zmask", [4, 1024], F32)
    hard = pl.sb("hard", [4, 3, 1024], F32)
    pl.dma("sync", "bvld", bq[:], bv_q)
    pl.dma("sync", "bvld", bk[:], bv_k)
    pl.dma("sync", "bvld", bvv[:], bv_v)
    pl.wait_slot("vector", "bvld")
    pl.wait_slot("scalar", "bvld")
    for i, b in enumerate((bq, bk, bvv)):
        v.tensor_reduce(alpha[:, i:i + 1], b[:], axis=AX.X, op=ALU.add,
                        apply_absolute_value=True)
        v.tensor_scalar_mul(alpha[:, i:i + 1], alpha[:, i:i + 1], 1.0 / 1024.0)
        s.activation(out=hard[:, i, :], in_=b[:], func=AF.Sign)
        pl.after("vector", "scalar")
        v.tensor_scalar(out=hard[:, i, :], in0=hard[:, i, :],
                        scalar1=alpha[:, i:i + 1], scalar2=None,
                        op0=ALU.mult, op1=ALU.bypass)
        v.tensor_single_scalar(out=zmask[:], in_=hard[:, i, :],
                               scalar=0.0, op=ALU.is_equal)
        v.scalar_tensor_tensor(out=hard[:, i, :], in0=pl.sbufs["zmask"][:],
                               scalar=alpha[:, i:i + 1], in1=hard[:, i, :],
                               op0=ALU.mult, op1=ALU.add)
    v.tensor_tensor(hard[:, 0, :], hard[:, 0, :], hard[:, 1, :], op=ALU.mult)
    v.tensor_scalar_mul(hard[:, 0, :], hard[:, 0, :], 0.125)
    pl.after("sync", "vector")
    pl.dma("sync", "bvst", qk_dram.rearrange("(h i) -> h i", h=4), hard[:, 0, :])
    pl.dma("sync", "bvst", vb_dram.rearrange("(h i) -> h i", h=4), hard[:, 2, :])
    pl.wait_slot("sync", "bvst")
    for c in range(32):
        pl.dma("sync", "bvld2", qk_cols[:, c:c + 1], qk_dram[128 * c:128 * (c + 1)].rearrange("(p o) -> p o", o=1))
        pl.dma("sync", "bvld2", vb_cols[:, c:c + 1], vb_dram[128 * c:128 * (c + 1)].rearrange("(p o) -> p o", o=1))
    pl.free("hard", "zmask", "alpha", "bv", "bk", "bq")
    return qk_cols, vb_cols


def l0_attn(pl, xn_fm, xn_tm, masks_dram, x1p_dram, x1_dram, qk_cols, vb_cols,
            pbanks):
    t, v, s, g = pl.t, pl.v, pl.s, pl.g

    K_res = pl.sb("l0K", [128, 8, TFULL], BF16, side="right")
    V_res = pl.sb("l0V", [128, 32, 1024], BF16, side="right")
    Ptile = pl.sb("l0P", [128, 32, 512], BF16, side="right")
    qres = [pl.sb("l0qA", [128, 8, 512], BF16, side="right"),
            pl.sb("l0qB", [128, 8, 512], BF16, side="right")]
    mres = [pl.sb("l0mA", [128, 8, 512], BF16, side="right"),
            pl.sb("l0mB", [128, 8, 512], BF16, side="right")]
    x1buf = [pl.sb("l0x1A", [128, 512], F32, side="right"),
             pl.sb("l0x1B", [128, 512], F32, side="right")]
    orow = [pl.sb("l0oA", [128, 512], F32, side="right"),
            pl.sb("l0oB", [128, 512], F32, side="right")]

    entry0 = pl.tok("tensor", "vector", "scalar", "gpsimd") + pl.all_slots_tok()
    ktok = entry0
    vtok = entry0
    qtok = [entry0, entry0]
    mtok = [entry0, entry0]
    x1tok = [entry0, entry0]
    ti_glob = 0
    for hd in range(4):
        d0c = 8 * hd
        pl.waits("gpsimd", ktok)
        for dc in range(8):
            pl.dma("gpsimd", "l0K", K_res[:, dc, :],
                   xn_fm[128 * (d0c + dc):128 * (d0c + dc + 1), :])
        pl.wait_slot("vector", "l0K")
        for dc in range(8):
            v.tensor_scalar(out=K_res[:, dc, :], in0=K_res[:, dc, :],
                            scalar1=qk_cols[:, d0c + dc:d0c + dc + 1], scalar2=None,
                            op0=ALU.mult, op1=ALU.bypass)
        kready = pl.tok("vector")
        pl.waits("sync", vtok)
        for sb in range(32):
            pl.dma("sync", "l0V", V_res[:, sb, :],
                   xn_tm[128 * sb:128 * (sb + 1), 1024 * hd:1024 * (hd + 1)])
        pl.wait_slot("tensor", "l0V")
        pl.waits("tensor", kready)

        ptok = entry0
        for ti in range(4):
            blocks, win = tile_blocks(ti)
            qb = ti_glob % 2
            pl.waits("sync", qtok[qb])
            for dc in range(8):
                pl.dma("sync", "l0q" + str(qb), qres[qb][:, dc, :],
                       xn_fm[128 * (d0c + dc):128 * (d0c + dc + 1), 512 * ti:512 * (ti + 1)])
            pl.waits("gpsimd", mtok[qb])
            pl.dma("gpsimd", "l0m" + str(qb), mres[qb][:, :, :],
                   masks_dram[ti].rearrange("k p j -> p k j"))
            pl.wait_slot("tensor", "l0q" + str(qb))
            pl.after("tensor", "scalar", "vector", "gpsimd")
            win_set = {blk: j for j, blk in enumerate(win)}
            sig_tok = []
            for si, sb in enumerate(blocks):
                bank = pbanks[4 + si % 2]
                if si >= 2:
                    pl.waits("tensor", sig_tok[si - 2])
                for dc in range(8):
                    t.matmul(bank[:, :], K_res[:, dc, 128 * sb:128 * (sb + 1)],
                             qres[qb][:, dc, :],
                             start=(dc == 0), stop=(dc == 7), auto=False)
                pl.tmark()
                mm_tok = pl.tok("tensor")
                pl.waits("scalar", mm_tok)
                if si == 0:
                    pl.waits("scalar", ptok)
                s.activation(out=Ptile[:, sb, :], in_=bank[:, :], func=AF.Sigmoid)
                if sb in win_set:
                    jj = win_set[sb]
                    e = "vector" if jj % 2 == 0 else "gpsimd"
                    pl.after(e, "scalar")
                    pl.wait_slot(e, "l0m" + str(qb))
                    ee = pl.eng[e]
                    ee.tensor_tensor(Ptile[:, sb, :], Ptile[:, sb, :],
                                     mres[qb][:, jj, :], op=ALU.mult)
                    sig_tok.append(pl.tok(e))
                else:
                    sig_tok.append(pl.tok("scalar"))
            pa_tok = pl.tok("vector", "scalar", "gpsimd")
            for dc in range(8):
                cidx = d0c + dc
                bank = pbanks[dc % 2]
                pl.waits("sync", x1tok[dc % 2])
                pl.dma("sync", "l0x1" + str(dc % 2), x1buf[dc % 2][:, :],
                       x1p_dram[128 * cidx:128 * (cidx + 1), 512 * ti:512 * (ti + 1)])
                if dc == 0:
                    pl.waits("tensor", pa_tok)
                if dc >= 2:
                    pl.waits("tensor", x1tok[dc % 2])
                for si, sb in enumerate(blocks):
                    t.matmul(bank[:, :], V_res[:, sb, 128 * dc:128 * (dc + 1)],
                             Ptile[:, sb, :], start=(si == 0), stop=(si == len(blocks) - 1),
                             auto=False)
                pl.tmark()
                mm_tok = pl.tok("tensor")
                pl.waits("vector", mm_tok)
                pl.wait_slot("vector", "l0x1" + str(dc % 2))
                pl.waits("vector", pl.stok("l0o" + str(dc % 2)))
                v.scalar_tensor_tensor(out=orow[dc % 2][:, :], in0=bank[:, :],
                                       scalar=vb_cols[:, cidx:cidx + 1],
                                       in1=x1buf[dc % 2][:, :], op0=ALU.mult, op1=ALU.add)
                x1tok[dc % 2] = pl.tok("vector")
                pl.after("scalar", "vector")
                pl.dma("scalar", "l0o" + str(dc % 2),
                       x1_dram[128 * cidx:128 * (cidx + 1), 512 * ti:512 * (ti + 1)],
                       orow[dc % 2][:, :])
            qtok[qb] = pl.tok("tensor")
            mtok[qb] = pl.tok("vector", "gpsimd")
            ptok = pl.tok("tensor")
            ti_glob += 1
        ktok = pl.tok("tensor")
        vtok = pl.tok("tensor")


def emit_l0(pl, io):
    pb = io["pb"]
    qk_cols, vb_cols = l0_bind_prep(pl, io["bv_q"], io["bv_k"], io["bv_v"],
                                    io["qkd"], io["vbd"])
    pl.wait_slot("vector", "bvld2")
    pl.wait_slot("tensor", "bvld2")
    st = ln_tm_alloc(pl)
    st["pb"] = pb
    st["xn_res"] = pl.sb("xn_res", [128, NCHUNK, TOWN], BF16, side="right")
    ln_tm_own_extra(pl, st)
    nown = TOWN // 128
    ntc = TFULL // 128
    for c in range(nown):
        ln_tm_part1(pl, st, c, io["x_tm"], io["xn0_tm"])
        ln_tm_part2(pl, st, c, io["xn0_fm"])
    ln_tm_own_stores(pl, st, io["xn0_fm"], nown)
    pl.free("bB", "sB")
    st["tail"] = True
    st["etok"][0] = (st["etok"][0] or []) + (st["etok"][1] or [])
    st["use"][0] = (st["use"][0] or []) + (st["use"][1] or [])
    st["out"][0] = (st["out"][0] or []) + (st["out"][1] or [])
    st["ntok"][0] = (st["ntok"][0] or []) + (st["ntok"][1] or [])

    def lnhook(a):
        if a >= 1 and 15 + a <= ntc - 1:
            ln_tm_part2(pl, st, 15 + a, io["xn0_fm"])
        if 16 + a <= ntc - 1:
            ln_tm_part1(pl, st, 16 + a, io["x_tm"], io["xn0_tm"])

    b1_sb = pl.sb("b1_sb", [128, 32], F32)
    b2_sb = pl.sb("b2_sb", [128, 32], F32)
    b3_sb = pl.sb("b3_sb", [128, 32], F32)
    pl.dma("sync", "bld", b1_sb[:], io["b1c"])
    pl.dma("sync", "bld", b2_sb[:], io["b2c"])
    pl.dma("sync", "bld", b3_sb[:], io["b3c"])
    pl.wait_slot("scalar", "bld")
    pl.wait_slot("vector", "bld")
    l0_ffn_c1(pl, st["xn_res"], io["w12T"], b1_sb, b2_sb, io["x4d"], TOWN, pb,
              hook=lnhook)
    pl.free("xn_res")
    pl.free("bD", "snk", "bA", "sA")
    pl.waits("sync", pl.tok("tensor", "vector", "scalar") + pl.all_slots_tok())
    x4_res = load_resident_fm(pl, "x4_res", io["x4d"], [(0, TOWN)])
    pl.wait_slot("tensor", "x4_res")
    l0_ffn_c2(pl, x4_res, io["w3T"], b3_sb, io["x_fm"], io["x1p"], TOWN,
              [(0, TOWN)], pb)
    for nme in ("c2oB", "c2oA", "c2xB", "c2xA", "x4_res"):
        pl.free(nme)
    for nme in ("x4rowB", "x4rowA", "silu_sb", "wgtD", "wgtC", "wgtB", "wgtA"):
        pl.free(nme)
    l0_attn(pl, io["xn0_fm"], io["xn0_tm"], io["masks"], io["x1p"], io["x1"],
            qk_cols, vb_cols, pb)


def declare_l0_io(nc):
    io = {}
    io["x_fm"] = nc.dram_tensor("x_fm", [D, TFULL], F32, kind="ExternalInput").ap()
    io["x_tm"] = nc.dram_tensor("x_tm", [TFULL, D], F32, kind="ExternalInput").ap()
    io["masks"] = nc.dram_tensor("masks", [4, 8, 128, 512], BF16, kind="ExternalInput").ap()
    for nme in ("bv_q", "bv_k", "bv_v"):
        io[nme] = nc.dram_tensor(nme, [4, 1024], F32, kind="ExternalInput").ap()
    io["w12T"] = nc.dram_tensor("w12T", [4096, 8192], F32, kind="ExternalInput").ap()
    io["w3T"] = nc.dram_tensor("w3T", [4096, 4096], F32, kind="ExternalInput").ap()
    for nme in ("b1c", "b2c", "b3c"):
        io[nme] = nc.dram_tensor(nme, [128, 32], F32, kind="ExternalInput").ap()
    io["xn0_fm"] = nc.dram_tensor("xn0_fm", [D, TFULL], BF16, kind="Internal").ap()
    io["xn0_tm"] = nc.dram_tensor("xn0_tm", [TFULL, D], BF16, kind="Internal").ap()
    io["rows_r"] = nc.dram_tensor("rows_r", [1, TFULL], F32, kind="Internal").ap()
    io["rows_n"] = nc.dram_tensor("rows_n", [1, TFULL], F32, kind="Internal").ap()
    io["x4d"] = nc.dram_tensor("x4d", [4096, TOWN], BF16, kind="Internal").ap()
    io["x1p"] = nc.dram_tensor("x1p", [D, TOWN], F32,
                               kind="ExternalOutput" if DEBUG_X1P else "Internal").ap()
    io["qkd"] = nc.dram_tensor("qkd", [4096], F32, kind="Internal").ap()
    io["vbd"] = nc.dram_tensor("vbd", [4096], F32, kind="Internal").ap()
    return io


def build_l0_program():
    nc = bass.Bass("TRN2", target_bir_lowering=False)
    stack = ExitStack()
    io = declare_l0_io(nc)
    io["x1"] = nc.dram_tensor("x1", [D, TOWN], F32, kind="ExternalOutput").ap()
    with stack:
        pl = Planner(nc, stack)
        declare_and_load_consts(pl)
        io["pb"] = [pl.ps(f"pb{i}", [128, 512], F32) for i in range(8)]
        for e in ("tensor", "vector", "scalar"):
            pl.wait_slot(e, "consts")
        emit_l0(pl, io)
        pl.fence_all()
        pl.after("sync", "scalar", "gpsimd", "vector", "tensor")
        counts = pl.finalize()
    return nc, counts


def prep_l0_inputs(x, bv_q, bv_k, bv_v, w12, b12, w3, b3, consts):
    shared = {
        "bv_q": np.asarray(bv_q, dtype=np.float32),
        "bv_k": np.asarray(bv_k, dtype=np.float32),
        "bv_v": np.asarray(bv_v, dtype=np.float32),
        "w12T": np.ascontiguousarray(np.asarray(w12).T),
        "b1c": np.ascontiguousarray(np.asarray(b12)[:4096].reshape(32, 128).T),
        "b2c": np.ascontiguousarray(np.asarray(b12)[4096:].reshape(32, 128).T),
        "w3T": np.ascontiguousarray(np.asarray(w3).T),
        "b3c": np.ascontiguousarray(np.asarray(b3).reshape(32, 128).T),
    }
    shared.update(host_const_inputs(consts))
    per_core = []
    for c in range(8):
        b, h = c // 2, c % 2
        cols = host_col_order(h)
        xb = np.asarray(x[b], dtype=np.float32)[cols]
        m = dict(shared)
        m["x_tm"] = np.ascontiguousarray(xb)
        m["x_fm"] = np.ascontiguousarray(xb.T)
        m["masks"] = host_masks(h)
        per_core.append(m)
    return per_core


def place_own_cols(out_full_b, own_cols_arr, h):
    cols = host_col_order(h)[:TOWN]
    out_full_b[cols] = own_cols_arr


def _rope_tile(pl, bank, bank_swap, cosd, sinds, out_slice, swap_bf, bufs, e="vector"):
    t, s = pl.t, pl.s
    ee = pl.eng[e]
    raw, swp, t1, t2 = bufs["raw"], bufs["swp"], bufs["t1"], bufs["t2"]
    mm_tok = pl.tok("tensor")
    pl.waits("scalar", mm_tok)
    s.copy(out=raw[:, :], in_=bank[:, :])
    pl.after("tensor", "scalar")
    t.matmul(bank_swap[:, :], swap_bf[:], raw[:, :], start=True, stop=True, auto=False)
    pl.tmark()
    pl.after("scalar", "tensor")
    s.copy(out=swp[:, :], in_=bank_swap[:, :])
    pl.after(e, "scalar")
    ee.tensor_tensor(t1[:, :], raw[:, :], cosd, op=ALU.mult)
    ee.tensor_tensor(t2[:, :], swp[:, :], sinds, op=ALU.mult)
    ee.tensor_tensor(out_slice, t1[:, :], t2[:, :], op=ALU.add)


def l1_qkv_q(pl, xn_res, wqkvp, ropeq_cos, ropeq_sin, q1_dram, pbanks, T, swap_bf):
    t = pl.t
    wgt = [pl.sb("wgtA", [128, NCHUNK, 128], BF16), pl.sb("wgtB", [128, NCHUNK, 128], BF16)]
    bufs = {n: pl.sb("rp_" + n, [128, 512], BF16, side="right") for n in ("raw", "swp", "t1", "t2")}
    q1row = [pl.sb("q1rowA", [128, 2048], BF16, side="right"),
             pl.sb("q1rowB", [128, 2048], BF16, side="right")]
    ntt = T // 512

    entry = pl.tok("tensor", "vector", "scalar", "gpsimd") + pl.all_slots_tok()
    wtok = [entry, entry]
    btok = [entry] * 2
    rtok = [entry, entry]
    rope_tok = entry
    for oc in range(32):
        stream_w_oc(pl, wqkvp, NCHUNK, oc * 128, 128, wgt[oc % 2], "wgt" + str(oc % 2),
                    war_tok=wtok[oc % 2])
        pl.wait_slot("tensor", "wgt" + str(oc % 2))
        for tt in range(ntt):
            bank = pbanks[tt % 2]
            bank_swap = pbanks[2 + tt % 2]
            pl.waits("tensor", btok[tt % 2])
            for ic in range(NCHUNK):
                t.matmul(bank[:, :], wgt[oc % 2][:, ic, :], xn_res[:, ic, 512 * tt:512 * (tt + 1)],
                         start=(ic == 0), stop=(ic == NCHUNK - 1), auto=False)
            pl.tmark()
            pl.waits("scalar", rope_tok)
            if tt == 0:
                pl.waits("vector", rtok[oc % 2])
            _rope_tile(pl, bank, bank_swap, ropeq_cos[:, 512 * tt:512 * (tt + 1)],
                       ropeq_sin[:, 512 * tt:512 * (tt + 1)],
                       q1row[oc % 2][:, 512 * tt:512 * (tt + 1)], swap_bf, bufs)
            rope_tok = pl.tok("vector")
            btok[tt % 2] = rope_tok + pl.tok("scalar")
        wtok[oc % 2] = pl.tok("tensor")
        pl.after("scalar", "vector")
        pl.dma("scalar", "q1row" + str(oc % 2),
               q1_dram[128 * oc:128 * (oc + 1), 0:T], q1row[oc % 2][:, 0:T])
        rtok[oc % 2] = pl.stok("q1row" + str(oc % 2))
    pl.fence("scalar")
    pl.waits("gpsimd", pl.tok("tensor", "vector", "scalar"))
    for nme in ("q1rowB", "q1rowA", "rp_t2", "rp_t1", "rp_swp", "rp_raw"):
        pl.free(nme)


def l1_qkv_kv(pl, xn_src_fm, Tkv, wqkvp, ropek_cos, ropek_sin, k1_dram, v1t_dram,
              pbanks, swap_bf, rank_order=False):
    t, v, s, g = pl.t, pl.v, pl.s, pl.g
    idb = pl.sbufs["c_ident_bf"]
    wkv = pl.sb("wkv_res", [128, NCHUNK, 2048], BF16, side="right")
    xt = pl.sb("kv_xt", [128, NCHUNK, 512], BF16, side="right")
    bufs = {n: pl.sb("rp_" + n, [128, 512], BF16, side="right") for n in ("raw", "swp", "t1", "t2")}
    krow = pl.sb("krow", [128, 512], BF16, side="right")
    vt_sb = pl.sb("vt_sb", [128, 4, 1024], BF16, side="right")
    pbt = pbanks[7][:, :].bitcast(BF16)

    entry = pl.tok("tensor", "vector", "scalar", "gpsimd") + pl.all_slots_tok()
    pl.waits("gpsimd", entry)
    for ic in range(NCHUNK):
        pl.dma("gpsimd", "wkv_res", wkv[:, ic, :],
               wqkvp[128 * ic:128 * (ic + 1), 4096:6144])
    pl.wait_slot("tensor", "wkv_res")

    xtok = entry
    rope_tok = entry
    ktok = [entry, entry]
    vtok = entry
    for tt in range(Tkv // 512):
        pl.waits("sync", xtok)
        if rank_order:
            rk, c0 = tt // 4, 512 * (tt % 4)
            xview = xn_src_fm[rk * D:(rk + 1) * D, c0:c0 + 512]
        else:
            xview = xn_src_fm[:, 512 * tt:512 * (tt + 1)]
        pl.dma("sync", "kv_xt", xt[:, :, :],
               xview.rearrange("(ic p) j -> p ic j", p=128))
        pl.wait_slot("tensor", "kv_xt")
        for oc in range(16):
            bank = pbanks[oc % 2]
            bank2 = pbanks[2 + oc % 2]
            pl.waits("tensor", ktok[oc % 2])
            for ic in range(NCHUNK):
                t.matmul(bank[:, :], wkv[:, ic, 128 * oc:128 * (oc + 1)], xt[:, ic, :],
                         start=(ic == 0), stop=(ic == NCHUNK - 1), auto=False)
            pl.tmark()
            if oc < 8:
                pl.waits("scalar", rope_tok)
                pl.waits("vector", pl.stok("krow"))
                _rope_tile(pl, bank, bank2, ropek_cos[:, 512 * tt:512 * (tt + 1)],
                           ropek_sin[:, 512 * tt:512 * (tt + 1)], krow[:, :], swap_bf, bufs)
                rope_tok = pl.tok("vector")
                ktok[oc % 2] = rope_tok + pl.tok("scalar")
                pl.after("scalar", "vector")
                pl.dma("scalar", "krow",
                       k1_dram[128 * oc:128 * (oc + 1), 512 * tt:512 * (tt + 1)], krow[:, :])
            else:
                mm_tok = pl.tok("tensor")
                pl.waits("scalar", mm_tok)
                pl.waits("scalar", pl.tok("tensor"))
                s.copy(out=bufs["raw"][:, :], in_=bank[:, :])
                pl.after("tensor", "scalar")
                if oc == 8:
                    pl.waits("tensor", vtok)
                for j in range(4):
                    t.transpose(pbt[:, 128 * j:128 * (j + 1)],
                                bufs["raw"][:, 128 * j:128 * (j + 1)], idb[:], auto=False)
                pl.tmark()
                pl.after("vector", "tensor")
                pl.waits("vector", pl.stok("vt_sb"))
                for j in range(4):
                    v.tensor_copy(vt_sb[:, j, 128 * (oc - 8):128 * (oc - 7)],
                                  pbt[:, 128 * j:128 * (j + 1)])
                rope_tok = rope_tok + pl.tok("vector")
                ktok[oc % 2] = pl.tok("vector")
        pl.after("sync", "vector")
        for j in range(4):
            pl.dma("sync", "vt_sb", v1t_dram[512 * tt + 128 * j:512 * tt + 128 * (j + 1), :],
                   vt_sb[:, j, :])
        vtok = pl.tok("vector")
        xtok = pl.tok("tensor")
    pl.fence("scalar")
    pl.fence("sync")
    pl.waits("gpsimd", pl.tok("tensor", "vector", "scalar"))
    for nme in ("vt_sb", "krow", "rp_t2", "rp_t1", "rp_swp", "rp_raw", "kv_xt", "wkv_res"):
        pl.free(nme)


def l1_moe(pl, xn_res, WrT, A2, B2, E_dram, x1_fm, x2p_dram, pbanks, T):
    t, v, s, g = pl.t, pl.v, pl.s, pl.g
    idf = pl.sbufs["c_ident_f32"]
    ntt = T // 512
    wr_sb = pl.sb("wr_sb", [128, NCHUNK, 64], BF16, side="right")
    sc_sb = pl.sb("sc_sb", [64, 2048], F32, side="right")
    st_tm = pl.sb("st_tm", [128, 16, 64], F32, side="right")
    tk = {n: pl.sb("tk_" + n, [128, 64], F32, side="right")
          for n in ("m16", "srep", "srep2", "ge", "mask")}
    Gfm = pl.sb("Gfm", [64, 2048], BF16, side="right")
    e_sb = pl.sb("e_sb", [64, 1024], BF16, side="right")
    wgt = [pl.sb("mwgtA", [128, NCHUNK, 128], BF16, side="right"),
           pl.sb("mwgtB", [128, NCHUNK, 128], BF16, side="right")]
    hg_tt = pl.sb("hg_tt", [128, 8, 512], BF16, side="right")
    gexp = pl.sb("gexp", [128, 512], BF16, side="right")
    x1b = [pl.sb("mx1A", [128, 512], F32, side="right"),
           pl.sb("mx1B", [128, 512], F32, side="right")]
    x2b = [pl.sb("mx2A", [128, 512], F32, side="right"),
           pl.sb("mx2B", [128, 512], F32, side="right")]

    entry = pl.tok("tensor", "vector", "scalar", "gpsimd") + pl.all_slots_tok()
    pl.waits("gpsimd", entry)
    stream_w_oc(pl, WrT, NCHUNK, 0, 64, wr_sb, "wr_sb", war_tok=None)
    pl.dma("gpsimd", "e_sb", e_sb[:, :], E_dram)
    pl.wait_slot("tensor", "wr_sb")
    pl.waits("tensor", entry)
    for tt in range(ntt):
        bank = pbanks[tt % 2]
        if tt >= 2:
            pl.after("tensor", "scalar")
        for ic in range(NCHUNK):
            t.matmul(bank[0:64, :], wr_sb[:, ic, :], xn_res[:, ic, 512 * tt:512 * (tt + 1)],
                     start=(ic == 0), stop=(ic == NCHUNK - 1), auto=False)
        pl.tmark()
        pl.after("scalar", "tensor")
        s.copy(out=sc_sb[:, 512 * tt:512 * (tt + 1)], in_=bank[0:64, :])
    pl.after("tensor", "scalar")
    for tb in range(T // 128):
        bank = pbanks[4 + tb % 2]
        if tb >= 2:
            pl.after("tensor", "vector")
        t.transpose(bank[0:128, 0:64], sc_sb[0:64, 128 * tb:128 * (tb + 1)],
                    idf[0:64, 0:64], auto=False)
        pl.tmark()
        pl.after("vector", "tensor")
        v.tensor_copy(st_tm[:, tb, 0:64], bank[0:128, 0:64])
    for tb in range(T // 128):
        pl.after("vector", "scalar")
        v.max(out=tk["m16"][:, 0:8], in_=st_tm[:, tb, :])
        v.match_replace(out=tk["srep"][:, :], in_to_replace=tk["m16"][:, 0:8],
                        in_values=st_tm[:, tb, :], imm_value=-1e30)
        v.max(out=tk["m16"][:, 8:16], in_=tk["srep"][:, :])
        v.match_replace(out=tk["srep2"][:, :], in_to_replace=tk["m16"][:, 8:16],
                        in_values=tk["srep"][:, :], imm_value=-1e30)
        v.tensor_scalar_mul(tk["m16"][:, 16:17], tk["m16"][:, 0:1], -1.0)
        pl.after("scalar", "vector")
        s.activation(out=tk["m16"][:, 32:48], in_=tk["m16"][:, 0:16], func=AF.Exp,
                     bias=tk["m16"][:, 16:17])
        s.activation(out=tk["ge"][:, :], in_=st_tm[:, tb, :], func=AF.Exp,
                     bias=tk["m16"][:, 16:17])
        pl.after("vector", "scalar")
        v.reduce_sum(tk["m16"][:, 17:18], tk["m16"][:, 32:48], axis=AX.X)
        v.reciprocal(tk["m16"][:, 18:19], tk["m16"][:, 17:18])
        v.tensor_scalar_mul(tk["m16"][:, 18:19], tk["m16"][:, 18:19], 2.0)
        v.tensor_tensor(tk["mask"][:, :], st_tm[:, tb, :], tk["srep2"][:, :], op=ALU.is_gt)
        v.scalar_tensor_tensor(out=st_tm[:, tb, :], in0=tk["ge"][:, :],
                               scalar=tk["m16"][:, 18:19], in1=tk["mask"][:, :],
                               op0=ALU.mult, op1=ALU.mult)
    for tb in range(T // 128):
        bank = pbanks[4 + tb % 2]
        pl.after("tensor", "vector", "scalar")
        t.transpose(bank[0:64, 0:128], st_tm[:, tb, 0:64], idf[:], auto=False)
        pl.tmark()
        pl.after("vector", "tensor")
        v.tensor_copy(Gfm[0:64, 128 * tb:128 * (tb + 1)], bank[0:64, 0:128])
    wtok = [pl.tok("tensor"), pl.tok("tensor")]
    btok = [pl.tok("vector")] * 2
    x1tok = [pl.tok("vector"), pl.tok("vector")]
    x2tok = [None, None]
    pl.wait_slot("tensor", "e_sb")
    for tt in range(ntt):
        for krc in range(8):
            it = tt * 8 + krc
            stream_w_oc(pl, A2, NCHUNK, krc * 128, 128, wgt[it % 2], "mwgt" + str(it % 2),
                        war_tok=wtok[it % 2])
            bankG = pbanks[4 + krc % 2]
            bankH = pbanks[2 + krc % 2]
            pl.after("tensor", "scalar", "vector")
            t.matmul(bankG[:, :], e_sb[:, 128 * krc:128 * (krc + 1)],
                     Gfm[0:64, 512 * tt:512 * (tt + 1)], start=True, stop=True, auto=False)
            pl.tmark()
            pl.after("scalar", "tensor")
            pl.after("scalar", "vector")
            s.copy(out=gexp[:, :], in_=bankG[:, :])
            pl.wait_slot("tensor", "mwgt" + str(it % 2))
            for ic in range(NCHUNK):
                t.matmul(bankH[:, :], wgt[it % 2][:, ic, :], xn_res[:, ic, 512 * tt:512 * (tt + 1)],
                         start=(ic == 0), stop=(ic == NCHUNK - 1), auto=False)
            pl.tmark()
            wtok[it % 2] = pl.tok("tensor")
            pl.after("vector", "tensor", "scalar")
            v.tensor_tensor(hg_tt[:, krc, :], bankH[:, :], gexp[:, :], op=ALU.mult)
        hgrdy = pl.tok("vector")
        for nc_ in range(32):
            it = tt * 32 + nc_
            stream_w_oc(pl, B2, 8, nc_ * 128, 128, wgt[it % 2], "mwgt" + str(it % 2),
                        war_tok=wtok[it % 2])
            bank = pbanks[nc_ % 2]
            pl.waits("sync", x1tok[nc_ % 2])
            pl.dma("sync", "mx1" + str(nc_ % 2), x1b[nc_ % 2][:, :],
                   x1_fm[128 * nc_:128 * (nc_ + 1), 512 * tt:512 * (tt + 1)])
            pl.wait_slot("tensor", "mwgt" + str(it % 2))
            pl.waits("tensor", btok[nc_ % 2], hgrdy)
            for ic in range(8):
                t.matmul(bank[:, :], wgt[it % 2][:, ic, :], hg_tt[:, ic, :],
                         start=(ic == 0), stop=(ic == 7), auto=False)
            pl.tmark()
            wtok[it % 2] = pl.tok("tensor")
            pl.waits("vector", pl.tok("tensor"))
            pl.wait_slot("vector", "mx1" + str(nc_ % 2))
            pl.waits("vector", x2tok[nc_ % 2])
            v.tensor_tensor(x2b[nc_ % 2][:, :], bank[:, :], x1b[nc_ % 2][:, :], op=ALU.add)
            btok[nc_ % 2] = pl.tok("vector")
            x1tok[nc_ % 2] = pl.tok("vector")
            pl.after("scalar", "vector")
            pl.dma("scalar", "mx2" + str(nc_ % 2),
                   x2p_dram[128 * nc_:128 * (nc_ + 1), 512 * tt:512 * (tt + 1)],
                   x2b[nc_ % 2][:, :])
            x2tok[nc_ % 2] = pl.stok("mx2" + str(nc_ % 2))
    for nme in ("mx2B", "mx2A", "mx1B", "mx1A", "gexp", "hg_tt", "mwgtB", "mwgtA",
                "e_sb", "Gfm", "tk_mask", "tk_ge", "tk_srep2", "tk_srep", "tk_m16",
                "st_tm", "sc_sb", "wr_sb"):
        pl.free(nme)


def l1_attn(pl, masks_dram, q1_dram, k1_dram, v1t_dram, Tkv, pbanks, kg_list=range(8), ql_list=range(4)):
    t, v, s, g = pl.t, pl.v, pl.s, pl.g
    ones_bf = pl.sbufs["c_ones_bf_col"]
    SC = 0.08838834764831845
    C2 = (SC * 1.0002) ** 2

    ao = pl.sb("ao_res", [128, NCHUNK, 2048], BF16, side="right")
    Kg = [pl.sb("l1KA", [128, Tkv], BF16, side="right"),
          pl.sb("l1KB", [128, Tkv], BF16, side="right")]
    Vg = pl.sb("l1V", [128, Tkv // 128, 128], BF16, side="right")
    Pt = pl.sb("l1P", [128, 32, 512], BF16, side="right")
    qre = [pl.sb("l1qA", [128, 2048], BF16, side="right"),
           pl.sb("l1qB", [128, 2048], BF16, side="right")]
    mre = pl.sb("l1mA", [128, 8, 512], BF16, side="right")
    ubias = pl.sb("l1ub", [128, 16], F32, side="right")
    stat = pl.sb("l1stat", [128, 16], F32, side="right")
    qmax = pl.sb("l1qmax", [1, 128], F32, side="right")
    negr = pl.sb("l1neg", [1, 128], F32, side="right")
    rnb = pl.sb("l1rnb", [128, 512], BF16, side="right")
    nrow = Pt[0:1, 30:32, :].rearrange("p a b -> p (a b)").bitcast(F32)
    rn_bf = pl.sb("l1rnr", [1, 512], BF16, side="right")

    pl.after("gpsimd", "tensor", "vector", "scalar")
    g.memset(stat[:, :], 0.0)
    g.memset(qmax[:, :], 0.0)
    g.memset(negr[:, :], -1.0)
    entry = pl.tok("tensor", "vector", "scalar", "gpsimd") + pl.all_slots_tok()
    pl.waits("tensor", entry)
    QKB = [pbanks[0], pbanks[1], pbanks[4], pbanks[5]]
    klist = list(kg_list)
    qlist = list(ql_list)
    sched = [(kg_, ql_) for kg_ in klist for ql_ in qlist]
    half = Tkv // 256

    def load_K(kg_):
        pl.dma("sync", "l1K" + str(kg_ % 2), Kg[kg_ % 2][:, :],
               k1_dram[128 * kg_:128 * (kg_ + 1), 0:Tkv])

    def load_V(kg_):
        pl.dma("sync", "l1V", Vg[:, 0:half, :],
               v1t_dram[0:Tkv // 2, 128 * kg_:128 * (kg_ + 1)].rearrange("(sb p) d -> p sb d", p=128))
        pl.dma("sync", "l1V", Vg[:, half:2 * half, :],
               v1t_dram[Tkv // 2:Tkv, 128 * kg_:128 * (kg_ + 1)].rearrange("(sb p) d -> p sb d", p=128))

    def load_q(qh_):
        pl.dma("sync", "l1q" + str(qh_ % 2), qre[qh_ % 2][:, :],
               q1_dram[128 * qh_:128 * (qh_ + 1), :])

    pl.waits("sync", entry)
    load_K(klist[0])
    load_V(klist[0])

    sqtok = [entry, entry]
    mmtok = [entry, entry]
    for qh in range(32):
        qb = qh % 2
        pl.waits("sync", sqtok[qb])
        load_q(qh)
        eng = "scalar" if qb == 0 else "vector"
        pl.wait_slot(eng, "l1q" + str(qb))
        pl.waits(eng, mmtok[qb])
        reg = Pt[:, 4 * qb:4 * qb + 4, :]
        if qb == 0:
            s.activation(out=reg, in_=qre[qb][:, :], func=AF.Square)
        else:
            v.tensor_tensor(reg, qre[qb][:, :], qre[qb][:, :], op=ALU.mult)
        sqtok[qb] = pl.tok(eng)
        pl.waits("tensor", sqtok[qb])
        pl.after("tensor", "vector")
        for jj in range(4):
            t.matmul(QKB[jj][0:1, :], ones_bf[:], Pt[:, 4 * qb + jj, :],
                     start=True, stop=True, auto=False)
        pl.tmark()
        mmtok[qb] = pl.tok("tensor")
        pl.waits("vector", mmtok[qb])
        for jj in range(4):
            v.reduce_max(qmax[0:1, 4 * qh + jj:4 * qh + jj + 1],
                         QKB[jj][0:1, :], axis=AX.X)

    vtok = entry
    kvtok = [entry, entry]
    qtok = [sqtok[0], sqtok[1]]
    mtok = entry
    otok = [entry, entry]
    pend = {"bankO": None}
    ti_glob = 0
    for ki, kg in enumerate(klist):
        kgb = kg % 2
        if ki > 0:
            pl.waits("sync", vtok)
            load_V(kg)
        if ki == 0:
            p0 = qlist[0] % 2
            pl.waits("sync", qtok[p0])
            load_q(4 * kg + qlist[0])
        if ki + 1 < len(klist):
            ng = klist[ki + 1]
            pl.waits("sync", kvtok[ng % 2])
            load_K(ng)
        pl.wait_slot("scalar", "l1K" + str(kgb))
        pl.waits("scalar", pl.tok("tensor"))
        s.activation(out=Pt[:, 0:8, :], in_=Kg[kgb][:, :], func=AF.Square)
        kst = pl.tok("scalar")
        pl.waits("tensor", kst)
        pl.after("tensor", "vector")
        pl.wait_slot("tensor", "l1V")
        for jj in range(Tkv // 512):
            if jj == 4:
                pl.after("tensor", "vector")
            t.matmul(QKB[jj % 4][0:1, :], ones_bf[:], Pt[:, jj, :],
                     start=True, stop=True, auto=False)
            if jj % 4 == 3:
                pl.tmark()
                grp = pl.tok("tensor")
                pl.waits("vector", grp)
                for j2 in range(jj - 3, jj + 1):
                    v.reduce_max(stat[0:1, j2:j2 + 1], QKB[j2 % 4][0:1, :],
                                 axis=AX.X)
        v.reduce_max(stat[0:1, 8:9], stat[0:1, 0:8], axis=AX.X)
        v.tensor_scalar(out=nrow[0:1, 0:16], in0=qmax[0:1, 16 * kg:16 * kg + 16],
                        scalar1=stat[0:1, 8:9], scalar2=C2,
                        op0=ALU.mult, op1=ALU.mult)
        pl.after("scalar", "vector")
        s.activation(out=nrow[0:1, 16:32], in_=nrow[0:1, 0:16], func=AF.Sqrt)
        pl.after("tensor", "vector")
        pl.after("tensor", "scalar")
        t.matmul(pbanks[6][0:128, 0:16], negr[:], nrow[0:1, 16:32],
                 start=True, stop=True, auto=False)
        pl.tmark()
        pl.after("scalar", "tensor")
        s.copy(out=ubias[:, :], in_=pbanks[6][0:128, 0:16])

        for qi, ql in enumerate(qlist):
            qh = 4 * kg + ql
            qb = ql % 2
            pl.wait_slot("tensor", "l1q" + str(qb))
            prev_tail = [pl.tok("scalar") + pl.tok("vector")] * 4
            for ti in range(4):
                blocks, win = tile_blocks(ti)
                if ti == 3:
                    idx = ki * len(qlist) + qi
                    if idx + 1 < len(sched):
                        nkg, nql = sched[idx + 1]
                        pl.waits("sync", qtok[nql % 2])
                        load_q(4 * nkg + nql)
                pl.waits("gpsimd", mtok)
                pl.dma("gpsimd", "l1mA", mre[:, :, :],
                       masks_dram[ti].rearrange("k p j -> p k j"))
                win_set = {blk: jx for jx, blk in enumerate(win)}
                nonwin = [b_ for b_ in blocks if b_ not in win_set]
                blocks = nonwin[0:4] + win + nonwin[4:]
                bankO = pbanks[2 + ti_glob % 2]
                bankN = pbanks[6]
                exp_toks = []
                nb = len(blocks)
                for si, sb in enumerate(blocks):
                    bank = QKB[si % 4]
                    pl.waits("tensor", exp_toks[si - 4] if si >= 4 else prev_tail[si])
                    t.matmul(bank[:, :], Kg[kgb][:, 128 * sb:128 * (sb + 1)],
                             qre[qb][:, 512 * ti:512 * (ti + 1)],
                             start=True, stop=True, auto=False)
                    pl.tmark()
                    mm_tok = pl.tok("tensor")
                    pl.waits("scalar", mm_tok)
                    s.activation(out=Pt[:, sb, :], in_=bank[:, :], func=AF.Exp,
                                 bias=ubias[:, 4 * ql + ti:4 * ql + ti + 1],
                                 scale=SC)
                    if sb in win_set:
                        jx = win_set[sb]
                        pl.after("vector", "scalar")
                        pl.wait_slot("vector", "l1mA")
                        v.tensor_tensor(Pt[:, sb, :], Pt[:, sb, :],
                                        mre[:, jx, :], op=ALU.mult)
                        exp_toks.append(pl.tok("vector"))
                    else:
                        exp_toks.append(pl.tok("scalar"))
                    if si == 2 and pend["bankO"] is not None:
                        pl.waits("tensor", pend["nc_tok"])
                        pl.after("tensor", "scalar")
                        t.matmul(bankN[:, :], pl.sbufs["c_ones_bf_row"][:],
                                 pend["rn"], start=True, stop=True, auto=False)
                        pl.tmark()
                        pl.after("scalar", "tensor")
                        s.copy(out=rnb[:, :], in_=bankN[:, :])
                        pl.after("vector", "scalar")
                        v.tensor_tensor(pend["ao"], pend["bankO"][:, :], rnb[:, :],
                                        op=ALU.mult)
                        otok[pend["par"]] = pl.tok("vector")
                        pend["bankO"] = None
                    if si >= 2:
                        pb_ = blocks[si - 2]
                        pl.waits("tensor", exp_toks[si - 2])
                        if si == 2:
                            pl.waits("tensor", otok[ti_glob % 2])
                            pl.after("tensor", "scalar")
                        t.matmul(bankO[:, :], Vg[:, pb_, :], Pt[:, pb_, :],
                                 start=(si - 2 == 0), stop=False, auto=False)
                        t.matmul(bankN[0:1, :], ones_bf[:], Pt[:, pb_, :],
                                 start=(si - 2 == 0), stop=False, auto=False)
                        pl.tmark()
                for tail in (nb - 2, nb - 1):
                    pb_ = blocks[tail]
                    pl.waits("tensor", exp_toks[tail])
                    t.matmul(bankO[:, :], Vg[:, pb_, :], Pt[:, pb_, :],
                             start=(tail == 0), stop=(tail == nb - 1), auto=False)
                    t.matmul(bankN[0:1, :], ones_bf[:], Pt[:, pb_, :],
                             start=(tail == 0), stop=(tail == nb - 1), auto=False)
                    pl.tmark()
                prev_tail = [exp_toks[-4], exp_toks[-3], exp_toks[-2], exp_toks[-1]]
                nrm_tok = pl.tok("tensor")
                pl.waits("scalar", nrm_tok)
                s.activation(out=nrow[0:1, :], in_=bankN[0:1, :], func=AF.Ln)
                s.activation(out=rn_bf[0:1, :], in_=nrow[0:1, :], func=AF.Exp,
                             scale=-1.0)
                pend["bankO"] = bankO
                pend["ao"] = ao[:, qh, 512 * ti:512 * (ti + 1)]
                pend["rn"] = rn_bf[0:1, :]
                pend["nc_tok"] = pl.tok("vector")
                pend["par"] = ti_glob % 2
                mtok = pl.tok("vector", "gpsimd")
                ti_glob += 1
            qtok[qb] = pl.tok("tensor")
        kvtok[kgb] = pl.tok("tensor", "scalar")
        vtok = pl.tok("tensor")
    if pend["bankO"] is not None:
        pl.waits("tensor", pend["nc_tok"])
        pl.after("tensor", "scalar")
        t.matmul(pbanks[6][:, :], pl.sbufs["c_ones_bf_row"][:], pend["rn"],
                 start=True, stop=True, auto=False)
        pl.tmark()
        pl.after("scalar", "tensor")
        s.copy(out=rnb[:, :], in_=pbanks[6][:, :])
        pl.after("vector", "scalar")
        v.tensor_tensor(pend["ao"], pend["bankO"][:, :], rnb[:, :], op=ALU.mult)
    return ao


def l1_wo(pl, ao_res, woT, x2p_dram, out_dram, pbanks, T):
    t, v = pl.t, pl.v
    wgt = [pl.sb("wgtA", [128, NCHUNK, 128], BF16), pl.sb("wgtB", [128, NCHUNK, 128], BF16)]
    x2b = [pl.sb("wx2A", [128, 512], F32, side="right"),
           pl.sb("wx2B", [128, 512], F32, side="right")]
    orow = [pl.sb("woA", [128, 2048], F32, side="right"),
            pl.sb("woB", [128, 2048], F32, side="right")]
    ntt = T // 512

    entry = pl.tok("tensor", "vector", "scalar", "gpsimd") + pl.all_slots_tok()
    wtok = [entry, entry]
    btok = [entry] * 4
    xtok = [entry, entry]
    otok = [entry, entry]
    for nc_ in range(32):
        stream_w_oc(pl, woT, NCHUNK, nc_ * 128, 128, wgt[nc_ % 2], "wgt" + str(nc_ % 2),
                    war_tok=wtok[nc_ % 2])
        pl.wait_slot("tensor", "wgt" + str(nc_ % 2))
        for tt in range(ntt):
            bank = pbanks[(nc_ * ntt + tt) % 4]
            pl.waits("sync", xtok[tt % 2])
            pl.dma("sync", "wx2" + str(tt % 2), x2b[tt % 2][:, :],
                   x2p_dram[128 * nc_:128 * (nc_ + 1), 512 * tt:512 * (tt + 1)])
            pl.waits("tensor", btok[(nc_ * ntt + tt) % 4])
            for ic in range(NCHUNK):
                t.matmul(bank[:, :], wgt[nc_ % 2][:, ic, :], ao_res[:, ic, 512 * tt:512 * (tt + 1)],
                         start=(ic == 0), stop=(ic == NCHUNK - 1), auto=False)
            pl.tmark()
            mm_tok = pl.tok("tensor")
            pl.waits("vector", mm_tok)
            pl.wait_slot("vector", "wx2" + str(tt % 2))
            if tt <= 1:
                pl.waits("vector", otok[nc_ % 2])
            v.tensor_tensor(orow[nc_ % 2][:, 512 * tt:512 * (tt + 1)], bank[:, :],
                            x2b[tt % 2][:, :], op=ALU.add)
            btok[(nc_ * ntt + tt) % 4] = pl.tok("vector")
            xtok[tt % 2] = pl.tok("vector")
        wtok[nc_ % 2] = pl.tok("tensor")
        pl.after("scalar", "vector")
        pl.dma("scalar", "wo_out" + str(nc_ % 2),
               out_dram[128 * nc_:128 * (nc_ + 1), 0:T], orow[nc_ % 2][:, 0:T])
        otok[nc_ % 2] = pl.stok("wo_out" + str(nc_ % 2))


def emit_l1(pl, io):
    pb = io["pb"]
    pl.pool()
    ln_fm_phase(pl, TFULL, io["x1_fm"], io["xn1_fm"], "A1", pb)
    pl.free_pool()
    pl.waits("sync", pl.tok("vector", "gpsimd", "scalar", "tensor") + pl.all_slots_tok())
    xn_res = load_resident_fm(pl, "xn1_res", io["xn1_fm"], [(0, TOWN)])
    pl.wait_slot("tensor", "xn1_res")
    rqc = pl.sb("rqc", [128, 2048], BF16, side="right")
    rqs = pl.sb("rqs", [128, 2048], BF16, side="right")
    pl.dma("sync", "ropeq", rqc[:], io["rqc"])
    pl.dma("sync", "ropeq", rqs[:], io["rqs"])
    pl.wait_slot("vector", "ropeq")
    pl.wait_slot("tensor", "ropeq")
    l1_qkv_q(pl, xn_res, io["wqkvp"], rqc, rqs, io["q1"], pb, TOWN,
             pl.sbufs["c_swap_bf"])
    pl.free("rqs", "rqc")
    l1_moe(pl, xn_res, io["WrT"], io["A2"], io["B2"], io["E_exp"],
           io["x1_fm"][:, 0:TOWN], io["x2p"], pb, TOWN)
    pl.free("xn1_res")
    pl.free("wgtB", "wgtA")
    rkc = pl.sb("rkc", [128, TFULL], BF16, side="right")
    rks = pl.sb("rks", [128, TFULL], BF16, side="right")
    pl.dma("sync", "ropek", rkc[:], io["rkc"])
    pl.dma("sync", "ropek", rks[:], io["rks"])
    pl.wait_slot("vector", "ropek")
    pl.wait_slot("tensor", "ropek")
    pl.waits("gpsimd", pl.tok("tensor", "vector", "scalar") + pl.all_slots_tok())
    l1_qkv_kv(pl, io["xn1_fm"], TFULL, io["wqkvp"], rkc, rks, io["k1"], io["v1t"],
              pb, pl.sbufs["c_swap_bf"])
    pl.free("rks", "rkc")
    pl.waits("sync", pl.tok("tensor", "vector", "scalar", "gpsimd") + pl.all_slots_tok())
    ao = l1_attn(pl, io["masks"], io["q1"], io["k1"], io["v1t"], TFULL, pb)
    for nme in ("l1rnr", "l1rnb", "l1neg", "l1qmax", "l1stat",
                "l1ub", "l1mA", "l1qB", "l1qA", "l1P", "l1V", "l1KB", "l1KA"):
        pl.free(nme)
    if "aodbg" in io:
        pl.after("sync", "vector")
        for ic in range(NCHUNK):
            pl.dma("sync", "aodbg", io["aodbg"][128 * ic:128 * (ic + 1), :],
                   ao[:, ic, :])
        pl.wait_slot("sync", "aodbg")
    l1_wo(pl, ao, io["woT"], io["x2p"], io["out"], pb, TOWN)


def declare_l1_io(nc, out_kind="ExternalOutput"):
    io = {}
    io["x1_fm"] = nc.dram_tensor("x1_fm", [D, TFULL], F32, kind="ExternalInput").ap()
    io["masks"] = nc.dram_tensor("masks", [4, 8, 128, 512], BF16, kind="ExternalInput").ap()
    io["wqkvp"] = nc.dram_tensor("wqkvp", [4096, 6144], F32, kind="ExternalInput").ap()
    io["woT"] = nc.dram_tensor("woT", [4096, 4096], F32, kind="ExternalInput").ap()
    io["A2"] = nc.dram_tensor("A2", [4096, 1024], F32, kind="ExternalInput").ap()
    io["B2"] = nc.dram_tensor("B2", [1024, 4096], F32, kind="ExternalInput").ap()
    io["WrT"] = nc.dram_tensor("WrT", [4096, 64], F32, kind="ExternalInput").ap()
    io["rqc"] = nc.dram_tensor("rqc", [128, 2048], BF16, kind="ExternalInput").ap()
    io["rqs"] = nc.dram_tensor("rqs", [128, 2048], BF16, kind="ExternalInput").ap()
    io["rkc"] = nc.dram_tensor("rkc", [128, TFULL], BF16, kind="ExternalInput").ap()
    io["rks"] = nc.dram_tensor("rks", [128, TFULL], BF16, kind="ExternalInput").ap()
    k_ = "ExternalOutput" if DEBUG_L1 else "Internal"
    io["xn1_fm"] = nc.dram_tensor("xn1_fm", [D, TFULL], BF16, kind=k_).ap()
    io["q1"] = nc.dram_tensor("q1", [4096, TOWN], BF16, kind=k_).ap()
    io["k1"] = nc.dram_tensor("k1", [1024, TFULL], BF16, kind=k_).ap()
    io["v1t"] = nc.dram_tensor("v1t", [TFULL, 1024], BF16, kind=k_).ap()
    io["x2p"] = nc.dram_tensor("x2p", [D, TOWN], F32, kind=k_).ap()
    return io


def build_l1_program():
    nc = bass.Bass("TRN2", target_bir_lowering=False)
    stack = ExitStack()
    io = declare_l1_io(nc)
    io["out"] = nc.dram_tensor("out", [D, TOWN], F32, kind="ExternalOutput").ap()
    if DEBUG_L1:
        io["aodbg"] = nc.dram_tensor("aodbg", [4096, TOWN], BF16, kind="ExternalOutput").ap()
    with stack:
        pl = Planner(nc, stack)
        io["E_exp"] = declare_and_load_consts(pl, need_E=True, need_swap=True)["E_exp"]
        io["pb"] = [pl.ps(f"pb{i}", [128, 512], F32) for i in range(8)]
        for e in ("tensor", "vector", "scalar"):
            pl.wait_slot(e, "consts")
        emit_l1(pl, io)
        pl.fence_all()
        pl.after("sync", "scalar", "gpsimd", "vector", "tensor")
        counts = pl.finalize()
    return nc, counts


def prep_l1_shared(w_qkv, w_o, A, B, Wr, consts):
    perm = []
    for hq in range(32):
        base = 128 * hq
        perm.extend(range(base, base + 128, 2))
        perm.extend(range(base + 1, base + 128, 2))
    for kh in range(8):
        base = 4096 + 128 * kh
        perm.extend(range(base, base + 128, 2))
        perm.extend(range(base + 1, base + 128, 2))
    perm.extend(range(5120, 6144))
    perm = np.array(perm)
    shared = {
        "wqkvp": np.ascontiguousarray(np.asarray(w_qkv)[:, perm]),
        "woT": np.ascontiguousarray(np.asarray(w_o)),
        "A2": np.ascontiguousarray(np.asarray(A).transpose(2, 0, 1).reshape(4096, 1024)),
        "B2": np.ascontiguousarray(np.asarray(B).transpose(0, 2, 1).reshape(1024, 4096)),
        "WrT": np.ascontiguousarray(np.asarray(Wr).T),
    }
    shared.update(host_const_inputs(consts, need_E=True, need_swap=True))
    return shared


def prep_l1_core_inputs(h, x1_full_reordered, shared):
    m = dict(shared)
    m["x1_fm"] = np.ascontiguousarray(x1_full_reordered.T)
    m["masks"] = host_masks(h)
    qc, qs, kc, ks = host_rope(h)
    m["rqc"], m["rqs"], m["rkc"], m["rks"] = qc, qs, kc, ks
    return m


def host_masks_l1(h):
    qcols = host_col_order(h)[:2048]
    s_abs = []
    for t0 in [512 * (2 * i + 1) for i in range(4)]:
        s_abs.extend(range(t0, t0 + 512))
    for t0 in [1024 * i for i in range(4)]:
        s_abs.extend(range(t0, t0 + 512))
    s_abs = np.array(s_abs)
    M = np.zeros((4, 8, 128, 512), dtype=np.float32)
    for i in range(4):
        aq = qcols[512 * i:512 * (i + 1)]
        blocks, win = tile_blocks(i)
        for j, blk in enumerate(win):
            as_ = s_abs[128 * blk:128 * (blk + 1)]
            M[i, j] = (as_[:, None] <= aq[None, :])
    return M.astype(ml_dtypes.bfloat16)


def host_rope_k_rank():
    hd = 128
    inv = (1.0 / (10000.0 ** (np.arange(0, hd, 2, dtype=np.float32) / hd))).astype(np.float32)
    s_abs = []
    for t0 in [512 * (2 * i + 1) for i in range(4)]:
        s_abs.extend(range(t0, t0 + 512))
    for t0 in [1024 * i for i in range(4)]:
        s_abs.extend(range(t0, t0 + 512))
    cols = np.array(s_abs, dtype=np.float32)
    ang = cols[None, :] * inv[:, None]
    cos = np.cos(ang.astype(np.float64)).astype(np.float32)
    sin = np.sin(ang.astype(np.float64)).astype(np.float32)
    return (np.concatenate([cos, cos], 0).astype(ml_dtypes.bfloat16),
            np.concatenate([-sin, sin], 0).astype(ml_dtypes.bfloat16))


def build_fused_program():
    nc = bass.Bass("TRN2", target_bir_lowering=False)
    stack = ExitStack()
    io = declare_l0_io(nc)
    io["x1"] = nc.dram_tensor("x1", [D, TOWN], F32, kind="Internal").ap()
    io1 = {}
    io1["masksL1"] = nc.dram_tensor("masksL1", [4, 8, 128, 512], BF16, kind="ExternalInput").ap()
    io1["wqkvp"] = nc.dram_tensor("wqkvp", [4096, 6144], F32, kind="ExternalInput").ap()
    io1["woT"] = nc.dram_tensor("woT", [4096, 4096], F32, kind="ExternalInput").ap()
    io1["A2"] = nc.dram_tensor("A2", [4096, 1024], F32, kind="ExternalInput").ap()
    io1["B2"] = nc.dram_tensor("B2", [1024, 4096], F32, kind="ExternalInput").ap()
    io1["WrT"] = nc.dram_tensor("WrT", [4096, 64], F32, kind="ExternalInput").ap()
    io1["rqc"] = nc.dram_tensor("rqc", [128, 2048], BF16, kind="ExternalInput").ap()
    io1["rqs"] = nc.dram_tensor("rqs", [128, 2048], BF16, kind="ExternalInput").ap()
    io1["rkc"] = nc.dram_tensor("rkc", [128, TFULL], BF16, kind="ExternalInput").ap()
    io1["rks"] = nc.dram_tensor("rks", [128, TFULL], BF16, kind="ExternalInput").ap()
    io1["xn1_own"] = nc.dram_tensor("xn1_own", [D, TOWN], BF16, kind="Internal").ap()
    io1["xn1_full"] = nc.dram_tensor("xn1_full", [2 * D, TOWN], BF16, kind="Internal").ap()
    io1["q1"] = nc.dram_tensor("q1", [4096, TOWN], BF16, kind="Internal").ap()
    io1["k1"] = nc.dram_tensor("k1", [1024, TFULL], BF16, kind="Internal").ap()
    io1["v1t"] = nc.dram_tensor("v1t", [TFULL, 1024], BF16, kind="Internal").ap()
    io1["x2p"] = nc.dram_tensor("x2p", [D, TOWN], F32, kind="Internal").ap()
    io1["out"] = nc.dram_tensor("out", [D, TOWN], F32, kind="ExternalOutput").ap()

    with stack:
        pl = Planner(nc, stack)
        cc_aps = declare_and_load_consts(pl, need_E=True, need_swap=True)
        io1["E_exp"] = cc_aps["E_exp"]
        pb = [pl.ps(f"pb{i}", [128, 512], F32) for i in range(8)]
        io["pb"] = pb
        io1["pb"] = pb
        for e in ("tensor", "vector", "scalar"):
            pl.wait_slot(e, "consts")
        emit_l0(pl, io)
        for nme in ("l0oB", "l0oA", "l0x1B", "l0x1A", "l0mB", "l0mA",
                    "l0qB", "l0qA", "l0P", "l0V", "l0K"):
            pl.free(nme)
        pl.pool()
        ln_fm_phase(pl, TOWN, io["x1"], io1["xn1_own"], "A1", pb)
        pl.free_pool()
        pl.fence("scalar")
        pl.waits("gpsimd", pl.tok("scalar"))
        pl.g.collective_compute(
            "AllGather", ALU.bypass,
            replica_groups=[[0, 1], [2, 3], [4, 5], [6, 7]],
            ins=[io1["xn1_own"].opt()], outs=[io1["xn1_full"].opt()])
        cc_count = pl.cc_sem.count
        pl.waits("sync", pl.tok("vector", "gpsimd", "scalar", "tensor") + pl.all_slots_tok())
        xn_res = load_resident_fm(pl, "xn1_res", io1["xn1_own"], [(0, TOWN)])
        pl.wait_slot("tensor", "xn1_res")
        rqc = pl.sb("rqc", [128, 2048], BF16, side="right")
        rqs = pl.sb("rqs", [128, 2048], BF16, side="right")
        pl.dma("sync", "ropeq", rqc[:], io1["rqc"])
        pl.dma("sync", "ropeq", rqs[:], io1["rqs"])
        pl.wait_slot("vector", "ropeq")
        pl.wait_slot("tensor", "ropeq")
        l1_qkv_q(pl, xn_res, io1["wqkvp"], rqc, rqs, io1["q1"], pb, TOWN,
                 pl.sbufs["c_swap_bf"])
        pl.free("rqs", "rqc")
        l1_moe(pl, xn_res, io1["WrT"], io1["A2"], io1["B2"], io1["E_exp"],
               io["x1"], io1["x2p"], pb, TOWN)
        pl.free("xn1_res")
        pl.free("wgtB", "wgtA")
        rkc = pl.sb("rkc", [128, TFULL], BF16, side="right")
        rks = pl.sb("rks", [128, TFULL], BF16, side="right")
        pl.dma("sync", "ropek", rkc[:], io1["rkc"])
        pl.dma("sync", "ropek", rks[:], io1["rks"])
        pl.wait_slot("vector", "ropek")
        pl.wait_slot("tensor", "ropek")
        pl.waits("gpsimd", pl.tok("tensor", "vector", "scalar") + pl.all_slots_tok())
        pl.eng["sync"].wait_ge_(pl.cc_sem, cc_count)
        pl.eng["gpsimd"].wait_ge_(pl.cc_sem, cc_count)
        l1_qkv_kv(pl, io1["xn1_full"], TFULL, io1["wqkvp"], rkc, rks,
                  io1["k1"], io1["v1t"], pb, pl.sbufs["c_swap_bf"],
                  rank_order=True)
        pl.free("rks", "rkc")
        pl.waits("sync", pl.tok("tensor", "vector", "scalar", "gpsimd") + pl.all_slots_tok())
        ao = l1_attn(pl, io1["masksL1"], io1["q1"], io1["k1"], io1["v1t"], TFULL, pb)
        for nme in ("l1rnr", "l1rnb", "l1neg", "l1qmax", "l1stat",
                    "l1ub", "l1mA", "l1qB", "l1qA", "l1P", "l1V", "l1KB", "l1KA"):
            pl.free(nme)
        l1_wo(pl, ao, io1["woT"], io1["x2p"], io1["out"], pb, TOWN)
        pl.fence_all()
        pl.after("sync", "scalar", "gpsimd", "vector", "tensor")
        counts = pl.finalize()
    return nc, counts


def prep_fused_inputs(inputs_dict, consts):
    x = np.asarray(inputs_dict["L0_attn_x"], dtype=np.float32)
    maps0 = prep_l0_inputs(x, inputs_dict["L0_attn_bv_q"], inputs_dict["L0_attn_bv_k"],
                           inputs_dict["L0_attn_bv_v"], inputs_dict["L0_ffn_w12"],
                           inputs_dict["L0_ffn_b12"], inputs_dict["L0_ffn_w3"],
                           inputs_dict["L0_ffn_b3"], consts)
    shared1 = prep_l1_shared(inputs_dict["L1_attn_w_qkv"], inputs_dict["L1_attn_w_o"],
                             inputs_dict["L1_ffn_A"], inputs_dict["L1_ffn_B"],
                             inputs_dict["L1_ffn_Wr"], consts)
    rkc, rks = host_rope_k_rank()
    per_core = []
    for c in range(8):
        h = c % 2
        m = dict(maps0[c])
        for k in ("wqkvp", "woT", "A2", "B2", "WrT", "E_exp", "swap_bf"):
            m[k] = shared1[k]
        qc, qs, _, _ = host_rope(h)
        m["rqc"], m["rqs"] = qc, qs
        m["rkc"], m["rks"] = rkc, rks
        m["masksL1"] = host_masks_l1(h)
        per_core.append(m)
    return per_core


from concourse.bass_utils import run_bass_kernel_spmd

_CACHE = {}


def _program():
    if "fused" not in _CACHE:
        _CACHE["fused"] = build_fused_program()
        _CACHE["consts"] = host_constants()
    return _CACHE


def kernel(L0_attn_x, L0_attn_bv_q, L0_attn_bv_k, L0_attn_bv_v,
           L0_ffn_w12, L0_ffn_b12, L0_ffn_w3, L0_ffn_b3,
           L1_attn_w_qkv, L1_attn_w_o, L1_ffn_A, L1_ffn_B, L1_ffn_Wr):
    P = _program()
    nc, _ = P["fused"]
    consts = P["consts"]
    inputs = dict(L0_attn_x=L0_attn_x, L0_attn_bv_q=L0_attn_bv_q,
                  L0_attn_bv_k=L0_attn_bv_k, L0_attn_bv_v=L0_attn_bv_v,
                  L0_ffn_w12=L0_ffn_w12, L0_ffn_b12=L0_ffn_b12,
                  L0_ffn_w3=L0_ffn_w3, L0_ffn_b3=L0_ffn_b3,
                  L1_attn_w_qkv=L1_attn_w_qkv, L1_attn_w_o=L1_attn_w_o,
                  L1_ffn_A=L1_ffn_A, L1_ffn_B=L1_ffn_B, L1_ffn_Wr=L1_ffn_Wr)
    in_maps = prep_fused_inputs(inputs, consts)
    res = run_bass_kernel_spmd(nc, in_maps, core_ids=list(range(8)))
    out = np.empty((4, TFULL, D), dtype=np.float32)
    for b in range(4):
        place_own_cols(out[b], res.results[2 * b]["out"].T, 0)
        place_own_cols(out[b], res.results[2 * b + 1]["out"].T, 1)
    return out


# version 19
# speedup vs baseline: 1.0832x; 1.0437x over previous
import sys
sys.path.insert(0, "/opt/trn_rl_repo")
from contextlib import ExitStack

import numpy as np
import ml_dtypes
import concourse.bass as bass
import concourse.mybir as mybir

F32 = mybir.dt.float32
BF16 = mybir.dt.bfloat16
AF = mybir.ActivationFunctionType
ALU = mybir.AluOpType
AX = mybir.AxisListType

D = 4096
NCHUNK = 32
TFULL = 4096
TOWN = 2048
EPS = 1e-5

ENGS = ("tensor", "vector", "scalar", "gpsimd", "sync")
DEBUG_X1P = False
DEBUG_L1 = False


class Sem:
    __slots__ = ("name", "count", "h")

    def __init__(self, name):
        self.name = name
        self.count = 0
        self.h = None


class Deferred:
    __slots__ = ("meth", "args", "kwargs", "incs")

    def __init__(self, meth, args, kwargs):
        self.meth = meth
        self.args = args
        self.kwargs = kwargs
        self.incs = []

    def then_inc(self, sem, n=1):
        sem.count += n
        self.incs.append((sem, n))
        return self

    def run(self, eng):
        args = [a.h if isinstance(a, Sem) else a for a in self.args]
        kwargs = {k: (v.h if isinstance(v, Sem) else v) for k, v in self.kwargs.items()}
        instr = getattr(eng, self.meth)(*args, **kwargs)
        for sem, n in self.incs:
            instr.then_inc(sem.h, n)
        return instr


class EngPlan:
    def __init__(self, pl, name):
        self._pl = pl
        self._name = name

    def _rec(self, meth, args, kwargs, auto=True):
        pl, name = self._pl, self._name
        if (name in ("vector", "scalar", "gpsimd")
                and meth not in ("wait_ge", "nop")):
            cnt = pl.prog[name].count
            if cnt > 0 and pl._selfwait.get(name, 0) < cnt:
                w = Deferred("wait_ge", (pl.prog[name], cnt), {})
                pl.ops[name].append(w)
                pl._selfwait[name] = cnt
        d = Deferred(meth, args, kwargs)
        pl.ops[name].append(d)
        if meth in ("dma_start", "dma_start_transpose"):
            d.then_inc(pl.dmas[name], 16)
        elif meth == "collective_compute":
            d.then_inc(pl.cc_sem, 1)
        elif auto and meth not in ("wait_ge", "nop"):
            d.then_inc(pl.prog[name], 1)
        return d

    def __getattr__(self, meth):
        def rec(*a, auto=True, **k):
            return self._rec(meth, a, k, auto=auto)

        return rec

    def wait_ge_(self, sem, val):
        d = Deferred("wait_ge", (sem, int(val)), {})
        self._pl.ops[self._name].append(d)
        return d


class Planner:
    def __init__(self, nc, stack):
        self.nc = nc
        self.stack = stack
        self.ops = {e: [] for e in ENGS}
        self.prog = {e: self._mksem(f"prog_{e}") for e in ENGS}
        self.dmas = {e: self._mksem(f"dmas_{e}") for e in ENGS}
        self.qf = {e: self._mksem(f"qf_{e}") for e in ENGS}
        self._fenced = {e: ({}, 0) for e in ENGS}
        self.cc_sem = self._mksem("cc_done")
        self.t = EngPlan(self, "tensor")
        self.v = EngPlan(self, "vector")
        self.s = EngPlan(self, "scalar")
        self.g = EngPlan(self, "gpsimd")
        self.q = EngPlan(self, "sync")
        self.eng = {"tensor": self.t, "vector": self.v, "scalar": self.s,
                    "gpsimd": self.g, "sync": self.q}
        self.sbufs = {}
        self._ctxs = {}
        self._semcache = {}
        self._qslots = {}
        self._selfwait = {}

    def _mksem(self, name):
        s = Sem(name)
        s.h = self.stack.enter_context(self.nc.semaphore(name))
        return s

    def sem(self, name):
        if name not in self._semcache:
            self._semcache[name] = self._mksem(name)
        return self._semcache[name]

    def sb(self, name, shape, dtype, side=None):
        if name not in self.sbufs:
            kw = {"allow_name_mangling": True}
            if side is not None:
                kw["side"] = side
            cm = self.nc.sbuf_tensor(name, shape, dtype, **kw)
            self.sbufs[name] = cm.__enter__()
            self._ctxs[name] = cm
        return self.sbufs[name]

    def ps(self, name, shape, dtype):
        if name not in self.sbufs:
            cm = self.nc.psum_tensor(name, shape, dtype)
            self.sbufs[name] = cm.__enter__()
            self._ctxs[name] = cm
        return self.sbufs[name]

    def free(self, *names):
        for name in names:
            cm = self._ctxs.pop(name)
            cm.__exit__(None, None, None)
            del self.sbufs[name]

    POOL_BIG = ("sA", "sB", "sC", "sD", "rspl", "nspl", "bA", "bB", "bC", "bD")

    def pool(self, rows=True):
        p = {}
        names = ("sA", "sB", "sC", "sD", "rspl", "nspl") if rows else \
                ("sA", "sB", "sC", "sD")
        self.POOL_BIG = list(names) + ["bA", "bB", "bC", "bD"]
        for n in names:
            p[n] = self.sb(n, [128, 4096], F32, side="right")
        for n in ("bA", "bB", "bC", "bD"):
            p[n] = self.sb(n, [128, 4096], BF16, side="right")
        for n in ("cA", "cB", "cC", "cD"):
            if n not in self.sbufs:
                p[n] = self.sb(n, [128, 128], F32)
                self.g.memset(p[n][:], 0.0)
        return p

    def free_pool(self):
        for n in reversed(self.POOL_BIG):
            self.free(n)

    def tok(self, *engs):
        out = []
        for e in engs:
            if self.prog[e].count > 0:
                out.append((self.prog[e], self.prog[e].count))
            if self.qf[e].count > 0:
                out.append((self.qf[e], self.qf[e].count))
        return out

    def all_slots_tok(self):
        out = []
        for name, sem in self._semcache.items():
            if name.startswith("slot_") and sem.count > 0:
                out.append((sem, sem.count))
        return out

    def stok(self, *slotnames):
        out = []
        for n in slotnames:
            sl = self.slot(n)
            if sl.count > 0:
                out.append((sl, sl.count))
        return out

    def waits(self, consumer, *toks):
        c = self.eng[consumer]
        for tk in toks:
            if tk is None:
                continue
            for sem, val in tk:
                if val > 0:
                    c.wait_ge_(sem, val)

    def after(self, consumer, *producers):
        self.waits(consumer, self.tok(*producers))

    def slot(self, name):
        return self.sem("slot_" + name)

    def dma(self, queue, slotname, out, in_, **kw):
        sl = self.slot(slotname)
        d = self.eng[queue]._rec("dma_start", (), dict(out=out, in_=in_, **kw),
                                auto=False)
        d.incs.clear()
        self.dmas[queue].count -= 16
        d.then_inc(sl, 16)
        self._qslots.setdefault(queue, {})[slotname] = True
        return d

    def wait_slot(self, consumer, slotname, val=None):
        sl = self.slot(slotname)
        v = sl.count if val is None else val
        if v > 0:
            self.eng[consumer].wait_ge_(sl, v)

    def fence(self, queue):
        dm = self.dmas[queue]
        prev_slots, prev_dm = self._fenced[queue]
        last = None
        if dm.count > prev_dm:
            last = self.eng[queue].wait_ge_(dm, dm.count)
        new_slots = dict(prev_slots)
        for slotname in self._qslots.get(queue, {}):
            sl = self.slot(slotname)
            if sl.count > prev_slots.get(slotname, 0):
                last = self.eng[queue].wait_ge_(sl, sl.count)
                new_slots[slotname] = sl.count
        if last is not None:
            last.then_inc(self.qf[queue], 1)
            self._fenced[queue] = (new_slots, dm.count)

    def fence_all(self, queues=("sync", "scalar", "gpsimd", "vector")):
        for qn in queues:
            self.fence(qn)

    def tmark(self):
        self.ops["tensor"][-1].then_inc(self.prog["tensor"], 1)

    def finalize(self):
        nc = self.nc
        with nc.Block() as block:
            for name, deco in (("tensor", block.tensor), ("vector", block.vector),
                               ("scalar", block.scalar), ("gpsimd", block.gpsimd),
                               ("sync", block.sync)):
                ops = self.ops[name]
                if not ops:
                    continue

                def emit(eng, ops=ops):
                    for d in ops:
                        d.run(eng)

                deco(emit)
        return {e: len(self.ops[e]) for e in ENGS}


def host_constants():
    c = {}
    c["ones_bf_col"] = np.ones((128, 1), dtype=np.float32).astype(ml_dtypes.bfloat16)
    c["ones_f32_row"] = np.ones((1, 128), dtype=np.float32)
    c["ident_bf"] = np.eye(128, dtype=np.float32).astype(ml_dtypes.bfloat16)
    c["ident_f32"] = np.eye(128, dtype=np.float32)
    sw = np.zeros((128, 128), dtype=np.float32)
    for i_ in range(128):
        sw[i_, (i_ + 64) % 128] = 1.0
    c["swap_bf"] = sw.astype(ml_dtypes.bfloat16)
    E = np.zeros((64, 1024), dtype=np.float32)
    for k in range(64):
        E[k, k * 16:(k + 1) * 16] = 1.0
    c["E_exp"] = E
    return c


CONST_SPECS = [
    ("ones_bf_col", [128, 1], BF16),
    ("ones_f32_row", [1, 128], F32),
    ("ident_bf", [128, 128], BF16),
    ("ident_f32", [128, 128], F32),
    ("E_exp", [64, 1024], F32),
    ("swap_bf", [128, 128], BF16),
]


def declare_and_load_consts(pl, need_E=False, need_swap=False):
    nc = pl.nc
    aps = {}
    for name, shape, dt in CONST_SPECS:
        if name == "E_exp" and not need_E:
            continue
        if name == "swap_bf" and not need_swap:
            continue
        aps[name] = nc.dram_tensor(name, shape, dt, kind="ExternalInput").ap()
    pl.sb("c_ones_bf_col", [128, 1], BF16)
    pl.sb("c_ones_f32_row", [1, 128], F32)
    pl.sb("c_ident_bf", [128, 128], BF16)
    pl.sb("c_ident_f32", [128, 128], F32)
    pl.dma("sync", "consts", pl.sbufs["c_ones_bf_col"][:], aps["ones_bf_col"])
    pl.dma("sync", "consts", pl.sbufs["c_ones_f32_row"][:], aps["ones_f32_row"])
    pl.dma("sync", "consts", pl.sbufs["c_ident_bf"][:], aps["ident_bf"])
    pl.dma("sync", "consts", pl.sbufs["c_ident_f32"][:], aps["ident_f32"])
    if need_swap:
        pl.sb("c_swap_bf", [128, 128], BF16)
        pl.dma("sync", "consts", pl.sbufs["c_swap_bf"][:], aps["swap_bf"])
    eps_col = pl.sb("c_eps_col", [128, 1], F32)
    pl.g.memset(eps_col[:], EPS)
    onesf32c = pl.sb("c_ones_f32_col", [128, 1], F32)
    pl.g.memset(onesf32c[:], 1.0)
    onesbfr = pl.sb("c_ones_bf_row", [1, 128], BF16)
    pl.g.memset(onesbfr[:], 1.0)
    return aps


def host_const_inputs(consts, need_E=False, need_swap=False):
    d = {}
    for name, _, _ in CONST_SPECS:
        if name == "E_exp" and not need_E:
            continue
        if name == "swap_bf" and not need_swap:
            continue
        d[name] = consts[name]
    return d


def host_col_order(h):
    own = [512 * (2 * i + 1) if h == 0 else 1024 * i for i in range(4)]
    sib = [1024 * i if h == 0 else 512 * (2 * i + 1) for i in range(4)]
    cols = []
    for t0 in own + sib:
        cols.extend(range(t0, t0 + 512))
    return np.array(cols)


def tile_blocks(i):
    own = list(range(0, 4 * i + 4))
    sib = list(range(16, 16 + 4 * i + 4))
    return own + sib, own[-4:] + sib[-4:]


def host_masks(h):
    cols = host_col_order(h)
    M = np.zeros((4, 8, 128, 512), dtype=np.float32)
    for i in range(4):
        aq = cols[512 * i:512 * (i + 1)]
        blocks, win = tile_blocks(i)
        for j, blk in enumerate(win):
            as_ = cols[128 * blk:128 * (blk + 1)]
            M[i, j] = (as_[:, None] <= aq[None, :])
    return M.astype(ml_dtypes.bfloat16)


def host_rope(h):
    hd = 128
    inv = (1.0 / (10000.0 ** (np.arange(0, hd, 2, dtype=np.float32) / hd))).astype(np.float32)
    cols = host_col_order(h).astype(np.float32)
    ang = cols[None, :] * inv[:, None]
    cos = np.cos(ang.astype(np.float64)).astype(np.float32)
    sin = np.sin(ang.astype(np.float64)).astype(np.float32)
    cosd = np.concatenate([cos, cos], axis=0)
    sinds = np.concatenate([-sin, sin], axis=0)
    return (cosd[:, 0:2048].astype(ml_dtypes.bfloat16),
            sinds[:, 0:2048].astype(ml_dtypes.bfloat16),
            cosd.astype(ml_dtypes.bfloat16),
            sinds.astype(ml_dtypes.bfloat16))



def ln_tm_alloc(pl):
    st = {}
    st["sA"] = pl.sb("sA", [128, 4096], F32, side="right")
    st["bA"] = pl.sb("bA", [128, 4096], BF16, side="right")
    st["snk"] = pl.sb("snk", [128, 2048], BF16, side="right")
    st["bD"] = pl.sb("bD", [128, 2048], BF16, side="right")
    for n in ("cA", "cB", "cC", "cD"):
        if n not in pl.sbufs:
            pl.sb(n, [128, 128], F32)
            pl.g.memset(pl.sbufs[n][:], 0.0)
    pl.after("scalar", "gpsimd")
    pl.after("vector", "gpsimd")
    st["cols"] = (pl.sbufs["cA"], pl.sbufs["cB"], pl.sbufs["cC"], pl.sbufs["cD"])
    st["use"] = [None, None]
    st["out"] = [None, None]
    st["ntok"] = [None, None]
    st["etok"] = [None, None]
    st["ftok"] = [None, None]
    st["tail"] = False
    return st


def ln_tm_own_extra(pl, st):
    st["sB"] = pl.sb("sB", [128, 4096], F32, side="right")
    st["bB"] = pl.sb("bB", [128, 4096], BF16, side="right")


def ln_tm_part1(pl, st, c, x_tm, xn_tm):
    t, v, s, g = pl.t, pl.v, pl.s, pl.g
    P = pl.sbufs
    col_sum, col_sq, col_r, col_nmr = st["cols"]
    qn = "sync" if c % 2 == 0 else "scalar"
    if st["tail"]:
        ui, oi = 0, 0
    else:
        ui, oi = c % 2, c % 2
    ldb = st[("sA", "sB")[ui]]
    slot = ("sA", "sB")[ui]
    tmb = st[("bA", "bB")[oi]]
    pl.waits(qn, st["use"][ui])
    pl.dma(qn, slot, ldb[:], x_tm[128 * c:128 * (c + 1), :])
    pl.wait_slot("vector", slot)
    v.reduce_sum(col_sum[:, c:c + 1], ldb[:], axis=AX.X)
    pl.wait_slot("scalar", slot)
    s.activation(out=st["snk"][:], in_=ldb[:, 0:2048], func=AF.Square,
                 accum_out=col_sq[:, c:c + 1])
    s.activation(out=st["snk"][:], in_=ldb[:, 2048:4096], func=AF.Square,
                 accum_out=col_r[:, c:c + 1])
    pl.after("vector", "scalar")
    v.tensor_tensor(col_sq[:, c:c + 1], col_sq[:, c:c + 1], col_r[:, c:c + 1], op=ALU.add)
    v.tensor_scalar_mul(col_sum[:, c:c + 1], col_sum[:, c:c + 1], 1.0 / D)
    v.tensor_scalar_mul(col_sq[:, c:c + 1], col_sq[:, c:c + 1], 1.0 / D)
    v.tensor_tensor(col_r[:, c:c + 1], col_sum[:, c:c + 1], col_sum[:, c:c + 1], op=ALU.mult)
    v.tensor_tensor(col_r[:, c:c + 1], col_sq[:, c:c + 1], col_r[:, c:c + 1], op=ALU.subtract)
    pl.after("scalar", "vector")
    s.activation(out=col_r[:, c:c + 1], in_=col_r[:, c:c + 1], func=AF.Ln,
                 bias=P["c_eps_col"][:])
    s.activation(out=col_r[:, c:c + 1], in_=col_r[:, c:c + 1], func=AF.Exp, scale=-0.5)
    pl.after("vector", "scalar")
    v.tensor_tensor(col_nmr[:, c:c + 1], col_sum[:, c:c + 1], col_r[:, c:c + 1], op=ALU.mult)
    v.tensor_scalar_mul(col_nmr[:, c:c + 1], col_nmr[:, c:c + 1], -1.0)
    pl.after("gpsimd", "vector", "scalar")
    pl.waits("gpsimd", st["out"][oi])
    g.tensor_scalar(out=tmb[:], in0=ldb[:], scalar1=col_r[:, c:c + 1],
                    scalar2=col_nmr[:, c:c + 1], op0=ALU.mult, op1=ALU.add)
    st["use"][ui] = pl.tok("gpsimd", "scalar")
    if st["tail"]:
        tq, tslot = "sync", "xntmT"
    else:
        tq, tslot = "gpsimd", "xntm" + str(oi)
    pl.after(tq, "gpsimd")
    pl.dma(tq, tslot, xn_tm[128 * c:128 * (c + 1), :], tmb[:])
    st["tslot"] = tslot
    st["ntok"][oi] = pl.tok("gpsimd")


def ln_tm_part2(pl, st, c, xn_fm):
    t, v, s, g = pl.t, pl.v, pl.s, pl.g
    P = pl.sbufs
    idb = P["c_ident_bf"]
    pb = st["pb"]
    qn = "sync" if c % 2 == 0 else "scalar"
    if st["tail"]:
        oi, ei = 0, 0
        bs = 4
    else:
        oi = c % 2
        ei = oi
        bs = 4 * oi
    tmb = st[("bA", "bB")[oi]]
    pl.waits("tensor", st["ntok"][oi], st["etok"][ei])
    for dc in range(32):
        bank = pb[bs + dc // 8]
        reg = bank[:, 64 * (dc % 8):64 * (dc % 8 + 1)].bitcast(BF16)
        t.transpose(reg, tmb[:, 128 * dc:128 * (dc + 1)], idb[:], auto=False)
    pl.tmark()
    ttok = pl.tok("tensor")
    st["out"][oi] = pl.stok(st["tslot"]) + ttok
    pl.waits("scalar", ttok)
    pl.waits("vector", ttok)
    if st["tail"]:
        dstf = xn_fm[:, 128 * c:128 * (c + 1)].rearrange("(dc p) tt -> p dc tt", p=128)
        for half in range(2):
            pl.waits("scalar", st["ftok"][1 - half])
            pl.waits("vector", st["ftok"][1 - half])
            for b in range(2):
                dst = st["bD"][:, 1024 * b:1024 * (b + 1)]
                srcb = pb[bs + 2 * half + b][:, :].bitcast(BF16)
                if b == 0:
                    s.copy(out=dst, in_=srcb)
                else:
                    v.tensor_copy(dst, srcb)
            st["etok"][0] = pl.tok("scalar", "vector")
            qn2 = "sync" if half == 0 else "scalar"
            pl.waits(qn2, st["etok"][0])
            pl.dma(qn2, "xfm" + str(half), dstf[:, 16 * half:16 * (half + 1), :],
                   st["bD"].rearrange("p (a b2) -> p a b2", b2=128))
            st["ftok"][half] = pl.stok("xfm" + str(half))
    else:
        for b in range(4):
            dst = st["xn_res"][:, 8 * b:8 * (b + 1), 128 * c:128 * (c + 1)]
            srcb = pb[bs + b][:, :].bitcast(BF16).rearrange("p (a b2) -> p a b2", b2=128)
            if b < 2:
                s.copy(out=dst, in_=srcb)
            else:
                v.tensor_copy(dst, srcb)
        st["etok"][ei] = pl.tok("scalar", "vector")


def ln_tm_own_stores(pl, st, xn_fm, nown):
    gtok = pl.tok("scalar", "vector")
    xr = st["xn_res"]
    for gg in range((nown + 7) // 8):
        w = min(1024, 128 * nown - 1024 * gg)
        dst = xn_fm[:, 1024 * gg:1024 * gg + w].rearrange("(dc p) tt -> p dc tt", p=128)
        pl.waits("sync", gtok)
        pl.dma("sync", "xfmA", dst[:, 0:16, :], xr[:, 0:16, 1024 * gg:1024 * gg + w])
        pl.waits("scalar", gtok)
        pl.dma("scalar", "xfmB", dst[:, 16:32, :], xr[:, 16:32, 1024 * gg:1024 * gg + w])
        st["ftok"][0] = pl.stok("xfmA")
        st["ftok"][1] = pl.stok("xfmB")


def ln_fm_v2(pl, T, x_fm, xn_own, xn_res, pbanks):
    t, v, s, g = pl.t, pl.v, pl.s, pl.g
    P = pl.sbufs
    ones_bf = P["c_ones_bf_col"]
    ones_f32 = P["c_ones_f32_row"]
    la = pl.sb("A1la", [128, T], F32, side="right")
    lb = pl.sb("A1lb", [128, T], F32, side="right")
    sq = pl.sb("A1sq", [128, T], BF16, side="right")
    rs = pl.sb("A1rs", [128, T], F32, side="right")
    ns = pl.sb("A1ns", [128, T], F32, side="right")
    tmp = [pl.sb("A1t0", [128, T], F32, side="right"),
           pl.sb("A1t1", [128, T], F32, side="right")]
    rows = pl.sb("A1rows", [1, 2 * T], F32, side="right")
    sum_row = rows[0:1, 0:T]
    sq_row = rows[0:1, T:2 * T]
    r_row = rs[0:1, 0:T]
    nmr_row = ns[0:1, 0:T]
    ntt = T // 512
    ldbs, slots = [la, lb], ["A1la", "A1lb"]

    mm_tok = [None, None]
    for c in range(NCHUNK):
        qn = "sync" if c % 2 == 0 else "scalar"
        pl.waits(qn, mm_tok[c % 2])
        pl.dma(qn, slots[c % 2], ldbs[c % 2][:], x_fm[128 * c:128 * (c + 1), :])
        pl.wait_slot("vector", slots[c % 2])
        v.tensor_copy(xn_res[:, c, :], ldbs[c % 2][:])
        pl.wait_slot("scalar", slots[c % 2])
        pl.after("scalar", "tensor")
        s.activation(out=sq[:], in_=ldbs[c % 2][:], func=AF.Square)
        pl.after("tensor", "vector", "scalar")
        for tt in range(ntt):
            t.matmul(pbanks[tt][0:1, :], ones_bf[:], xn_res[:, c, 512 * tt:512 * (tt + 1)],
                     start=(c == 0), stop=(c == NCHUNK - 1), auto=False)
            t.matmul(pbanks[4 + tt][0:1, :], ones_bf[:], sq[:, 512 * tt:512 * (tt + 1)],
                     start=(c == 0), stop=(c == NCHUNK - 1), auto=False)
        pl.tmark()
        mm_tok[c % 2] = pl.tok("tensor")

    pl.after("vector", "tensor")
    for tt in range(ntt):
        v.tensor_copy(sum_row[0:1, 512 * tt:512 * (tt + 1)], pbanks[tt][0:1, :])
        v.tensor_copy(sq_row[0:1, 512 * tt:512 * (tt + 1)], pbanks[4 + tt][0:1, :])
    v.tensor_scalar_mul(sum_row[:], sum_row[:], 1.0 / D)
    v.tensor_scalar_mul(sq_row[:], sq_row[:], 1.0 / D)
    v.tensor_tensor(r_row[:], sum_row[:], sum_row[:], op=ALU.mult)
    v.tensor_tensor(r_row[:], sq_row[:], r_row[:], op=ALU.subtract)
    pl.after("scalar", "vector")
    s.activation(out=r_row[:], in_=r_row[:], func=AF.Ln, bias=P["c_eps_col"][0:1, :])
    s.activation(out=r_row[:], in_=r_row[:], func=AF.Exp, scale=-0.5)
    pl.after("vector", "scalar")
    v.tensor_tensor(nmr_row[:], sum_row[:], r_row[:], op=ALU.mult)
    v.tensor_scalar_mul(nmr_row[:], nmr_row[:], -1.0)

    pl.after("tensor", "scalar", "vector")
    for tt in range(T // 512):
        bankA = pbanks[2 * (tt % 2)]
        bankB = pbanks[2 * (tt % 2) + 1]
        if tt >= 2:
            pl.after("tensor", "scalar")
        t.matmul(bankA[:, :], ones_f32[:], r_row[0:1, 512 * tt:512 * (tt + 1)],
                 start=True, stop=True, auto=False)
        t.matmul(bankB[:, :], ones_f32[:], nmr_row[0:1, 512 * tt:512 * (tt + 1)],
                 start=True, stop=True, auto=False)
        pl.tmark()
        pl.after("scalar", "tensor")
        s.copy(out=rs[:, 512 * tt:512 * (tt + 1)], in_=bankA[:, :])
        s.copy(out=ns[:, 512 * tt:512 * (tt + 1)], in_=bankB[:, :])
    spl_tok = pl.tok("scalar")

    cons_tok = [pl.tok("tensor"), pl.tok("tensor")]
    for c in range(NCHUNK):
        qn = "sync" if c % 2 == 0 else "scalar"
        pl.waits(qn, cons_tok[c % 2])
        pl.dma(qn, slots[c % 2], ldbs[c % 2][:], x_fm[128 * c:128 * (c + 1), :])
        e = "vector" if c % 2 == 0 else "gpsimd"
        ee = pl.eng[e]
        pl.wait_slot(e, slots[c % 2])
        pl.waits(e, spl_tok)
        ee.tensor_tensor(tmp[c % 2][:], ldbs[c % 2][:], rs[:, :], op=ALU.mult)
        ee.tensor_tensor(xn_res[:, c, :], tmp[c % 2][:], ns[:, :], op=ALU.add)
        cons_tok[c % 2] = pl.tok(e)
        pl.after("gpsimd", e)
        pl.dma("gpsimd", "A1xo" + str(c % 2),
               xn_own[128 * c:128 * (c + 1), :], xn_res[:, c, :])

    pl.free("A1rows", "A1t1", "A1t0", "A1ns", "A1rs", "A1sq", "A1lb", "A1la")


def ln_fm_phase(pl, T, x_fm, xn_fm, tag, pbanks):
    t, v, s, g, q = pl.t, pl.v, pl.s, pl.g, pl.q
    P = pl.sbufs
    ones_bf = P["c_ones_bf_col"]

    xb = [P["sA"], P["sB"]]
    slots = ["sA", "sB"]
    xbf = [P["bA"], P["bB"]]
    xsq = [P["bC"], P["bD"]]
    sum_row = P["rspl"][0:1, 0:T]
    sq_row = P["nspl"][0:1, 0:T]
    r_row = P["sC"][0:1, 0:T]
    nmr_row = P["sD"][0:1, 0:T]

    nhalf = (T + 2047) // 2048
    mm_tok = [None, None]
    for th in range(nhalf):
        W = min(2048, T - 2048 * th)
        ntt = W // 512
        for c in range(NCHUNK):
            it = th * NCHUNK + c
            qn = "sync" if it % 2 == 0 else "scalar"
            pl.waits(qn, mm_tok[it % 2])
            pl.dma(qn, slots[it % 2], xb[it % 2][:, 0:W],
                   x_fm[128 * c:128 * (c + 1), 2048 * th:2048 * th + W])
            pl.wait_slot("vector", slots[it % 2])
            v.tensor_copy(xbf[it % 2][:, 0:W], xb[it % 2][:, 0:W])
            pl.wait_slot("scalar", slots[it % 2])
            s.activation(out=xsq[it % 2][:, 0:W], in_=xb[it % 2][:, 0:W], func=AF.Square)
            pl.after("tensor", "vector", "scalar")
            for tt in range(ntt):
                t.matmul(pbanks[tt][0:1, :], ones_bf[:], xbf[it % 2][:, 512 * tt:512 * (tt + 1)],
                         start=(c == 0), stop=(c == NCHUNK - 1), auto=False)
                t.matmul(pbanks[4 + tt][0:1, :], ones_bf[:], xsq[it % 2][:, 512 * tt:512 * (tt + 1)],
                         start=(c == 0), stop=(c == NCHUNK - 1), auto=False)
            pl.tmark()
            mm_tok[it % 2] = pl.tok("tensor")
        base = 2048 * th
        pl.after("vector", "tensor")
        for tt in range(ntt):
            v.tensor_copy(sum_row[0:1, base + 512 * tt:base + 512 * (tt + 1)], pbanks[tt][0:1, :])
            v.tensor_copy(sq_row[0:1, base + 512 * tt:base + 512 * (tt + 1)], pbanks[4 + tt][0:1, :])

    v.tensor_scalar_mul(sum_row[:], sum_row[:], 1.0 / D)
    v.tensor_scalar_mul(sq_row[:], sq_row[:], 1.0 / D)
    v.tensor_tensor(r_row[:], sum_row[:], sum_row[:], op=ALU.mult)
    v.tensor_tensor(r_row[:], sq_row[:], r_row[:], op=ALU.subtract)
    pl.after("scalar", "vector")
    s.activation(out=r_row[:], in_=r_row[:], func=AF.Ln, bias=P["c_eps_col"][0:1, :])
    s.activation(out=r_row[:], in_=r_row[:], func=AF.Exp, scale=-0.5)
    pl.after("vector", "scalar")
    v.tensor_tensor(nmr_row[:], sum_row[:], r_row[:], op=ALU.mult)
    v.tensor_scalar_mul(nmr_row[:], nmr_row[:], -1.0)

    _fm_normalize(pl, T, x_fm, xn_fm, tag, pbanks, rows_sbuf=(r_row, nmr_row))


def _fm_normalize(pl, T, x_fm, xn_fm, tag, pbanks, rows_dram=None, rows_sbuf=None):
    t, v, s, g, q = pl.t, pl.v, pl.s, pl.g, pl.q
    P = pl.sbufs
    ones_f32 = P["c_ones_f32_row"]

    if rows_dram is not None:
        rr = P["sC"][0:1, 0:T]
        nn = P["sD"][0:1, 0:T]
        pl.wait_slot("sync", "rowsw")
        pl.dma("sync", "rowr", rr, rows_dram[0])
        pl.dma("sync", "rown", nn, rows_dram[1])
        pl.wait_slot("tensor", "rowr")
        pl.wait_slot("tensor", "rown")
    else:
        rr, nn = rows_sbuf
        pl.after("tensor", "vector", "scalar")

    r_spl = P["rspl"][:, 0:T]
    n_spl = P["nspl"][:, 0:T]
    xch = [P["sA"], P["sB"]]
    slots = [tag + "nA", tag + "nB"]
    tmp = [P["sC"], P["sD"]]
    xnb = [P["bA"], P["bB"]]

    for tt in range(T // 512):
        bankA = pbanks[2 * (tt % 2)]
        bankB = pbanks[2 * (tt % 2) + 1]
        if tt >= 2:
            pl.after("tensor", "scalar")
        t.matmul(bankA[:, :], ones_f32[:], rr[0:1, 512 * tt:512 * (tt + 1)],
                 start=True, stop=True, auto=False)
        t.matmul(bankB[:, :], ones_f32[:], nn[0:1, 512 * tt:512 * (tt + 1)],
                 start=True, stop=True, auto=False)
        pl.tmark()
        pl.after("scalar", "tensor")
        s.copy(out=r_spl[:, 512 * tt:512 * (tt + 1)], in_=bankA[:, :])
        s.copy(out=n_spl[:, 512 * tt:512 * (tt + 1)], in_=bankB[:, :])

    spl_tok = pl.tok("scalar")
    pre = pl.tok("vector", "scalar", "gpsimd", "tensor") + pl.all_slots_tok()
    cons_tok = [pre, pre]
    st_tok = [None, None]
    for c in range(NCHUNK):
        qn = "sync" if c % 2 == 0 else "scalar"
        pl.waits(qn, cons_tok[c % 2])
        pl.dma(qn, slots[c % 2], xch[c % 2][:, 0:T], x_fm[128 * c:128 * (c + 1), :])
        e = "vector" if c % 2 == 0 else "gpsimd"
        ee = pl.eng[e]
        pl.wait_slot(e, slots[c % 2])
        pl.waits(e, spl_tok, st_tok[c % 2])
        ee.tensor_tensor(tmp[c % 2][:, 0:T], xch[c % 2][:, 0:T], r_spl, op=ALU.mult)
        ee.tensor_tensor(xnb[c % 2][:, 0:T], tmp[c % 2][:, 0:T], n_spl, op=ALU.add)
        cons_tok[c % 2] = pl.tok(e)
        pl.after("gpsimd", e)
        pl.dma("gpsimd", tag + "xo" + str(c % 2),
               xn_fm[128 * c:128 * (c + 1), :], xnb[c % 2][:, 0:T])
        st_tok[c % 2] = pl.stok(tag + "xo" + str(c % 2))


def stream_w_oc(pl, w_dram, n_ic, col0, width, wgt_sb, slotname, war_tok=None,
                queue="gpsimd"):
    view = w_dram[:, col0:col0 + width].rearrange("(ic p) j -> p ic j", p=128)
    half = n_ic // 2 if n_ic >= 2 else n_ic
    pl.waits(queue, war_tok)
    pl.dma(queue, slotname, wgt_sb[:, 0:half, 0:width], view[:, 0:half, :])
    if half < n_ic:
        pl.dma(queue, slotname, wgt_sb[:, half:n_ic, 0:width], view[:, half:n_ic, :])


def load_resident_fm(pl, name, src_fm, ranges, nic=NCHUNK, queue="sync"):
    Ttot = sum(a1 - a0 for a0, a1 in ranges)
    res = pl.sb(name, [128, nic, Ttot], BF16, side="right")
    for ic in range(nic):
        o = 0
        for a0, a1 in ranges:
            pl.dma(queue, name, res[:, ic, o:o + (a1 - a0)],
                   src_fm[128 * ic:128 * (ic + 1), a0:a1])
            o += a1 - a0
    return res


def l0_ffn_c1(pl, xn_res, w12T, b1cols, b2cols, x4_dram, T, pbanks, hook=None):
    t, v, s, g = pl.t, pl.v, pl.s, pl.g
    wgtG = [pl.sb("wgtA", [128, NCHUNK, 128], BF16), pl.sb("wgtB", [128, NCHUNK, 128], BF16)]
    wgtU = [pl.sb("wgtC", [128, NCHUNK, 128], BF16), pl.sb("wgtD", [128, NCHUNK, 128], BF16)]
    silu_sb = pl.sb("silu_sb", [128, 512], F32)
    x4row = [pl.sb("x4rowA", [128, 2048], BF16), pl.sb("x4rowB", [128, 2048], BF16)]
    ntt = T // 512

    entry = pl.tok("tensor", "vector", "scalar", "gpsimd") + pl.all_slots_tok()
    wtok = [entry, entry]
    btok = [entry] * 4
    rtok = [entry, entry]
    stok = entry
    for a in range(32):
        stream_w_oc(pl, w12T, NCHUNK, a * 128, 128, wgtG[a % 2], "wgtG" + str(a % 2),
                    war_tok=wtok[a % 2])
        stream_w_oc(pl, w12T, NCHUNK, 4096 + a * 128, 128, wgtU[a % 2], "wgtU" + str(a % 2),
                    war_tok=wtok[a % 2])
        if hook is not None:
            hook(a)
        pl.wait_slot("tensor", "wgtG" + str(a % 2))
        pl.wait_slot("tensor", "wgtU" + str(a % 2))
        for tt in range(ntt):
            bg = pbanks[2 * (tt % 2)]
            bu = pbanks[2 * (tt % 2) + 1]
            pl.waits("tensor", btok[2 * (tt % 2)], btok[2 * (tt % 2) + 1])
            for ic in range(NCHUNK):
                t.matmul(bg[:, :], wgtG[a % 2][:, ic, :], xn_res[:, ic, 512 * tt:512 * (tt + 1)],
                         start=(ic == 0), stop=(ic == NCHUNK - 1), auto=False)
            pl.tmark()
            for ic in range(NCHUNK):
                t.matmul(bu[:, :], wgtU[a % 2][:, ic, :], xn_res[:, ic, 512 * tt:512 * (tt + 1)],
                         start=(ic == 0), stop=(ic == NCHUNK - 1), auto=False)
            pl.tmark()
            mm_tok = pl.tok("tensor")
            pl.waits("scalar", mm_tok, stok)
            s.activation(out=silu_sb[:, :], in_=bg[:, :], func=AF.Silu,
                         bias=b1cols[:, a:a + 1])
            pl.waits("vector", mm_tok)
            pl.after("vector", "scalar")
            if tt == 0:
                pl.waits("vector", rtok[a % 2])
            v.scalar_tensor_tensor(out=x4row[a % 2][:, 512 * tt:512 * (tt + 1)],
                                   in0=bu[:, :], scalar=b2cols[:, a:a + 1],
                                   in1=silu_sb[:, :], op0=ALU.add, op1=ALU.mult)
            btok[2 * (tt % 2)] = pl.tok("vector")
            btok[2 * (tt % 2) + 1] = pl.tok("vector")
            stok = pl.tok("vector")
        wtok[a % 2] = pl.tok("tensor")
        pl.after("scalar", "vector")
        pl.dma("scalar", "x4row" + str(a % 2),
               x4_dram[128 * a:128 * (a + 1), 0:T], x4row[a % 2][:, 0:T])
        rtok[a % 2] = pl.stok("x4row" + str(a % 2))


def l0_ffn_c2(pl, x4_res, w3T, b3cols, x_fm_own, x1p_dram, T, own_rngs, pbanks):
    t, v, s, g = pl.t, pl.v, pl.s, pl.g
    wgt = [pl.sbufs["wgtA"], pl.sbufs["wgtB"]]
    xbuf = [pl.sb("c2xA", [128, 512], F32, side="right"),
            pl.sb("c2xB", [128, 512], F32, side="right")]
    orow = [pl.sb("c2oA", [128, 2048], F32, side="right"),
            pl.sb("c2oB", [128, 2048], F32, side="right")]
    ntt = T // 512
    tcols = []
    for a0, a1 in own_rngs:
        for a in range(a0, a1, 512):
            tcols.append(a)

    wslots = ["wgtA", "wgtB"]
    entry = pl.tok("tensor", "vector", "scalar", "gpsimd") + pl.all_slots_tok()
    wtok = [entry, entry]
    btok = [entry] * 4
    xtok = [entry, entry]
    otok = [entry, entry]
    for nc_ in range(32):
        stream_w_oc(pl, w3T, NCHUNK, nc_ * 128, 128, wgt[nc_ % 2], wslots[nc_ % 2],
                    war_tok=wtok[nc_ % 2])
        pl.wait_slot("tensor", wslots[nc_ % 2])
        for tt in range(ntt):
            bank = pbanks[(nc_ * ntt + tt) % 4]
            pl.waits("sync", xtok[tt % 2])
            pl.dma("sync", "c2x" + str(tt % 2), xbuf[tt % 2][:, :],
                   x_fm_own[128 * nc_:128 * (nc_ + 1), tcols[tt]:tcols[tt] + 512])
            pl.waits("tensor", btok[(nc_ * ntt + tt) % 4])
            for ic in range(NCHUNK):
                t.matmul(bank[:, :], wgt[nc_ % 2][:, ic, :], x4_res[:, ic, 512 * tt:512 * (tt + 1)],
                         start=(ic == 0), stop=(ic == NCHUNK - 1), auto=False)
            pl.tmark()
            mm_tok = pl.tok("tensor")
            pl.waits("vector", mm_tok)
            pl.wait_slot("vector", "c2x" + str(tt % 2))
            if tt <= 1:
                pl.waits("vector", otok[nc_ % 2])
            v.scalar_tensor_tensor(out=orow[nc_ % 2][:, 512 * tt:512 * (tt + 1)],
                                   in0=bank[:, :], scalar=b3cols[:, nc_:nc_ + 1],
                                   in1=xbuf[tt % 2][:, :],
                                   op0=ALU.add, op1=ALU.add)
            btok[(nc_ * ntt + tt) % 4] = pl.tok("vector")
            xtok[tt % 2] = pl.tok("vector")
        wtok[nc_ % 2] = pl.tok("tensor")
        pl.after("scalar", "vector")
        pl.dma("scalar", "c2o" + str(nc_ % 2),
               x1p_dram[128 * nc_:128 * (nc_ + 1), 0:T], orow[nc_ % 2][:, 0:T])
        otok[nc_ % 2] = pl.stok("c2o" + str(nc_ % 2))


def l0_bind_prep(pl, bv_q, bv_k, bv_v, qk_dram, vb_dram):
    v, s, q = pl.v, pl.s, pl.q
    qk_cols = pl.sb("qk_cols", [128, 32], F32)
    vb_cols = pl.sb("vb_cols", [128, 32], F32)
    bq = pl.sb("bq", [4, 1024], F32)
    bk = pl.sb("bk", [4, 1024], F32)
    bvv = pl.sb("bv", [4, 1024], F32)
    alpha = pl.sb("alpha", [4, 3], F32)
    zmask = pl.sb("zmask", [4, 1024], F32)
    hard = pl.sb("hard", [4, 3, 1024], F32)
    pl.dma("sync", "bvld", bq[:], bv_q)
    pl.dma("sync", "bvld", bk[:], bv_k)
    pl.dma("sync", "bvld", bvv[:], bv_v)
    pl.wait_slot("vector", "bvld")
    pl.wait_slot("scalar", "bvld")
    for i, b in enumerate((bq, bk, bvv)):
        v.tensor_reduce(alpha[:, i:i + 1], b[:], axis=AX.X, op=ALU.add,
                        apply_absolute_value=True)
        v.tensor_scalar_mul(alpha[:, i:i + 1], alpha[:, i:i + 1], 1.0 / 1024.0)
        s.activation(out=hard[:, i, :], in_=b[:], func=AF.Sign)
        pl.after("vector", "scalar")
        v.tensor_scalar(out=hard[:, i, :], in0=hard[:, i, :],
                        scalar1=alpha[:, i:i + 1], scalar2=None,
                        op0=ALU.mult, op1=ALU.bypass)
        v.tensor_single_scalar(out=zmask[:], in_=hard[:, i, :],
                               scalar=0.0, op=ALU.is_equal)
        v.scalar_tensor_tensor(out=hard[:, i, :], in0=pl.sbufs["zmask"][:],
                               scalar=alpha[:, i:i + 1], in1=hard[:, i, :],
                               op0=ALU.mult, op1=ALU.add)
    v.tensor_tensor(hard[:, 0, :], hard[:, 0, :], hard[:, 1, :], op=ALU.mult)
    v.tensor_scalar_mul(hard[:, 0, :], hard[:, 0, :], 0.125)
    pl.after("sync", "vector")
    pl.dma("sync", "bvst", qk_dram.rearrange("(h i) -> h i", h=4), hard[:, 0, :])
    pl.dma("sync", "bvst", vb_dram.rearrange("(h i) -> h i", h=4), hard[:, 2, :])
    pl.wait_slot("sync", "bvst")
    for c in range(32):
        pl.dma("sync", "bvld2", qk_cols[:, c:c + 1], qk_dram[128 * c:128 * (c + 1)].rearrange("(p o) -> p o", o=1))
        pl.dma("sync", "bvld2", vb_cols[:, c:c + 1], vb_dram[128 * c:128 * (c + 1)].rearrange("(p o) -> p o", o=1))
    pl.free("hard", "zmask", "alpha", "bv", "bk", "bq")
    return qk_cols, vb_cols


def l0_attn(pl, xn_fm, xn_tm, masks_dram, x1p_dram, x1_dram, qk_cols, vb_cols,
            pbanks):
    t, v, s, g = pl.t, pl.v, pl.s, pl.g

    K_res = pl.sb("l0K", [128, 8, TFULL], BF16, side="right")
    V_res = pl.sb("l0V", [128, 32, 1024], BF16, side="right")
    Ptile = pl.sb("l0P", [128, 32, 512], BF16, side="right")
    qres = [pl.sb("l0qA", [128, 8, 512], BF16, side="right"),
            pl.sb("l0qB", [128, 8, 512], BF16, side="right")]
    mres = [pl.sb("l0mA", [128, 8, 512], BF16, side="right"),
            pl.sb("l0mB", [128, 8, 512], BF16, side="right")]
    x1buf = [pl.sb("l0x1A", [128, 512], F32, side="right"),
             pl.sb("l0x1B", [128, 512], F32, side="right")]
    orow = [pl.sb("l0oA", [128, 512], F32, side="right"),
            pl.sb("l0oB", [128, 512], F32, side="right")]

    entry0 = pl.tok("tensor", "vector", "scalar", "gpsimd") + pl.all_slots_tok()
    ktok = entry0
    vtok = entry0
    qtok = [entry0, entry0]
    mtok = [entry0, entry0]
    x1tok = [entry0, entry0]
    ti_glob = 0
    for hd in range(4):
        d0c = 8 * hd
        pl.waits("gpsimd", ktok)
        for dc in range(8):
            pl.dma("gpsimd", "l0K", K_res[:, dc, :],
                   xn_fm[128 * (d0c + dc):128 * (d0c + dc + 1), :])
        pl.wait_slot("vector", "l0K")
        for dc in range(8):
            v.tensor_scalar(out=K_res[:, dc, :], in0=K_res[:, dc, :],
                            scalar1=qk_cols[:, d0c + dc:d0c + dc + 1], scalar2=None,
                            op0=ALU.mult, op1=ALU.bypass)
        kready = pl.tok("vector")
        pl.waits("sync", vtok)
        for sb in range(32):
            pl.dma("sync", "l0V", V_res[:, sb, :],
                   xn_tm[128 * sb:128 * (sb + 1), 1024 * hd:1024 * (hd + 1)])
        pl.wait_slot("tensor", "l0V")
        pl.waits("tensor", kready)

        ptok = entry0
        for ti in range(4):
            blocks, win = tile_blocks(ti)
            qb = ti_glob % 2
            pl.waits("sync", qtok[qb])
            for dc in range(8):
                pl.dma("sync", "l0q" + str(qb), qres[qb][:, dc, :],
                       xn_fm[128 * (d0c + dc):128 * (d0c + dc + 1), 512 * ti:512 * (ti + 1)])
            pl.waits("gpsimd", mtok[qb])
            pl.dma("gpsimd", "l0m" + str(qb), mres[qb][:, :, :],
                   masks_dram[ti].rearrange("k p j -> p k j"))
            pl.wait_slot("tensor", "l0q" + str(qb))
            pl.after("tensor", "scalar", "vector", "gpsimd")
            win_set = {blk: j for j, blk in enumerate(win)}
            sig_tok = []
            for si, sb in enumerate(blocks):
                bank = pbanks[4 + si % 2]
                if si >= 2:
                    pl.waits("tensor", sig_tok[si - 2])
                for dc in range(8):
                    t.matmul(bank[:, :], K_res[:, dc, 128 * sb:128 * (sb + 1)],
                             qres[qb][:, dc, :],
                             start=(dc == 0), stop=(dc == 7), auto=False)
                pl.tmark()
                mm_tok = pl.tok("tensor")
                pl.waits("scalar", mm_tok)
                if si == 0:
                    pl.waits("scalar", ptok)
                s.activation(out=Ptile[:, sb, :], in_=bank[:, :], func=AF.Sigmoid)
                if sb in win_set:
                    jj = win_set[sb]
                    e = "vector" if jj % 2 == 0 else "gpsimd"
                    pl.after(e, "scalar")
                    pl.wait_slot(e, "l0m" + str(qb))
                    ee = pl.eng[e]
                    ee.tensor_tensor(Ptile[:, sb, :], Ptile[:, sb, :],
                                     mres[qb][:, jj, :], op=ALU.mult)
                    sig_tok.append(pl.tok(e))
                else:
                    sig_tok.append(pl.tok("scalar"))
            pa_tok = pl.tok("vector", "scalar", "gpsimd")
            for dc in range(8):
                cidx = d0c + dc
                bank = pbanks[dc % 2]
                pl.waits("sync", x1tok[dc % 2])
                pl.dma("sync", "l0x1" + str(dc % 2), x1buf[dc % 2][:, :],
                       x1p_dram[128 * cidx:128 * (cidx + 1), 512 * ti:512 * (ti + 1)])
                if dc == 0:
                    pl.waits("tensor", pa_tok)
                if dc >= 2:
                    pl.waits("tensor", x1tok[dc % 2])
                for si, sb in enumerate(blocks):
                    t.matmul(bank[:, :], V_res[:, sb, 128 * dc:128 * (dc + 1)],
                             Ptile[:, sb, :], start=(si == 0), stop=(si == len(blocks) - 1),
                             auto=False)
                pl.tmark()
                mm_tok = pl.tok("tensor")
                pl.waits("vector", mm_tok)
                pl.wait_slot("vector", "l0x1" + str(dc % 2))
                pl.waits("vector", pl.stok("l0o" + str(dc % 2)))
                v.scalar_tensor_tensor(out=orow[dc % 2][:, :], in0=bank[:, :],
                                       scalar=vb_cols[:, cidx:cidx + 1],
                                       in1=x1buf[dc % 2][:, :], op0=ALU.mult, op1=ALU.add)
                x1tok[dc % 2] = pl.tok("vector")
                pl.after("scalar", "vector")
                pl.dma("scalar", "l0o" + str(dc % 2),
                       x1_dram[128 * cidx:128 * (cidx + 1), 512 * ti:512 * (ti + 1)],
                       orow[dc % 2][:, :])
            qtok[qb] = pl.tok("tensor")
            mtok[qb] = pl.tok("vector", "gpsimd")
            ptok = pl.tok("tensor")
            ti_glob += 1
        ktok = pl.tok("tensor")
        vtok = pl.tok("tensor")


def emit_l0(pl, io):
    pb = io["pb"]
    qk_cols, vb_cols = l0_bind_prep(pl, io["bv_q"], io["bv_k"], io["bv_v"],
                                    io["qkd"], io["vbd"])
    pl.wait_slot("vector", "bvld2")
    pl.wait_slot("tensor", "bvld2")
    st = ln_tm_alloc(pl)
    st["pb"] = pb
    st["xn_res"] = pl.sb("xn_res", [128, NCHUNK, TOWN], BF16, side="right")
    ln_tm_own_extra(pl, st)
    nown = TOWN // 128
    ntc = TFULL // 128
    for c in range(nown):
        ln_tm_part1(pl, st, c, io["x_tm"], io["xn0_tm"])
        ln_tm_part2(pl, st, c, io["xn0_fm"])
    ln_tm_own_stores(pl, st, io["xn0_fm"], nown)
    pl.free("bB", "sB")
    st["tail"] = True
    st["etok"][0] = (st["etok"][0] or []) + (st["etok"][1] or [])
    st["use"][0] = (st["use"][0] or []) + (st["use"][1] or [])
    st["out"][0] = (st["out"][0] or []) + (st["out"][1] or [])
    st["ntok"][0] = (st["ntok"][0] or []) + (st["ntok"][1] or [])

    def lnhook(a):
        if a >= 1 and 15 + a <= ntc - 1:
            ln_tm_part2(pl, st, 15 + a, io["xn0_fm"])
        if 16 + a <= ntc - 1:
            ln_tm_part1(pl, st, 16 + a, io["x_tm"], io["xn0_tm"])

    b1_sb = pl.sb("b1_sb", [128, 32], F32)
    b2_sb = pl.sb("b2_sb", [128, 32], F32)
    b3_sb = pl.sb("b3_sb", [128, 32], F32)
    pl.dma("sync", "bld", b1_sb[:], io["b1c"])
    pl.dma("sync", "bld", b2_sb[:], io["b2c"])
    pl.dma("sync", "bld", b3_sb[:], io["b3c"])
    pl.wait_slot("scalar", "bld")
    pl.wait_slot("vector", "bld")
    l0_ffn_c1(pl, st["xn_res"], io["w12T"], b1_sb, b2_sb, io["x4d"], TOWN, pb,
              hook=lnhook)
    pl.free("xn_res")
    pl.free("bD", "snk", "bA", "sA")
    pl.waits("sync", pl.tok("tensor", "vector", "scalar") + pl.all_slots_tok())
    x4_res = load_resident_fm(pl, "x4_res", io["x4d"], [(0, TOWN)])
    pl.wait_slot("tensor", "x4_res")
    l0_ffn_c2(pl, x4_res, io["w3T"], b3_sb, io["x_fm"], io["x1p"], TOWN,
              [(0, TOWN)], pb)
    for nme in ("c2oB", "c2oA", "c2xB", "c2xA", "x4_res"):
        pl.free(nme)
    for nme in ("x4rowB", "x4rowA", "silu_sb", "wgtD", "wgtC", "wgtB", "wgtA"):
        pl.free(nme)
    l0_attn(pl, io["xn0_fm"], io["xn0_tm"], io["masks"], io["x1p"], io["x1"],
            qk_cols, vb_cols, pb)


def declare_l0_io(nc):
    io = {}
    io["x_fm"] = nc.dram_tensor("x_fm", [D, TFULL], F32, kind="ExternalInput").ap()
    io["x_tm"] = nc.dram_tensor("x_tm", [TFULL, D], F32, kind="ExternalInput").ap()
    io["masks"] = nc.dram_tensor("masks", [4, 8, 128, 512], BF16, kind="ExternalInput").ap()
    for nme in ("bv_q", "bv_k", "bv_v"):
        io[nme] = nc.dram_tensor(nme, [4, 1024], F32, kind="ExternalInput").ap()
    io["w12T"] = nc.dram_tensor("w12T", [4096, 8192], F32, kind="ExternalInput").ap()
    io["w3T"] = nc.dram_tensor("w3T", [4096, 4096], F32, kind="ExternalInput").ap()
    for nme in ("b1c", "b2c", "b3c"):
        io[nme] = nc.dram_tensor(nme, [128, 32], F32, kind="ExternalInput").ap()
    io["xn0_fm"] = nc.dram_tensor("xn0_fm", [D, TFULL], BF16, kind="Internal").ap()
    io["xn0_tm"] = nc.dram_tensor("xn0_tm", [TFULL, D], BF16, kind="Internal").ap()
    io["rows_r"] = nc.dram_tensor("rows_r", [1, TFULL], F32, kind="Internal").ap()
    io["rows_n"] = nc.dram_tensor("rows_n", [1, TFULL], F32, kind="Internal").ap()
    io["x4d"] = nc.dram_tensor("x4d", [4096, TOWN], BF16, kind="Internal").ap()
    io["x1p"] = nc.dram_tensor("x1p", [D, TOWN], F32,
                               kind="ExternalOutput" if DEBUG_X1P else "Internal").ap()
    io["qkd"] = nc.dram_tensor("qkd", [4096], F32, kind="Internal").ap()
    io["vbd"] = nc.dram_tensor("vbd", [4096], F32, kind="Internal").ap()
    return io


def build_l0_program():
    nc = bass.Bass("TRN2", target_bir_lowering=False)
    stack = ExitStack()
    io = declare_l0_io(nc)
    io["x1"] = nc.dram_tensor("x1", [D, TOWN], F32, kind="ExternalOutput").ap()
    with stack:
        pl = Planner(nc, stack)
        declare_and_load_consts(pl)
        io["pb"] = [pl.ps(f"pb{i}", [128, 512], F32) for i in range(8)]
        for e in ("tensor", "vector", "scalar"):
            pl.wait_slot(e, "consts")
        emit_l0(pl, io)
        pl.fence_all()
        pl.after("sync", "scalar", "gpsimd", "vector", "tensor")
        counts = pl.finalize()
    return nc, counts


def prep_l0_inputs(x, bv_q, bv_k, bv_v, w12, b12, w3, b3, consts):
    shared = {
        "bv_q": np.asarray(bv_q, dtype=np.float32),
        "bv_k": np.asarray(bv_k, dtype=np.float32),
        "bv_v": np.asarray(bv_v, dtype=np.float32),
        "w12T": np.ascontiguousarray(np.asarray(w12).T),
        "b1c": np.ascontiguousarray(np.asarray(b12)[:4096].reshape(32, 128).T),
        "b2c": np.ascontiguousarray(np.asarray(b12)[4096:].reshape(32, 128).T),
        "w3T": np.ascontiguousarray(np.asarray(w3).T),
        "b3c": np.ascontiguousarray(np.asarray(b3).reshape(32, 128).T),
    }
    shared.update(host_const_inputs(consts))
    per_core = []
    for c in range(8):
        b, h = c // 2, c % 2
        cols = host_col_order(h)
        xb = np.asarray(x[b], dtype=np.float32)[cols]
        m = dict(shared)
        m["x_tm"] = np.ascontiguousarray(xb)
        m["x_fm"] = np.ascontiguousarray(xb.T)
        m["masks"] = host_masks(h)
        per_core.append(m)
    return per_core


def place_own_cols(out_full_b, own_cols_arr, h):
    cols = host_col_order(h)[:TOWN]
    out_full_b[cols] = own_cols_arr


def _rope_tile(pl, bank, bank_swap, cosd, sinds, out_slice, swap_bf, bufs, e="vector"):
    t, s = pl.t, pl.s
    ee = pl.eng[e]
    raw, swp, t1, t2 = bufs["raw"], bufs["swp"], bufs["t1"], bufs["t2"]
    mm_tok = pl.tok("tensor")
    pl.waits("scalar", mm_tok)
    s.copy(out=raw[:, :], in_=bank[:, :])
    pl.after("tensor", "scalar")
    t.matmul(bank_swap[:, :], swap_bf[:], raw[:, :], start=True, stop=True, auto=False)
    pl.tmark()
    pl.after("scalar", "tensor")
    s.copy(out=swp[:, :], in_=bank_swap[:, :])
    pl.after(e, "scalar")
    ee.tensor_tensor(t1[:, :], raw[:, :], cosd, op=ALU.mult)
    ee.tensor_tensor(t2[:, :], swp[:, :], sinds, op=ALU.mult)
    ee.tensor_tensor(out_slice, t1[:, :], t2[:, :], op=ALU.add)


def l1_qkv_q(pl, xn_res, wqkvp, ropeq_cos, ropeq_sin, q1_dram, pbanks, T, swap_bf):
    t = pl.t
    wgt = [pl.sb("wgtA", [128, NCHUNK, 128], BF16), pl.sb("wgtB", [128, NCHUNK, 128], BF16)]
    bufs = {n: pl.sb("rp_" + n, [128, 512], BF16, side="right") for n in ("raw", "swp", "t1", "t2")}
    q1row = [pl.sb("q1rowA", [128, 2048], BF16, side="right"),
             pl.sb("q1rowB", [128, 2048], BF16, side="right")]
    ntt = T // 512

    entry = pl.tok("tensor", "vector", "scalar", "gpsimd") + pl.all_slots_tok()
    wtok = [entry, entry]
    btok = [entry] * 2
    rtok = [entry, entry]
    rope_tok = entry
    for oc in range(32):
        stream_w_oc(pl, wqkvp, NCHUNK, oc * 128, 128, wgt[oc % 2], "wgt" + str(oc % 2),
                    war_tok=wtok[oc % 2])
        pl.wait_slot("tensor", "wgt" + str(oc % 2))
        for tt in range(ntt):
            bank = pbanks[tt % 2]
            bank_swap = pbanks[2 + tt % 2]
            pl.waits("tensor", btok[tt % 2])
            for ic in range(NCHUNK):
                t.matmul(bank[:, :], wgt[oc % 2][:, ic, :], xn_res[:, ic, 512 * tt:512 * (tt + 1)],
                         start=(ic == 0), stop=(ic == NCHUNK - 1), auto=False)
            pl.tmark()
            pl.waits("scalar", rope_tok)
            if tt == 0:
                pl.waits("vector", rtok[oc % 2])
            _rope_tile(pl, bank, bank_swap, ropeq_cos[:, 512 * tt:512 * (tt + 1)],
                       ropeq_sin[:, 512 * tt:512 * (tt + 1)],
                       q1row[oc % 2][:, 512 * tt:512 * (tt + 1)], swap_bf, bufs)
            rope_tok = pl.tok("vector")
            btok[tt % 2] = rope_tok + pl.tok("scalar")
        wtok[oc % 2] = pl.tok("tensor")
        pl.after("scalar", "vector")
        pl.dma("scalar", "q1row" + str(oc % 2),
               q1_dram[128 * oc:128 * (oc + 1), 0:T], q1row[oc % 2][:, 0:T])
        rtok[oc % 2] = pl.stok("q1row" + str(oc % 2))
    pl.fence("scalar")
    pl.waits("gpsimd", pl.tok("tensor", "vector", "scalar"))
    for nme in ("q1rowB", "q1rowA", "rp_t2", "rp_t1", "rp_swp", "rp_raw"):
        pl.free(nme)


def l1_qkv_kv(pl, xn_src_fm, Tkv, wqkvp, ropek_cos, ropek_sin, k1_dram, v1t_dram,
              pbanks, swap_bf, rank_order=False):
    t, v, s, g = pl.t, pl.v, pl.s, pl.g
    idb = pl.sbufs["c_ident_bf"]
    wkv = pl.sb("wkv_res", [128, NCHUNK, 2048], BF16, side="right")
    xt = pl.sb("kv_xt", [128, NCHUNK, 512], BF16, side="right")
    bufs = {n: pl.sb("rp_" + n, [128, 512], BF16, side="right") for n in ("raw", "swp", "t1", "t2")}
    krow = pl.sb("krow", [128, 512], BF16, side="right")
    vt_sb = pl.sb("vt_sb", [128, 4, 1024], BF16, side="right")
    pbt = pbanks[7][:, :].bitcast(BF16)

    entry = pl.tok("tensor", "vector", "scalar", "gpsimd") + pl.all_slots_tok()
    pl.waits("gpsimd", entry)
    for ic in range(NCHUNK):
        pl.dma("gpsimd", "wkv_res", wkv[:, ic, :],
               wqkvp[128 * ic:128 * (ic + 1), 4096:6144])
    pl.wait_slot("tensor", "wkv_res")

    xtok = entry
    rope_tok = entry
    ktok = [entry, entry]
    vtok = entry
    for tt in range(Tkv // 512):
        pl.waits("sync", xtok)
        if rank_order:
            rk, c0 = tt // 4, 512 * (tt % 4)
            xview = xn_src_fm[rk * D:(rk + 1) * D, c0:c0 + 512]
        else:
            xview = xn_src_fm[:, 512 * tt:512 * (tt + 1)]
        pl.dma("sync", "kv_xt", xt[:, :, :],
               xview.rearrange("(ic p) j -> p ic j", p=128))
        pl.wait_slot("tensor", "kv_xt")
        for oc in range(16):
            bank = pbanks[oc % 2]
            bank2 = pbanks[2 + oc % 2]
            pl.waits("tensor", ktok[oc % 2])
            for ic in range(NCHUNK):
                t.matmul(bank[:, :], wkv[:, ic, 128 * oc:128 * (oc + 1)], xt[:, ic, :],
                         start=(ic == 0), stop=(ic == NCHUNK - 1), auto=False)
            pl.tmark()
            if oc < 8:
                pl.waits("scalar", rope_tok)
                pl.waits("vector", pl.stok("krow"))
                _rope_tile(pl, bank, bank2, ropek_cos[:, 512 * tt:512 * (tt + 1)],
                           ropek_sin[:, 512 * tt:512 * (tt + 1)], krow[:, :], swap_bf, bufs)
                rope_tok = pl.tok("vector")
                ktok[oc % 2] = rope_tok + pl.tok("scalar")
                pl.after("scalar", "vector")
                pl.dma("scalar", "krow",
                       k1_dram[128 * oc:128 * (oc + 1), 512 * tt:512 * (tt + 1)], krow[:, :])
            else:
                mm_tok = pl.tok("tensor")
                pl.waits("scalar", mm_tok)
                pl.waits("scalar", pl.tok("tensor"))
                s.copy(out=bufs["raw"][:, :], in_=bank[:, :])
                pl.after("tensor", "scalar")
                if oc == 8:
                    pl.waits("tensor", vtok)
                for j in range(4):
                    t.transpose(pbt[:, 128 * j:128 * (j + 1)],
                                bufs["raw"][:, 128 * j:128 * (j + 1)], idb[:], auto=False)
                pl.tmark()
                pl.after("vector", "tensor")
                pl.waits("vector", pl.stok("vt_sb"))
                for j in range(4):
                    v.tensor_copy(vt_sb[:, j, 128 * (oc - 8):128 * (oc - 7)],
                                  pbt[:, 128 * j:128 * (j + 1)])
                rope_tok = rope_tok + pl.tok("vector")
                ktok[oc % 2] = pl.tok("vector")
        pl.after("sync", "vector")
        for j in range(4):
            pl.dma("sync", "vt_sb", v1t_dram[512 * tt + 128 * j:512 * tt + 128 * (j + 1), :],
                   vt_sb[:, j, :])
        vtok = pl.tok("vector")
        xtok = pl.tok("tensor")
    pl.fence("scalar")
    pl.fence("sync")
    pl.waits("gpsimd", pl.tok("tensor", "vector", "scalar"))
    for nme in ("vt_sb", "krow", "rp_t2", "rp_t1", "rp_swp", "rp_raw", "kv_xt", "wkv_res"):
        pl.free(nme)


def l1_moe(pl, xn_res, WrT, A2, B2, E_dram, x1_fm, x2p_dram, pbanks, T):
    t, v, s, g = pl.t, pl.v, pl.s, pl.g
    idf = pl.sbufs["c_ident_f32"]
    ntt = T // 512
    wr_sb = pl.sb("wr_sb", [128, NCHUNK, 64], BF16, side="right")
    sc_sb = pl.sb("sc_sb", [64, 2048], F32, side="right")
    st_tm = pl.sb("st_tm", [128, 16, 64], F32, side="right")
    tk = {n: pl.sb("tk_" + n, [128, 64], F32, side="right")
          for n in ("m16", "srep", "srep2", "ge", "mask")}
    Gfm = pl.sb("Gfm", [64, 2048], BF16, side="right")
    e_sb = pl.sb("e_sb", [64, 1024], BF16, side="right")
    wgt = [pl.sb("mwgtA", [128, NCHUNK, 128], BF16, side="right"),
           pl.sb("mwgtB", [128, NCHUNK, 128], BF16, side="right")]
    hg_tt = pl.sb("hg_tt", [128, 8, 512], BF16, side="right")
    gexp = pl.sb("gexp", [128, 512], BF16, side="right")
    x1b = [pl.sb("mx1A", [128, 512], F32, side="right"),
           pl.sb("mx1B", [128, 512], F32, side="right")]
    x2b = [pl.sb("mx2A", [128, 512], F32, side="right"),
           pl.sb("mx2B", [128, 512], F32, side="right")]

    entry = pl.tok("tensor", "vector", "scalar", "gpsimd") + pl.all_slots_tok()
    pl.waits("gpsimd", entry)
    stream_w_oc(pl, WrT, NCHUNK, 0, 64, wr_sb, "wr_sb", war_tok=None)
    pl.dma("gpsimd", "e_sb", e_sb[:, :], E_dram)
    pl.wait_slot("tensor", "wr_sb")
    pl.waits("tensor", entry)
    for tt in range(ntt):
        bank = pbanks[tt % 2]
        if tt >= 2:
            pl.after("tensor", "scalar")
        for ic in range(NCHUNK):
            t.matmul(bank[0:64, :], wr_sb[:, ic, :], xn_res[:, ic, 512 * tt:512 * (tt + 1)],
                     start=(ic == 0), stop=(ic == NCHUNK - 1), auto=False)
        pl.tmark()
        pl.after("scalar", "tensor")
        s.copy(out=sc_sb[:, 512 * tt:512 * (tt + 1)], in_=bank[0:64, :])
    pl.after("tensor", "scalar")
    for tb in range(T // 128):
        bank = pbanks[4 + tb % 2]
        if tb >= 2:
            pl.after("tensor", "vector")
        t.transpose(bank[0:128, 0:64], sc_sb[0:64, 128 * tb:128 * (tb + 1)],
                    idf[0:64, 0:64], auto=False)
        pl.tmark()
        pl.after("vector", "tensor")
        v.tensor_copy(st_tm[:, tb, 0:64], bank[0:128, 0:64])
    for tb in range(T // 128):
        pl.after("vector", "scalar")
        v.max(out=tk["m16"][:, 0:8], in_=st_tm[:, tb, :])
        v.match_replace(out=tk["srep"][:, :], in_to_replace=tk["m16"][:, 0:8],
                        in_values=st_tm[:, tb, :], imm_value=-1e30)
        v.max(out=tk["m16"][:, 8:16], in_=tk["srep"][:, :])
        v.match_replace(out=tk["srep2"][:, :], in_to_replace=tk["m16"][:, 8:16],
                        in_values=tk["srep"][:, :], imm_value=-1e30)
        v.tensor_scalar_mul(tk["m16"][:, 16:17], tk["m16"][:, 0:1], -1.0)
        pl.after("scalar", "vector")
        s.activation(out=tk["m16"][:, 32:48], in_=tk["m16"][:, 0:16], func=AF.Exp,
                     bias=tk["m16"][:, 16:17])
        s.activation(out=tk["ge"][:, :], in_=st_tm[:, tb, :], func=AF.Exp,
                     bias=tk["m16"][:, 16:17])
        pl.after("vector", "scalar")
        v.reduce_sum(tk["m16"][:, 17:18], tk["m16"][:, 32:48], axis=AX.X)
        v.reciprocal(tk["m16"][:, 18:19], tk["m16"][:, 17:18])
        v.tensor_scalar_mul(tk["m16"][:, 18:19], tk["m16"][:, 18:19], 2.0)
        v.tensor_tensor(tk["mask"][:, :], st_tm[:, tb, :], tk["srep2"][:, :], op=ALU.is_gt)
        v.scalar_tensor_tensor(out=st_tm[:, tb, :], in0=tk["ge"][:, :],
                               scalar=tk["m16"][:, 18:19], in1=tk["mask"][:, :],
                               op0=ALU.mult, op1=ALU.mult)
    for tb in range(T // 128):
        bank = pbanks[4 + tb % 2]
        pl.after("tensor", "vector", "scalar")
        t.transpose(bank[0:64, 0:128], st_tm[:, tb, 0:64], idf[:], auto=False)
        pl.tmark()
        pl.after("vector", "tensor")
        v.tensor_copy(Gfm[0:64, 128 * tb:128 * (tb + 1)], bank[0:64, 0:128])
    wtok = [pl.tok("tensor"), pl.tok("tensor")]
    btok = [pl.tok("vector")] * 2
    x1tok = [pl.tok("vector"), pl.tok("vector")]
    x2tok = [None, None]
    pl.wait_slot("tensor", "e_sb")
    for tt in range(ntt):
        for krc in range(8):
            it = tt * 8 + krc
            stream_w_oc(pl, A2, NCHUNK, krc * 128, 128, wgt[it % 2], "mwgt" + str(it % 2),
                        war_tok=wtok[it % 2])
            bankG = pbanks[4 + krc % 2]
            bankH = pbanks[2 + krc % 2]
            pl.after("tensor", "scalar", "vector")
            t.matmul(bankG[:, :], e_sb[:, 128 * krc:128 * (krc + 1)],
                     Gfm[0:64, 512 * tt:512 * (tt + 1)], start=True, stop=True, auto=False)
            pl.tmark()
            pl.after("scalar", "tensor")
            pl.after("scalar", "vector")
            s.copy(out=gexp[:, :], in_=bankG[:, :])
            pl.wait_slot("tensor", "mwgt" + str(it % 2))
            for ic in range(NCHUNK):
                t.matmul(bankH[:, :], wgt[it % 2][:, ic, :], xn_res[:, ic, 512 * tt:512 * (tt + 1)],
                         start=(ic == 0), stop=(ic == NCHUNK - 1), auto=False)
            pl.tmark()
            wtok[it % 2] = pl.tok("tensor")
            pl.after("vector", "tensor", "scalar")
            v.tensor_tensor(hg_tt[:, krc, :], bankH[:, :], gexp[:, :], op=ALU.mult)
        hgrdy = pl.tok("vector")
        for nc_ in range(32):
            it = tt * 32 + nc_
            stream_w_oc(pl, B2, 8, nc_ * 128, 128, wgt[it % 2], "mwgt" + str(it % 2),
                        war_tok=wtok[it % 2])
            bank = pbanks[nc_ % 2]
            pl.waits("sync", x1tok[nc_ % 2])
            pl.dma("sync", "mx1" + str(nc_ % 2), x1b[nc_ % 2][:, :],
                   x1_fm[128 * nc_:128 * (nc_ + 1), 512 * tt:512 * (tt + 1)])
            pl.wait_slot("tensor", "mwgt" + str(it % 2))
            pl.waits("tensor", btok[nc_ % 2], hgrdy)
            for ic in range(8):
                t.matmul(bank[:, :], wgt[it % 2][:, ic, :], hg_tt[:, ic, :],
                         start=(ic == 0), stop=(ic == 7), auto=False)
            pl.tmark()
            wtok[it % 2] = pl.tok("tensor")
            pl.waits("vector", pl.tok("tensor"))
            pl.wait_slot("vector", "mx1" + str(nc_ % 2))
            pl.waits("vector", x2tok[nc_ % 2])
            v.tensor_tensor(x2b[nc_ % 2][:, :], bank[:, :], x1b[nc_ % 2][:, :], op=ALU.add)
            btok[nc_ % 2] = pl.tok("vector")
            x1tok[nc_ % 2] = pl.tok("vector")
            pl.after("scalar", "vector")
            pl.dma("scalar", "mx2" + str(nc_ % 2),
                   x2p_dram[128 * nc_:128 * (nc_ + 1), 512 * tt:512 * (tt + 1)],
                   x2b[nc_ % 2][:, :])
            x2tok[nc_ % 2] = pl.stok("mx2" + str(nc_ % 2))
    for nme in ("mx2B", "mx2A", "mx1B", "mx1A", "gexp", "hg_tt", "mwgtB", "mwgtA",
                "e_sb", "Gfm", "tk_mask", "tk_ge", "tk_srep2", "tk_srep", "tk_m16",
                "st_tm", "sc_sb", "wr_sb"):
        pl.free(nme)


def l1_attn(pl, masks_dram, q1_dram, k1_dram, v1t_dram, Tkv, pbanks, kg_list=range(8), ql_list=range(4)):
    t, v, s, g = pl.t, pl.v, pl.s, pl.g
    ones_bf = pl.sbufs["c_ones_bf_col"]
    SC = 0.08838834764831845
    C2 = (SC * 1.0002) ** 2

    ao = pl.sb("ao_res", [128, NCHUNK, 2048], BF16, side="right")
    Kg = [pl.sb("l1KA", [128, Tkv], BF16, side="right"),
          pl.sb("l1KB", [128, Tkv], BF16, side="right")]
    Vg = pl.sb("l1V", [128, Tkv // 128, 128], BF16, side="right")
    Pt = pl.sb("l1P", [128, 32, 512], BF16, side="right")
    qre = [pl.sb("l1qA", [128, 2048], BF16, side="right"),
           pl.sb("l1qB", [128, 2048], BF16, side="right")]
    mre = pl.sb("l1mA", [128, 8, 512], BF16, side="right")
    ubias = pl.sb("l1ub", [128, 16], F32, side="right")
    stat = pl.sb("l1stat", [128, 16], F32, side="right")
    qmax = pl.sb("l1qmax", [1, 128], F32, side="right")
    negr = pl.sb("l1neg", [1, 128], F32, side="right")
    rnb = pl.sb("l1rnb", [128, 512], BF16, side="right")
    nrow = Pt[0:1, 30:32, :].rearrange("p a b -> p (a b)").bitcast(F32)
    rn_bf = pl.sb("l1rnr", [1, 512], BF16, side="right")

    pl.after("gpsimd", "tensor", "vector", "scalar")
    g.memset(stat[:, :], 0.0)
    g.memset(qmax[:, :], 0.0)
    g.memset(negr[:, :], -1.0)
    entry = pl.tok("tensor", "vector", "scalar", "gpsimd") + pl.all_slots_tok()
    pl.waits("tensor", entry)
    QKB = [pbanks[0], pbanks[1], pbanks[4], pbanks[5]]
    klist = list(kg_list)
    qlist = list(ql_list)
    sched = [(kg_, ql_) for kg_ in klist for ql_ in qlist]
    half = Tkv // 256

    def load_K(kg_):
        pl.dma("sync", "l1K" + str(kg_ % 2), Kg[kg_ % 2][:, :],
               k1_dram[128 * kg_:128 * (kg_ + 1), 0:Tkv])

    def load_V(kg_):
        pl.dma("sync", "l1V", Vg[:, 0:half, :],
               v1t_dram[0:Tkv // 2, 128 * kg_:128 * (kg_ + 1)].rearrange("(sb p) d -> p sb d", p=128))
        pl.dma("sync", "l1V", Vg[:, half:2 * half, :],
               v1t_dram[Tkv // 2:Tkv, 128 * kg_:128 * (kg_ + 1)].rearrange("(sb p) d -> p sb d", p=128))

    def load_q(qh_):
        pl.dma("sync", "l1q" + str(qh_ % 2), qre[qh_ % 2][:, :],
               q1_dram[128 * qh_:128 * (qh_ + 1), :])

    pl.waits("sync", entry)
    load_K(klist[0])
    load_V(klist[0])

    sqtok = [entry, entry]
    mmtok = [entry, entry]
    for qh in range(32):
        qb = qh % 2
        pl.waits("sync", sqtok[qb])
        load_q(qh)
        eng = "scalar" if qb == 0 else "vector"
        pl.wait_slot(eng, "l1q" + str(qb))
        pl.waits(eng, mmtok[qb])
        reg = Pt[:, 4 * qb:4 * qb + 4, :]
        if qb == 0:
            s.activation(out=reg, in_=qre[qb][:, :], func=AF.Square)
        else:
            v.tensor_tensor(reg, qre[qb][:, :], qre[qb][:, :], op=ALU.mult)
        sqtok[qb] = pl.tok(eng)
        pl.waits("tensor", sqtok[qb])
        pl.after("tensor", "vector")
        for jj in range(4):
            t.matmul(QKB[jj][0:1, :], ones_bf[:], Pt[:, 4 * qb + jj, :],
                     start=True, stop=True, auto=False)
        pl.tmark()
        mmtok[qb] = pl.tok("tensor")
        pl.waits("vector", mmtok[qb])
        for jj in range(4):
            v.reduce_max(qmax[0:1, 4 * qh + jj:4 * qh + jj + 1],
                         QKB[jj][0:1, :], axis=AX.X)

    vtok = entry
    kvtok = [entry, entry]
    qtok = [sqtok[0], sqtok[1]]
    mtok = entry
    otok = [entry, entry]
    pend = {"bankO": None}
    ti_glob = 0
    for ki, kg in enumerate(klist):
        kgb = kg % 2
        if ki > 0:
            pl.waits("sync", vtok)
            load_V(kg)
        if ki == 0:
            p0 = qlist[0] % 2
            pl.waits("sync", qtok[p0])
            load_q(4 * kg + qlist[0])
        if ki + 1 < len(klist):
            ng = klist[ki + 1]
            pl.waits("sync", kvtok[ng % 2])
            load_K(ng)
        pl.wait_slot("scalar", "l1K" + str(kgb))
        pl.waits("scalar", pl.tok("tensor"))
        s.activation(out=Pt[:, 0:8, :], in_=Kg[kgb][:, :], func=AF.Square)
        kst = pl.tok("scalar")
        pl.waits("tensor", kst)
        pl.after("tensor", "vector")
        pl.wait_slot("tensor", "l1V")
        for jj in range(Tkv // 512):
            if jj == 4:
                pl.after("tensor", "vector")
            t.matmul(QKB[jj % 4][0:1, :], ones_bf[:], Pt[:, jj, :],
                     start=True, stop=True, auto=False)
            if jj % 4 == 3:
                pl.tmark()
                grp = pl.tok("tensor")
                pl.waits("vector", grp)
                for j2 in range(jj - 3, jj + 1):
                    v.reduce_max(stat[0:1, j2:j2 + 1], QKB[j2 % 4][0:1, :],
                                 axis=AX.X)
        v.reduce_max(stat[0:1, 8:9], stat[0:1, 0:8], axis=AX.X)
        v.tensor_scalar(out=nrow[0:1, 0:16], in0=qmax[0:1, 16 * kg:16 * kg + 16],
                        scalar1=stat[0:1, 8:9], scalar2=C2,
                        op0=ALU.mult, op1=ALU.mult)
        pl.after("scalar", "vector")
        s.activation(out=nrow[0:1, 16:32], in_=nrow[0:1, 0:16], func=AF.Sqrt)
        pl.after("tensor", "vector")
        pl.after("tensor", "scalar")
        t.matmul(pbanks[6][0:128, 0:16], negr[:], nrow[0:1, 16:32],
                 start=True, stop=True, auto=False)
        pl.tmark()
        pl.after("scalar", "tensor")
        s.copy(out=ubias[:, :], in_=pbanks[6][0:128, 0:16])

        for qi, ql in enumerate(qlist):
            qh = 4 * kg + ql
            qb = ql % 2
            pl.wait_slot("tensor", "l1q" + str(qb))
            prev_tail = [pl.tok("scalar") + pl.tok("vector")] * 4
            for ti in range(4):
                blocks, win = tile_blocks(ti)
                if ti == 3:
                    idx = ki * len(qlist) + qi
                    if idx + 1 < len(sched):
                        nkg, nql = sched[idx + 1]
                        pl.waits("sync", qtok[nql % 2])
                        load_q(4 * nkg + nql)
                pl.waits("gpsimd", mtok)
                pl.dma("gpsimd", "l1mA", mre[:, :, :],
                       masks_dram[ti].rearrange("k p j -> p k j"))
                win_set = {blk: jx for jx, blk in enumerate(win)}
                nonwin = [b_ for b_ in blocks if b_ not in win_set]
                blocks = nonwin[0:4] + win + nonwin[4:]
                bankO = pbanks[2 + ti_glob % 2]
                bankN = pbanks[6]
                exp_toks = []
                nb = len(blocks)
                for si, sb in enumerate(blocks):
                    bank = QKB[si % 4]
                    pl.waits("tensor", exp_toks[si - 4] if si >= 4 else prev_tail[si])
                    t.matmul(bank[:, :], Kg[kgb][:, 128 * sb:128 * (sb + 1)],
                             qre[qb][:, 512 * ti:512 * (ti + 1)],
                             start=True, stop=True, auto=False)
                    pl.tmark()
                    mm_tok = pl.tok("tensor")
                    pl.waits("scalar", mm_tok)
                    s.activation(out=Pt[:, sb, :], in_=bank[:, :], func=AF.Exp,
                                 bias=ubias[:, 4 * ql + ti:4 * ql + ti + 1],
                                 scale=SC)
                    if sb in win_set:
                        jx = win_set[sb]
                        pl.after("vector", "scalar")
                        pl.wait_slot("vector", "l1mA")
                        v.tensor_tensor(Pt[:, sb, :], Pt[:, sb, :],
                                        mre[:, jx, :], op=ALU.mult)
                        exp_toks.append(pl.tok("vector"))
                    else:
                        exp_toks.append(pl.tok("scalar"))
                    if si == 2 and pend["bankO"] is not None:
                        pl.waits("tensor", pend["nc_tok"])
                        pl.after("tensor", "scalar")
                        t.matmul(bankN[:, :], pl.sbufs["c_ones_bf_row"][:],
                                 pend["rn"], start=True, stop=True, auto=False)
                        pl.tmark()
                        pl.after("scalar", "tensor")
                        s.copy(out=rnb[:, :], in_=bankN[:, :])
                        pl.after("vector", "scalar")
                        v.tensor_tensor(pend["ao"], pend["bankO"][:, :], rnb[:, :],
                                        op=ALU.mult)
                        otok[pend["par"]] = pl.tok("vector")
                        pend["bankO"] = None
                    if si >= 2:
                        pb_ = blocks[si - 2]
                        pl.waits("tensor", exp_toks[si - 2])
                        if si == 2:
                            pl.waits("tensor", otok[ti_glob % 2])
                            pl.after("tensor", "scalar")
                        t.matmul(bankO[:, :], Vg[:, pb_, :], Pt[:, pb_, :],
                                 start=(si - 2 == 0), stop=False, auto=False)
                        t.matmul(bankN[0:1, :], ones_bf[:], Pt[:, pb_, :],
                                 start=(si - 2 == 0), stop=False, auto=False)
                        pl.tmark()
                for tail in (nb - 2, nb - 1):
                    pb_ = blocks[tail]
                    pl.waits("tensor", exp_toks[tail])
                    t.matmul(bankO[:, :], Vg[:, pb_, :], Pt[:, pb_, :],
                             start=(tail == 0), stop=(tail == nb - 1), auto=False)
                    t.matmul(bankN[0:1, :], ones_bf[:], Pt[:, pb_, :],
                             start=(tail == 0), stop=(tail == nb - 1), auto=False)
                    pl.tmark()
                prev_tail = [exp_toks[-4], exp_toks[-3], exp_toks[-2], exp_toks[-1]]
                nrm_tok = pl.tok("tensor")
                pl.waits("scalar", nrm_tok)
                s.activation(out=nrow[0:1, :], in_=bankN[0:1, :], func=AF.Ln)
                s.activation(out=rn_bf[0:1, :], in_=nrow[0:1, :], func=AF.Exp,
                             scale=-1.0)
                pend["bankO"] = bankO
                pend["ao"] = ao[:, qh, 512 * ti:512 * (ti + 1)]
                pend["rn"] = rn_bf[0:1, :]
                pend["nc_tok"] = pl.tok("vector")
                pend["par"] = ti_glob % 2
                mtok = pl.tok("vector", "gpsimd")
                ti_glob += 1
            qtok[qb] = pl.tok("tensor")
        kvtok[kgb] = pl.tok("tensor", "scalar")
        vtok = pl.tok("tensor")
    if pend["bankO"] is not None:
        pl.waits("tensor", pend["nc_tok"])
        pl.after("tensor", "scalar")
        t.matmul(pbanks[6][:, :], pl.sbufs["c_ones_bf_row"][:], pend["rn"],
                 start=True, stop=True, auto=False)
        pl.tmark()
        pl.after("scalar", "tensor")
        s.copy(out=rnb[:, :], in_=pbanks[6][:, :])
        pl.after("vector", "scalar")
        v.tensor_tensor(pend["ao"], pend["bankO"][:, :], rnb[:, :], op=ALU.mult)
    return ao


def l1_wo(pl, ao_res, woT, x2p_dram, out_dram, pbanks, T):
    t, v = pl.t, pl.v
    wgt = [pl.sb("wgtA", [128, NCHUNK, 128], BF16), pl.sb("wgtB", [128, NCHUNK, 128], BF16)]
    x2b = [pl.sb("wx2A", [128, 512], F32, side="right"),
           pl.sb("wx2B", [128, 512], F32, side="right")]
    orow = [pl.sb("woA", [128, 2048], F32, side="right"),
            pl.sb("woB", [128, 2048], F32, side="right")]
    ntt = T // 512

    entry = pl.tok("tensor", "vector", "scalar", "gpsimd") + pl.all_slots_tok()
    wtok = [entry, entry]
    btok = [entry] * 4
    xtok = [entry, entry]
    otok = [entry, entry]
    for nc_ in range(32):
        stream_w_oc(pl, woT, NCHUNK, nc_ * 128, 128, wgt[nc_ % 2], "wgt" + str(nc_ % 2),
                    war_tok=wtok[nc_ % 2])
        pl.wait_slot("tensor", "wgt" + str(nc_ % 2))
        for tt in range(ntt):
            bank = pbanks[(nc_ * ntt + tt) % 4]
            pl.waits("sync", xtok[tt % 2])
            pl.dma("sync", "wx2" + str(tt % 2), x2b[tt % 2][:, :],
                   x2p_dram[128 * nc_:128 * (nc_ + 1), 512 * tt:512 * (tt + 1)])
            pl.waits("tensor", btok[(nc_ * ntt + tt) % 4])
            for ic in range(NCHUNK):
                t.matmul(bank[:, :], wgt[nc_ % 2][:, ic, :], ao_res[:, ic, 512 * tt:512 * (tt + 1)],
                         start=(ic == 0), stop=(ic == NCHUNK - 1), auto=False)
            pl.tmark()
            mm_tok = pl.tok("tensor")
            pl.waits("vector", mm_tok)
            pl.wait_slot("vector", "wx2" + str(tt % 2))
            if tt <= 1:
                pl.waits("vector", otok[nc_ % 2])
            v.tensor_tensor(orow[nc_ % 2][:, 512 * tt:512 * (tt + 1)], bank[:, :],
                            x2b[tt % 2][:, :], op=ALU.add)
            btok[(nc_ * ntt + tt) % 4] = pl.tok("vector")
            xtok[tt % 2] = pl.tok("vector")
        wtok[nc_ % 2] = pl.tok("tensor")
        pl.after("scalar", "vector")
        pl.dma("scalar", "wo_out" + str(nc_ % 2),
               out_dram[128 * nc_:128 * (nc_ + 1), 0:T], orow[nc_ % 2][:, 0:T])
        otok[nc_ % 2] = pl.stok("wo_out" + str(nc_ % 2))


def emit_l1(pl, io):
    pb = io["pb"]
    pl.pool()
    ln_fm_phase(pl, TFULL, io["x1_fm"], io["xn1_fm"], "A1", pb)
    pl.free_pool()
    pl.waits("sync", pl.tok("vector", "gpsimd", "scalar", "tensor") + pl.all_slots_tok())
    xn_res = load_resident_fm(pl, "xn1_res", io["xn1_fm"], [(0, TOWN)])
    pl.wait_slot("tensor", "xn1_res")
    rqc = pl.sb("rqc", [128, 2048], BF16, side="right")
    rqs = pl.sb("rqs", [128, 2048], BF16, side="right")
    pl.dma("sync", "ropeq", rqc[:], io["rqc"])
    pl.dma("sync", "ropeq", rqs[:], io["rqs"])
    pl.wait_slot("vector", "ropeq")
    pl.wait_slot("tensor", "ropeq")
    l1_qkv_q(pl, xn_res, io["wqkvp"], rqc, rqs, io["q1"], pb, TOWN,
             pl.sbufs["c_swap_bf"])
    pl.free("rqs", "rqc")
    l1_moe(pl, xn_res, io["WrT"], io["A2"], io["B2"], io["E_exp"],
           io["x1_fm"][:, 0:TOWN], io["x2p"], pb, TOWN)
    pl.free("xn1_res")
    pl.free("wgtB", "wgtA")
    rkc = pl.sb("rkc", [128, TFULL], BF16, side="right")
    rks = pl.sb("rks", [128, TFULL], BF16, side="right")
    pl.dma("sync", "ropek", rkc[:], io["rkc"])
    pl.dma("sync", "ropek", rks[:], io["rks"])
    pl.wait_slot("vector", "ropek")
    pl.wait_slot("tensor", "ropek")
    pl.waits("gpsimd", pl.tok("tensor", "vector", "scalar") + pl.all_slots_tok())
    l1_qkv_kv(pl, io["xn1_fm"], TFULL, io["wqkvp"], rkc, rks, io["k1"], io["v1t"],
              pb, pl.sbufs["c_swap_bf"])
    pl.free("rks", "rkc")
    pl.waits("sync", pl.tok("tensor", "vector", "scalar", "gpsimd") + pl.all_slots_tok())
    ao = l1_attn(pl, io["masks"], io["q1"], io["k1"], io["v1t"], TFULL, pb)
    for nme in ("l1rnr", "l1rnb", "l1neg", "l1qmax", "l1stat",
                "l1ub", "l1mA", "l1qB", "l1qA", "l1P", "l1V", "l1KB", "l1KA"):
        pl.free(nme)
    if "aodbg" in io:
        pl.after("sync", "vector")
        for ic in range(NCHUNK):
            pl.dma("sync", "aodbg", io["aodbg"][128 * ic:128 * (ic + 1), :],
                   ao[:, ic, :])
        pl.wait_slot("sync", "aodbg")
    l1_wo(pl, ao, io["woT"], io["x2p"], io["out"], pb, TOWN)


def declare_l1_io(nc, out_kind="ExternalOutput"):
    io = {}
    io["x1_fm"] = nc.dram_tensor("x1_fm", [D, TFULL], F32, kind="ExternalInput").ap()
    io["masks"] = nc.dram_tensor("masks", [4, 8, 128, 512], BF16, kind="ExternalInput").ap()
    io["wqkvp"] = nc.dram_tensor("wqkvp", [4096, 6144], F32, kind="ExternalInput").ap()
    io["woT"] = nc.dram_tensor("woT", [4096, 4096], F32, kind="ExternalInput").ap()
    io["A2"] = nc.dram_tensor("A2", [4096, 1024], F32, kind="ExternalInput").ap()
    io["B2"] = nc.dram_tensor("B2", [1024, 4096], F32, kind="ExternalInput").ap()
    io["WrT"] = nc.dram_tensor("WrT", [4096, 64], F32, kind="ExternalInput").ap()
    io["rqc"] = nc.dram_tensor("rqc", [128, 2048], BF16, kind="ExternalInput").ap()
    io["rqs"] = nc.dram_tensor("rqs", [128, 2048], BF16, kind="ExternalInput").ap()
    io["rkc"] = nc.dram_tensor("rkc", [128, TFULL], BF16, kind="ExternalInput").ap()
    io["rks"] = nc.dram_tensor("rks", [128, TFULL], BF16, kind="ExternalInput").ap()
    k_ = "ExternalOutput" if DEBUG_L1 else "Internal"
    io["xn1_fm"] = nc.dram_tensor("xn1_fm", [D, TFULL], BF16, kind=k_).ap()
    io["q1"] = nc.dram_tensor("q1", [4096, TOWN], BF16, kind=k_).ap()
    io["k1"] = nc.dram_tensor("k1", [1024, TFULL], BF16, kind=k_).ap()
    io["v1t"] = nc.dram_tensor("v1t", [TFULL, 1024], BF16, kind=k_).ap()
    io["x2p"] = nc.dram_tensor("x2p", [D, TOWN], F32, kind=k_).ap()
    return io


def build_l1_program():
    nc = bass.Bass("TRN2", target_bir_lowering=False)
    stack = ExitStack()
    io = declare_l1_io(nc)
    io["out"] = nc.dram_tensor("out", [D, TOWN], F32, kind="ExternalOutput").ap()
    if DEBUG_L1:
        io["aodbg"] = nc.dram_tensor("aodbg", [4096, TOWN], BF16, kind="ExternalOutput").ap()
    with stack:
        pl = Planner(nc, stack)
        io["E_exp"] = declare_and_load_consts(pl, need_E=True, need_swap=True)["E_exp"]
        io["pb"] = [pl.ps(f"pb{i}", [128, 512], F32) for i in range(8)]
        for e in ("tensor", "vector", "scalar"):
            pl.wait_slot(e, "consts")
        emit_l1(pl, io)
        pl.fence_all()
        pl.after("sync", "scalar", "gpsimd", "vector", "tensor")
        counts = pl.finalize()
    return nc, counts


def prep_l1_shared(w_qkv, w_o, A, B, Wr, consts):
    perm = []
    for hq in range(32):
        base = 128 * hq
        perm.extend(range(base, base + 128, 2))
        perm.extend(range(base + 1, base + 128, 2))
    for kh in range(8):
        base = 4096 + 128 * kh
        perm.extend(range(base, base + 128, 2))
        perm.extend(range(base + 1, base + 128, 2))
    perm.extend(range(5120, 6144))
    perm = np.array(perm)
    shared = {
        "wqkvp": np.ascontiguousarray(np.asarray(w_qkv)[:, perm]),
        "woT": np.ascontiguousarray(np.asarray(w_o)),
        "A2": np.ascontiguousarray(np.asarray(A).transpose(2, 0, 1).reshape(4096, 1024)),
        "B2": np.ascontiguousarray(np.asarray(B).transpose(0, 2, 1).reshape(1024, 4096)),
        "WrT": np.ascontiguousarray(np.asarray(Wr).T),
    }
    shared.update(host_const_inputs(consts, need_E=True, need_swap=True))
    return shared


def prep_l1_core_inputs(h, x1_full_reordered, shared):
    m = dict(shared)
    m["x1_fm"] = np.ascontiguousarray(x1_full_reordered.T)
    m["masks"] = host_masks(h)
    qc, qs, kc, ks = host_rope(h)
    m["rqc"], m["rqs"], m["rkc"], m["rks"] = qc, qs, kc, ks
    return m


def host_masks_l1(h):
    qcols = host_col_order(h)[:2048]
    s_abs = []
    for t0 in [512 * (2 * i + 1) for i in range(4)]:
        s_abs.extend(range(t0, t0 + 512))
    for t0 in [1024 * i for i in range(4)]:
        s_abs.extend(range(t0, t0 + 512))
    s_abs = np.array(s_abs)
    M = np.zeros((4, 8, 128, 512), dtype=np.float32)
    for i in range(4):
        aq = qcols[512 * i:512 * (i + 1)]
        blocks, win = tile_blocks(i)
        for j, blk in enumerate(win):
            as_ = s_abs[128 * blk:128 * (blk + 1)]
            M[i, j] = (as_[:, None] <= aq[None, :])
    return M.astype(ml_dtypes.bfloat16)


def host_rope_k_rank():
    hd = 128
    inv = (1.0 / (10000.0 ** (np.arange(0, hd, 2, dtype=np.float32) / hd))).astype(np.float32)
    s_abs = []
    for t0 in [512 * (2 * i + 1) for i in range(4)]:
        s_abs.extend(range(t0, t0 + 512))
    for t0 in [1024 * i for i in range(4)]:
        s_abs.extend(range(t0, t0 + 512))
    cols = np.array(s_abs, dtype=np.float32)
    ang = cols[None, :] * inv[:, None]
    cos = np.cos(ang.astype(np.float64)).astype(np.float32)
    sin = np.sin(ang.astype(np.float64)).astype(np.float32)
    return (np.concatenate([cos, cos], 0).astype(ml_dtypes.bfloat16),
            np.concatenate([-sin, sin], 0).astype(ml_dtypes.bfloat16))


def build_fused_program():
    nc = bass.Bass("TRN2", target_bir_lowering=False)
    stack = ExitStack()
    io = declare_l0_io(nc)
    io["x1"] = nc.dram_tensor("x1", [D, TOWN], F32, kind="Internal").ap()
    io1 = {}
    io1["masksL1"] = nc.dram_tensor("masksL1", [4, 8, 128, 512], BF16, kind="ExternalInput").ap()
    io1["wqkvp"] = nc.dram_tensor("wqkvp", [4096, 6144], F32, kind="ExternalInput").ap()
    io1["woT"] = nc.dram_tensor("woT", [4096, 4096], F32, kind="ExternalInput").ap()
    io1["A2"] = nc.dram_tensor("A2", [4096, 1024], F32, kind="ExternalInput").ap()
    io1["B2"] = nc.dram_tensor("B2", [1024, 4096], F32, kind="ExternalInput").ap()
    io1["WrT"] = nc.dram_tensor("WrT", [4096, 64], F32, kind="ExternalInput").ap()
    io1["rqc"] = nc.dram_tensor("rqc", [128, 2048], BF16, kind="ExternalInput").ap()
    io1["rqs"] = nc.dram_tensor("rqs", [128, 2048], BF16, kind="ExternalInput").ap()
    io1["rkc"] = nc.dram_tensor("rkc", [128, TFULL], BF16, kind="ExternalInput").ap()
    io1["rks"] = nc.dram_tensor("rks", [128, TFULL], BF16, kind="ExternalInput").ap()
    io1["xn1_own"] = nc.dram_tensor("xn1_own", [D, TOWN], BF16, kind="Internal").ap()
    io1["xn1_full"] = nc.dram_tensor("xn1_full", [2 * D, TOWN], BF16, kind="Internal").ap()
    io1["q1"] = nc.dram_tensor("q1", [4096, TOWN], BF16, kind="Internal").ap()
    io1["k1"] = nc.dram_tensor("k1", [1024, TFULL], BF16, kind="Internal").ap()
    io1["v1t"] = nc.dram_tensor("v1t", [TFULL, 1024], BF16, kind="Internal").ap()
    io1["x2p"] = nc.dram_tensor("x2p", [D, TOWN], F32, kind="Internal").ap()
    io1["out"] = nc.dram_tensor("out", [D, TOWN], F32, kind="ExternalOutput").ap()

    with stack:
        pl = Planner(nc, stack)
        cc_aps = declare_and_load_consts(pl, need_E=True, need_swap=True)
        io1["E_exp"] = cc_aps["E_exp"]
        pb = [pl.ps(f"pb{i}", [128, 512], F32) for i in range(8)]
        io["pb"] = pb
        io1["pb"] = pb
        for e in ("tensor", "vector", "scalar"):
            pl.wait_slot(e, "consts")
        emit_l0(pl, io)
        for nme in ("l0oB", "l0oA", "l0x1B", "l0x1A", "l0mB", "l0mA",
                    "l0qB", "l0qA", "l0P", "l0V", "l0K"):
            pl.free(nme)
        pl.pool()
        ln_fm_phase(pl, TOWN, io["x1"], io1["xn1_own"], "A1", pb)
        pl.free_pool()
        pl.wait_slot("gpsimd", "A1xo0")
        pl.wait_slot("gpsimd", "A1xo1")
        pl.g.collective_compute(
            "AllGather", ALU.bypass,
            replica_groups=[[0, 1], [2, 3], [4, 5], [6, 7]],
            ins=[io1["xn1_own"].opt()], outs=[io1["xn1_full"].opt()])
        cc_count = pl.cc_sem.count
        pl.waits("sync", pl.tok("vector", "gpsimd", "scalar", "tensor") + pl.all_slots_tok())
        xn_res = load_resident_fm(pl, "xn1_res", io1["xn1_own"], [(0, TOWN)])
        pl.wait_slot("tensor", "xn1_res")
        rqc = pl.sb("rqc", [128, 2048], BF16, side="right")
        rqs = pl.sb("rqs", [128, 2048], BF16, side="right")
        pl.dma("sync", "ropeq", rqc[:], io1["rqc"])
        pl.dma("sync", "ropeq", rqs[:], io1["rqs"])
        pl.wait_slot("vector", "ropeq")
        pl.wait_slot("tensor", "ropeq")
        l1_qkv_q(pl, xn_res, io1["wqkvp"], rqc, rqs, io1["q1"], pb, TOWN,
                 pl.sbufs["c_swap_bf"])
        pl.free("rqs", "rqc")
        l1_moe(pl, xn_res, io1["WrT"], io1["A2"], io1["B2"], io1["E_exp"],
               io["x1"], io1["x2p"], pb, TOWN)
        pl.free("xn1_res")
        pl.free("wgtB", "wgtA")
        rkc = pl.sb("rkc", [128, TFULL], BF16, side="right")
        rks = pl.sb("rks", [128, TFULL], BF16, side="right")
        pl.dma("sync", "ropek", rkc[:], io1["rkc"])
        pl.dma("sync", "ropek", rks[:], io1["rks"])
        pl.wait_slot("vector", "ropek")
        pl.wait_slot("tensor", "ropek")
        pl.waits("gpsimd", pl.tok("tensor", "vector", "scalar") + pl.all_slots_tok())
        pl.eng["sync"].wait_ge_(pl.cc_sem, cc_count)
        pl.eng["gpsimd"].wait_ge_(pl.cc_sem, cc_count)
        l1_qkv_kv(pl, io1["xn1_full"], TFULL, io1["wqkvp"], rkc, rks,
                  io1["k1"], io1["v1t"], pb, pl.sbufs["c_swap_bf"],
                  rank_order=True)
        pl.free("rks", "rkc")
        pl.waits("sync", pl.tok("tensor", "vector", "scalar", "gpsimd") + pl.all_slots_tok())
        ao = l1_attn(pl, io1["masksL1"], io1["q1"], io1["k1"], io1["v1t"], TFULL, pb)
        for nme in ("l1rnr", "l1rnb", "l1neg", "l1qmax", "l1stat",
                    "l1ub", "l1mA", "l1qB", "l1qA", "l1P", "l1V", "l1KB", "l1KA"):
            pl.free(nme)
        l1_wo(pl, ao, io1["woT"], io1["x2p"], io1["out"], pb, TOWN)
        pl.fence_all()
        pl.after("sync", "scalar", "gpsimd", "vector", "tensor")
        counts = pl.finalize()
    return nc, counts


def prep_fused_inputs(inputs_dict, consts):
    x = np.asarray(inputs_dict["L0_attn_x"], dtype=np.float32)
    maps0 = prep_l0_inputs(x, inputs_dict["L0_attn_bv_q"], inputs_dict["L0_attn_bv_k"],
                           inputs_dict["L0_attn_bv_v"], inputs_dict["L0_ffn_w12"],
                           inputs_dict["L0_ffn_b12"], inputs_dict["L0_ffn_w3"],
                           inputs_dict["L0_ffn_b3"], consts)
    shared1 = prep_l1_shared(inputs_dict["L1_attn_w_qkv"], inputs_dict["L1_attn_w_o"],
                             inputs_dict["L1_ffn_A"], inputs_dict["L1_ffn_B"],
                             inputs_dict["L1_ffn_Wr"], consts)
    rkc, rks = host_rope_k_rank()
    per_core = []
    for c in range(8):
        h = c % 2
        m = dict(maps0[c])
        for k in ("wqkvp", "woT", "A2", "B2", "WrT", "E_exp", "swap_bf"):
            m[k] = shared1[k]
        qc, qs, _, _ = host_rope(h)
        m["rqc"], m["rqs"] = qc, qs
        m["rkc"], m["rks"] = rkc, rks
        m["masksL1"] = host_masks_l1(h)
        per_core.append(m)
    return per_core


from concourse.bass_utils import run_bass_kernel_spmd

_CACHE = {}


def _program():
    if "fused" not in _CACHE:
        _CACHE["fused"] = build_fused_program()
        _CACHE["consts"] = host_constants()
    return _CACHE


def kernel(L0_attn_x, L0_attn_bv_q, L0_attn_bv_k, L0_attn_bv_v,
           L0_ffn_w12, L0_ffn_b12, L0_ffn_w3, L0_ffn_b3,
           L1_attn_w_qkv, L1_attn_w_o, L1_ffn_A, L1_ffn_B, L1_ffn_Wr):
    P = _program()
    nc, _ = P["fused"]
    consts = P["consts"]
    inputs = dict(L0_attn_x=L0_attn_x, L0_attn_bv_q=L0_attn_bv_q,
                  L0_attn_bv_k=L0_attn_bv_k, L0_attn_bv_v=L0_attn_bv_v,
                  L0_ffn_w12=L0_ffn_w12, L0_ffn_b12=L0_ffn_b12,
                  L0_ffn_w3=L0_ffn_w3, L0_ffn_b3=L0_ffn_b3,
                  L1_attn_w_qkv=L1_attn_w_qkv, L1_attn_w_o=L1_attn_w_o,
                  L1_ffn_A=L1_ffn_A, L1_ffn_B=L1_ffn_B, L1_ffn_Wr=L1_ffn_Wr)
    in_maps = prep_fused_inputs(inputs, consts)
    res = run_bass_kernel_spmd(nc, in_maps, core_ids=list(range(8)))
    out = np.empty((4, TFULL, D), dtype=np.float32)
    for b in range(4):
        place_own_cols(out[b], res.results[2 * b]["out"].T, 0)
        place_own_cols(out[b], res.results[2 * b + 1]["out"].T, 1)
    return out
